# version 05
# baseline (speedup 1.0000x reference)
import math
from contextlib import ExitStack

import numpy as np
import ml_dtypes

import concourse.bass as bass
import concourse.mybir as mybir
from concourse.bass_utils import run_bass_kernel_spmd

F32 = mybir.dt.float32
BF16 = mybir.dt.bfloat16
I32 = mybir.dt.int32
AF = mybir.ActivationFunctionType
OP = mybir.AluOpType
AX = mybir.AxisListType

D = 2048
T = 1024
NH = 16
HD = 128
KC = D // 128
NQ = T // 128
SEQ = 2048
NKT = SEQ // 128
NCORE = 8
NBLOCK = 4
PHASES = 99
DEVDUMP = False

EPS_LN = 1e-5
EPS_L0Q = 1e-8
EPS_L1 = 1e-5
QB = 127.0
ISQD = 1.0 / math.sqrt(128.0)

ENGINES = ("sync", "gpsimd", "vector", "scalar", "tensor")


class StopEmission(Exception):
    pass


class Build:
    def __init__(self):
        self.q = {e: [] for e in ENGINES}
        self.ctr = {}
        self.phase = 0
        self.stopped = False

    def mark(self):
        self.phase += 1
        if self.phase >= PHASES:
            self.stopped = True

    def go(self):
        return not self.stopped

    def inc(self, sem, by=1):
        if not self.stopped:
            self.ctr[sem] = self.ctr.get(sem, 0) + by
        return self.ctr.get(sem, 0)

    def val(self, sem):
        return self.ctr.get(sem, 0)

    def emit(self, eng, fn):
        if not self.stopped:
            self.q[eng].append(fn)


def build_program():
    nc = bass.Bass("TRN2", target_bir_lowering=False, debug=False,
                   detect_race_conditions=False)

    di = lambda n, s, dt=F32: nc.dram_tensor(n, s, dt, kind="ExternalInput").ap()
    x_fm = di("x_fm", [D, T])
    cos_own_h = di("cos_own", [128, T])
    sin_own_h = di("sin_own", [128, T])
    maskA_h = di("maskA", [128, 128])
    maskB_h = di("maskB", [128, 128])
    ones_h = di("ones128", [128, 128], BF16)
    ident_h = di("ident", [128, 128], BF16)
    identf_h = di("identf", [128, 128], F32)
    jrot_h = di("jrot", [128, 128], BF16)
    e16_h = di("e16", [8, 128], BF16)
    sel8_h = di("sel8", [8, 8 * 128], BF16)
    sel32_h = di("sel32", [32, 32 * 128], BF16)
    wsl = {nm: di(nm + "_s", [256, D]) for nm in
           ("l0q", "l0k", "l0v", "l0o", "l1q", "l1k", "l1v", "l1o")}
    l0fWT = di("l0fWT", [D, D])
    l0f_b_h = di("l0f_b", [1, D])
    l0f_AT = di("l0f_AT", [D, 128])
    l0f_B = di("l0f_B", [128, D])
    l0f_gwT = di("l0f_gwT", [D, 8])
    l0f_gb_h = di("l0f_gb", [1, 8])
    l1f_rwT = di("l1f_rwT", [D, 32])
    l1f_wg = di("l1f_wg", [32, D, 768])
    l1f_wu = di("l1f_wu", [32, D, 768])
    l1f_wd = di("l1f_wd", [32, 768, D])
    out_fm = nc.dram_tensor("out_fm", [D, T], F32, kind="ExternalOutput").ap()
    if DEVDUMP:
        dbg_ctx = nc.dram_tensor("dbg_ctx", [128, NH * T], BF16, kind="ExternalOutput").ap()
        dbg_k = nc.dram_tensor("dbg_k", [128, SEQ], BF16, kind="ExternalOutput").ap()
        dbg_v = nc.dram_tensor("dbg_v", [128, SEQ], BF16, kind="ExternalOutput").ap()
        dbg_xln = nc.dram_tensor("dbg_xln", [128, KC * T], BF16, kind="ExternalOutput").ap()
        dbg_cols = nc.dram_tensor("dbg_cols", [128, 64], F32, kind="ExternalOutput").ap()
        dbg_kvin = nc.dram_tensor("dbg_kvin", [2, NH, HD, T], BF16, kind="ExternalOutput").ap()
        dbg_p0in = nc.dram_tensor("dbg_p0in", [4, D, 256], BF16, kind="ExternalOutput").ap()
        dbg_p0out = nc.dram_tensor("dbg_p0out", [2, 4, D, 256], BF16, kind="ExternalOutput").ap()

    prep0_in = nc.dram_tensor("prep0_in", [4, D, 256], BF16).ap()
    prep0_out = nc.dram_tensor("prep0_out", [NCORE, 4, D, 256], BF16).ap()
    prep1_in = nc.dram_tensor("prep1_in", [4, D, 256], BF16).ap()
    prep1_out = nc.dram_tensor("prep1_out", [NCORE, 4, D, 256], BF16).ap()
    sums_in = nc.dram_tensor("sums_in", [4, 1], F32).ap()
    sums_out = nc.dram_tensor("sums_out", [NCORE * 4, 1], F32).ap()
    kv_in = nc.dram_tensor("kv_in", [2, NH, HD, T], BF16).ap()
    kv_out = nc.dram_tensor("kv_out", [2, 2, NH, HD, T], BF16).ap()
    SHARD = 2 * NH * HD * T

    B = Build()
    ALL8 = [list(range(NCORE))]

    with ExitStack() as ctx:
        sbt = lambda n, s, dt=F32: ctx.enter_context(nc.sbuf_tensor(n, s, dt))
        pst = lambda n, s, dt=F32: ctx.enter_context(nc.psum_tensor(n, s, dt))
        sem = lambda n: ctx.enter_context(nc.semaphore(n))

        x = sbt("x", [128, KC * T])
        xln = sbt("xln", [128, KC * T], BF16)
        mean_rep = sbt("mean_rep", [128, T])
        rstd_rep = sbt("rstd_rep", [128, T])
        rep_a = sbt("rep_a", [128, T])
        rep_b = sbt("rep_b", [128, T])
        tmp1 = sbt("tmp1", [128, T])
        tmp2 = sbt("tmp2", [128, T])
        tmp3 = sbt("tmp3", [128, T])
        cosb = sbt("cosb", [128, T])
        sinb = sbt("sinb", [128, T])
        ones_bf = sbt("ones_bfb", [128, 128], BF16)
        ident = sbt("identb", [128, 128], BF16)
        identf = sbt("identfb", [128, 128])
        jrot = sbt("jrotb", [128, 128], BF16)
        e16 = sbt("e16b", [8, 128], BF16)
        selbuf = sbt("selbuf", [32, 128], BF16)
        mA = sbt("mA", [128, 128])
        mB = sbt("mB", [128, 128])
        fone = sbt("fone", [128, 1])
        onesrow = sbt("onesrow", [1, 512], BF16)
        qncol = sbt("qncol", [128, NQ])
        k2col = sbt("k2col", [128, 1])
        bcol = sbt("bcol", [128, NQ])
        lcol = sbt("lcol", [128, NQ])
        rlcol = sbt("rlcol", [128, NQ])
        gcolM = sbt("gcolM", [128, NQ])
        gcolR = sbt("gcolR", [128, NQ])
        gcolA = sbt("gcolA", [128, NQ])
        gcolB = sbt("gcolB", [128, NQ])
        brow = sbt("brow", [8, 256], BF16)
        sums8 = sbt("sums8", [32, 1])
        scol = sbt("scol", [128, 4])
        thcol = sbt("thcol", [128, 4])
        gtok = sbt("gtok", [128, 128])
        gtk2 = sbt("gtk2", [128, 32])
        gtk3 = sbt("gtk3", [128, 32])
        gm1 = sbt("gm1", [128, 1])
        gm2 = sbt("gm2", [128, 1])
        gden = sbt("gden", [128, 1])
        biasrow = sbt("biasrow", [1, D], BF16)
        gbrow = sbt("gbrow", [1, 8], BF16)
        wselb = sbt("wselb", [32, T], BF16)
        wrepf = sbt("wrepf", [128, T])

        tint = tmp2.ap().bitcast(I32)[:, 0:T]
        Mmax = tmp3

        s_ld = sem("s_ld")
        s_gd = sem("s_gd")
        s_cc = sem("s_cc")
        s_t = sem("s_t")
        s_v = sem("s_v")
        s_s = sem("s_s")
        s_st = sem("s_st")
        SEMS = {"ld": s_ld, "gd": s_gd, "cc": s_cc, "t": s_t, "v": s_v,
                "s": s_s, "st": s_st}

        def wt(e, semname, val):
            if val > 0:
                e.wait_ge(SEMS[semname], val)

        def dma(eng, out, in_, semname="ld", qsem=None):
            tgt = B.inc(semname, 16)

            def f(e, out=out, in_=in_, tgt=tgt):
                e.dma_start(out=out, in_=in_).then_inc(SEMS[semname], 16)
            B.emit(eng, f)
            return tgt

        s_vx = sem("s_vx")
        s_sx = sem("s_sx")
        s_gx = sem("s_gx")
        AUX = {"vector": s_vx, "scalar": s_sx, "gpsimd": s_gx}

        class _Rec:

            def __init__(self, eng, aux, ctr):
                self._e = eng
                self._aux = aux
                self._ctr = ctr
                self.last = None

            def __getattr__(self, name):
                a = getattr(self._e, name)
                if callable(a):
                    def call(*args, _a=a, _n=name, **kw):
                        if self._aux is not None and _n not in (
                                "wait_ge", "sem_inc", "nop", "partition_id",
                                "compute_val"):
                            if self._ctr[0] > 0:
                                self._e.wait_ge(self._aux, self._ctr[0])
                        r = _a(*args, **kw)
                        if hasattr(r, "then_inc"):
                            self.last = r
                            if self._aux is not None and _n not in (
                                    "wait_ge", "sem_inc", "nop"):
                                r.then_inc(self._aux, 1)
                                self._ctr[0] += 1
                        return r
                    return call
                return a

        AUXCTR = {"vector": [0], "scalar": [0], "gpsimd": [0]}

        def _mkstep(eng, semname, semh):
            aux = AUX.get(eng)
            actr = AUXCTR.get(eng)

            def step(fn, *waits):
                B.inc(semname)

                def f(e, fn=fn, waits=waits):
                    for sname, val in waits:
                        wt(e, sname, val)
                    rec = _Rec(e, aux, actr)
                    fn(rec)
                    assert rec.last is not None, "step emitted no instruction"
                    if aux is not None:
                        e.wait_ge(aux, actr[0])
                        e.sem_inc(semh, 1)
                    else:
                        rec.last.then_inc(semh, 1)
                B.emit(eng, f)
                return B.val(semname)
            return step

        vstep = _mkstep("vector", "v", s_v)
        sstep = _mkstep("scalar", "s", s_s)
        tstep = _mkstep("tensor", "t", s_t)
        gstep = _mkstep("gpsimd", "gd", s_gd)

        def gdma(fn, *waits):
            B.inc("gd", 16)

            def f(g, fn=fn, waits=waits):
                for sname, val in waits:
                    wt(g, sname, val)
                fn(g).then_inc(s_gd, 16)
            B.emit("gpsimd", f)
            return B.val("gd")

        def syncstep(fn, *waits):
            def f(e, fn=fn, waits=waits):
                for sname, val in waits:
                    wt(e, sname, val)
                fn(e)
            B.emit("sync", f)

        PAIRS = [[0, 1], [2, 3], [4, 5], [6, 7]]

        def collective(ins, outs, *waits, groups=None):
            tgt = B.inc("cc", 1)
            g_ = groups or ALL8

            def f(g, ins=ins, outs=outs, tgt=tgt, waits=waits, g_=g_):
                for sname, val in waits:
                    wt(g, sname, val)
                g.collective_compute("AllGather", OP.bypass, replica_groups=g_,
                                     ins=ins, outs=outs).then_inc(s_cc, 1)
            B.emit("gpsimd", f)
            return tgt

        def allv():
            return ("v", B.val("v"))

        def alls():
            return ("s", B.val("s"))

        def allt():
            return ("t", B.val("t"))

        def allld():
            return ("ld", B.val("ld"))

        def allgd():
            return ("gd", B.val("gd"))

        def allcc():
            return ("cc", B.val("cc"))

        pja = pst("pja", [128, T])
        pjb = pst("pjb", [128, T])
        trp = pst("trp", [128, 2048], BF16)
        smp = pst("smp", [128, 512])
        spx = pst("spx", [128, 512])

        for o, i in ((x[:, :].rearrange("p (kc t) -> p kc t", t=T),
                      x_fm.rearrange("(kc p) t -> p kc t", p=128)),
                     (cosb[:, :], cos_own_h), (sinb[:, :], sin_own_h),
                     (mA[:, :], maskA_h), (mB[:, :], maskB_h),
                     (ones_bf[:, :], ones_h), (ident[:, :], ident_h),
                     (identf[:, :], identf_h), (jrot[:, :], jrot_h),
                     (e16[:, :], e16_h)):
            dma("sync", o, i)
        LD_INIT = B.val("ld")
        vstep(lambda v: (v.memset(onesrow[:, :], 1.0), v.memset(fone[:, :], 1.0)))
        B.mark()


        with nc.sbuf_tensor("qsl0", [128, D], F32) as qsl0, \
             nc.sbuf_tensor("qsl1", [128, D], F32) as qsl1, \
             nc.sbuf_tensor("qc1", [128, D], F32) as qc1, \
             nc.sbuf_tensor("qtr", [128, D], BF16) as qtr, \
             nc.sbuf_tensor("qsc", [128, 16], F32) as qsc, \
             nc.sbuf_tensor("qth", [128, 16], F32) as qth:
            qsl = [qsl0, qsl1]

            def emit_prep(names, prep_in_ap, l1_mode):
                for wi, nm in enumerate(names):
                    for rt in range(2):
                        syncstep(lambda e, nm=nm, rt=rt:
                                 e.dma_start(out=qsl[rt][:, :],
                                             in_=wsl[nm][rt * 128:(rt + 1) * 128, :]
                                             ).then_inc(s_ld, 16), allv())
                        B.inc("ld", 16)
                        ld_t = B.val("ld")

                        def vq(v, rt=rt, wi=wi, l1_mode=l1_mode):
                            if l1_mode:
                                thr1 = thcol[:, wi:wi + 1]
                                scl1 = scol[:, wi:wi + 1]
                                thr = lambda g: thr1.broadcast_to([128, 128])
                                scl = lambda g: scl1.broadcast_to([128, 128])
                            else:
                                v.tensor_reduce(qsc[:, :],
                                                qsl[rt][:, :].rearrange("p (g n) -> p g n", g=16),
                                                AX.X, OP.add, apply_absolute_value=True)
                                v.tensor_scalar(qsc[:, :], qsc[:, :], 1.0 / 128, EPS_L0Q,
                                                OP.mult, OP.max)
                                v.tensor_scalar(qth[:, :], qsc[:, :], 0.5, None, OP.mult)
                                thr = lambda g: qth[:, g:g + 1].broadcast_to([128, 128])
                                scl = lambda g: qsc[:, g:g + 1].broadcast_to([128, 128])
                            for g in range(16):
                                sl = slice(g * 128, (g + 1) * 128)
                                v.tensor_tensor(qc1[:, sl], qsl[rt][:, sl], thr(g), OP.is_ge)
                            v.tensor_scalar(qsl[rt][:, :], qsl[rt][:, :], -1.0, None, OP.mult)
                            for g in range(16):
                                sl = slice(g * 128, (g + 1) * 128)
                                v.tensor_tensor(qsl[rt][:, sl], qsl[rt][:, sl], thr(g), OP.is_le)
                            v.tensor_tensor(qc1[:, :], qc1[:, :], qsl[rt][:, :], OP.add)
                            for g in range(16):
                                sl = slice(g * 128, (g + 1) * 128)
                                v.scalar_tensor_tensor(qtr[:, sl], qc1[:, sl], -1.0, scl(g),
                                                       OP.add, OP.mult)
                        v_t = vstep(vq, ("ld", ld_t), alls())

                        t_t = tstep(lambda t: [t.transpose(trp[:, kc * 128:(kc + 1) * 128],
                                                           qtr[:, kc * 128:(kc + 1) * 128],
                                                           ident[:, :])
                                               for kc in range(KC)], ("v", v_t), alls())
                        s_t2 = sstep(lambda s: s.copy(qtr[:, :], trp[:, 0:D]), ("t", t_t))
                        syncstep(lambda e, wi=wi, rt=rt:
                                 e.dma_start(
                                     out=prep_in_ap[wi, :, rt * 128:(rt + 1) * 128].rearrange(
                                         "(kc p) o -> p kc o", p=128),
                                     in_=qtr.ap().rearrange("p (kc o) -> p kc o", o=128)
                                 ).then_inc(s_ld, 16), ("s", s_t2))
                        B.inc("ld", 16)

            emit_prep(("l0q", "l0k", "l0v", "l0o"), prep0_in, l1_mode=False)
            B.mark()
            cc_prep0 = collective([prep0_in.opt()], [prep0_out.opt()], allld())
            B.mark()

            vstep(lambda v: v.memset(Mmax[:, 0:16], 0.0))
            for wi, nm in enumerate(("l1q", "l1k", "l1v", "l1o")):
                for rt in range(2):
                    syncstep(lambda e, nm=nm, rt=rt:
                             e.dma_start(out=qsl[rt][:, :],
                                         in_=wsl[nm][rt * 128:(rt + 1) * 128, :]
                                         ).then_inc(s_ld, 16), allv())
                    B.inc("ld", 16)
                    ld_t = B.val("ld")

                    def vs(v, wi=wi, rt=rt):
                        v.tensor_reduce(Mmax[:, 8 + wi:9 + wi], qsl[rt][:, :], AX.X,
                                        OP.add, apply_absolute_value=True)
                        v.tensor_tensor(Mmax[:, wi:wi + 1], Mmax[:, wi:wi + 1],
                                        Mmax[:, 8 + wi:9 + wi], OP.add)
                    vstep(vs, ("ld", ld_t))
            t_t = tstep(lambda t: t.matmul(smp[0:4, 0:1], Mmax[:, 0:4], fone[:, :],
                                           start=True, stop=True), allv())
            v_t = vstep(lambda v: v.tensor_copy(Mmax[0:4, 16:17] if False else tmp1[0:4, 0:1],
                                                smp[0:4, 0:1]), ("t", t_t))
            syncstep(lambda e: e.dma_start(out=sums_in, in_=tmp1[0:4, 0:1]
                                           ).then_inc(s_ld, 16), ("v", v_t))
            B.inc("ld", 16)
            B.mark()
            cc_sums = collective([sums_in.opt()], [sums_out.opt()], allld())
            B.mark()

            syncstep(lambda e: e.dma_start(out=sums8[:, :], in_=sums_out
                                           ).then_inc(s_ld, 16), allcc())
            B.inc("ld", 16)
            ld_t = B.val("ld")
            syncstep(lambda e: e.dma_start(
                out=gtok[0:8, 0:4], in_=sums_out.rearrange("(r w) one -> r (w one)", w=4)
            ).then_inc(s_ld, 16))
            B.inc("ld", 16)
            ld_t = B.val("ld")
            v_t = vstep(lambda v: v.memset(mean_rep[0:8, 0:128], 1.0), ("ld", ld_t))
            t_t = tstep(lambda t: t.matmul(smp[:, 4:8], mean_rep[0:8, 0:128],
                                           gtok[0:8, 0:4], start=True, stop=True),
                        ("v", v_t))
            vstep(lambda v: (v.tensor_scalar(scol[:, 0:4], smp[:, 4:8],
                                             1.0 / (D * D), EPS_L1, OP.mult, OP.add),
                             v.tensor_scalar(thcol[:, 0:4], scol[:, 0:4], 0.5, None,
                                             OP.mult)),
                  ("t", t_t))

            B.mark()
            emit_prep(("l1q", "l1k", "l1v", "l1o"), prep1_in, l1_mode=True)
            B.mark()
            cc_prep1 = collective([prep1_in.opt()], [prep1_out.opt()], allld())
            B.mark()

        def emit_ln_stats():
            for kc in range(KC):
                xb = x2b[:, 0:512] if False else None
                s1 = sstep(lambda s, kc=kc: s.copy(xln[:, kc * T:(kc + 1) * T],
                                                   x[:, kc * T:(kc + 1) * T]),
                           *([("ld", LD_INIT), allv(), allt()] if kc == 0 else []))
                half_buf = x2b.ap().rearrange("p (two t) -> p two t", two=2)[:, kc % 2, :] \
                    if False else None
                def ssq(s, kc=kc, prev_t=B.val("t")):
                    wt(s, "t", prev_t)
                    s.activation(x2b[:, 0:T], xln[:, kc * T:(kc + 1) * T], AF.Square)
                s2 = sstep(ssq)

                def tmm(t, kc=kc):
                    for half in range(2):
                        sl = slice(half * 512, (half + 1) * 512)
                        t.matmul(pja[:, sl], ones_bf[:, :],
                                 xln[:, kc * T + half * 512:kc * T + (half + 1) * 512],
                                 start=(kc == 0), stop=(kc == KC - 1))
                        t.matmul(pjb[:, sl], ones_bf[:, :], x2b[:, sl],
                                 start=(kc == 0), stop=(kc == KC - 1))
                tstep(tmm, ("s", s2))
            t_stats = B.val("t")

            def vst(v):
                v.tensor_scalar(mean_rep[:, :], pja[:, :], 1.0 / D, None, OP.mult)
                v.tensor_scalar(tmp1[:, :], pjb[:, :], 1.0 / D, None, OP.mult)
                v.tensor_tensor(tmp2[:, :], mean_rep[:, :], mean_rep[:, :], OP.mult)
                v.tensor_tensor(tmp1[:, :], tmp1[:, :], tmp2[:, :], OP.subtract)
                v.tensor_scalar(tmp1[:, :], tmp1[:, :], EPS_LN, None, OP.add)
            v_t = vstep(vst, ("t", t_stats))
            s_t2 = sstep(lambda s: s.activation(tmp2[:, :], tmp1[:, :], AF.Sqrt),
                         ("v", v_t))
            vstep(lambda v: v.reciprocal(rstd_rep[:, :], tmp2[:, :]), ("s", s_t2))

        def emit_ln_apply():
            for kc in range(KC):
                def va(v, kc=kc):
                    v.scalar_tensor_tensor(tmp1[:, :], mean_rep[:, :], -1.0,
                                           x[:, kc * T:(kc + 1) * T], OP.mult, OP.add)
                    v.tensor_tensor(xln[:, kc * T:(kc + 1) * T], tmp1[:, :],
                                    rstd_rep[:, :], OP.mult)
                vstep(va, *([allt()] if kc == 0 else []))

        def emit_colrow_broadcast(colA, colB, repA, repB):
            syncstep(lambda e: e.dma_start(out=sel8b, in_=sel8_h).then_inc(s_ld, 16),
                     allt(), alls())
            B.inc("ld", 16)
            ld_s8 = B.val("ld")
            v_t = vstep(lambda v: (v.memset(tmp1[:, 0:128], 0.0),
                                   v.memset(tmp2[:, 0:128], 0.0),
                                   v.tensor_copy(tmp1[:, 0:NQ], colA),
                                   v.tensor_copy(tmp2[:, 0:NQ], colB)))
            t_t = tstep(lambda t: (t.transpose(smp[:, 0:128], tmp1[:, 0:128], identf[:, :]),
                                   t.transpose(smp[:, 128:256], tmp2[:, 0:128], identf[:, :])),
                        ("v", v_t))
            v_t = vstep(lambda v: (v.tensor_copy(brow[0:NQ, 0:128], smp[0:NQ, 0:128]),
                                   v.tensor_copy(brow[0:NQ, 128:256], smp[0:NQ, 128:256])),
                        ("t", t_t))

            def tbc(t, ld_s8=ld_s8):
                wt(t, "ld", ld_s8)
                for j in range(NQ):
                    t.matmul(pja[:, j * 128:(j + 1) * 128],
                             sel8b[:, j * 128:(j + 1) * 128],
                             brow[0:NQ, 0:128], start=True, stop=True)
                    t.matmul(pjb[:, j * 128:(j + 1) * 128],
                             sel8b[:, j * 128:(j + 1) * 128],
                             brow[0:NQ, 128:256], start=True, stop=True)
            t_t = tstep(tbc, ("v", v_t))
            vstep(lambda v: (v.tensor_copy(repA, pja[:, :]),
                             v.tensor_copy(repB, pjb[:, :])), ("t", t_t))

        def emit_maxcols(src_is_ln):
            vstep(lambda v: v.memset(Mmax[:, 0:T], 0.0))
            for kc in range(KC):
                if src_is_ln:
                    v_t = vstep(lambda v, kc=kc: v.scalar_tensor_tensor(
                        tmp1[:, :], mean_rep[:, :], -1.0, x[:, kc * T:(kc + 1) * T],
                        OP.mult, OP.add))
                    s_t2 = sstep(lambda s: s.activation(tmp2[:, :], tmp1[:, :], AF.Abs),
                                 ("v", v_t))
                else:
                    s_t2 = sstep(lambda s, kc=kc: s.activation(
                        tmp2[:, :], ctx_all[:, kc * T:(kc + 1) * T], AF.Abs), allv())
                vstep(lambda v: v.tensor_max(Mmax[:, 0:T], Mmax[:, 0:T], tmp2[:, :]),
                      ("s", s_t2))
            for batch in range(2):
                t_t = tstep(lambda t, batch=batch: [t.transpose(
                    smp[:, jj * 128:(jj + 1) * 128],
                    Mmax[:, (batch * 4 + jj) * 128:(batch * 4 + jj + 1) * 128],
                    identf[:, :]) for jj in range(4)], allv())
                vstep(lambda v, batch=batch: [v.reduce_max(
                    gcolM[:, batch * 4 + jj:batch * 4 + jj + 1],
                    smp[:, jj * 128:(jj + 1) * 128], AX.X) for jj in range(4)],
                    ("t", t_t))
            if src_is_ln:
                for batch in range(2):
                    t_t = tstep(lambda t, batch=batch: [t.transpose(
                        smp[:, jj * 128:(jj + 1) * 128],
                        rstd_rep[:, (batch * 4 + jj) * 128:(batch * 4 + jj + 1) * 128],
                        identf[:, :]) for jj in range(4)], allv())
                    vstep(lambda v, batch=batch: [v.tensor_copy(
                        gcolR[:, batch * 4 + jj:batch * 4 + jj + 1],
                        smp[:, jj * 128:jj * 128 + 1]) for jj in range(4)], ("t", t_t))

        def emit_l1_input_quant():
            emit_maxcols(src_is_ln=True)

            def vg(v):
                v.tensor_tensor(gcolA[:, :], gcolR[:, :], gcolM[:, :], OP.mult)
                v.tensor_scalar(gcolA[:, :], gcolA[:, :], EPS_L1, None, OP.add)
                v.tensor_scalar(gcolB[:, :], gcolA[:, :], 1.0 / QB, None, OP.mult)
                v.reciprocal(gcolM[:, :], gcolA[:, :])
                v.tensor_tensor(gcolA[:, :], gcolM[:, :], gcolR[:, :], OP.mult)
                v.tensor_scalar(gcolA[:, :], gcolA[:, :], QB, None, OP.mult)
            vstep(vg)
            emit_colrow_broadcast(gcolA[:, :], gcolB[:, :], rep_a[:, :], rep_b[:, :])
            for kc in range(KC):
                v_t = vstep(lambda v, kc=kc: (
                    v.scalar_tensor_tensor(tmp1[:, :], mean_rep[:, :], -1.0,
                                           x[:, kc * T:(kc + 1) * T], OP.mult, OP.add),
                    v.tensor_tensor(tmp1[:, :], tmp1[:, :], rep_a[:, :], OP.mult),
                    v.tensor_copy(tint, tmp1[:, :]),
                    v.tensor_copy(xln[:, kc * T:(kc + 1) * T], tint)),
                    *([allt()] if kc == 0 else []))

        def emit_l1_ctx_quant():
            emit_maxcols(src_is_ln=False)

            def vg(v):
                v.tensor_scalar(gcolA[:, :], gcolM[:, :], EPS_L1, None, OP.add)
                v.tensor_scalar(gcolB[:, :], gcolA[:, :], 1.0 / QB, None, OP.mult)
                v.reciprocal(gcolM[:, :], gcolA[:, :])
                v.tensor_scalar(gcolA[:, :], gcolM[:, :], QB, None, OP.mult)
            vstep(vg)
            emit_colrow_broadcast(gcolA[:, :], gcolB[:, :], rep_a[:, :], rep_b[:, :])
            for kc in range(KC):
                s_t2 = sstep(lambda s, kc=kc: s.copy(tmp2[:, :],
                                                     ctx_all[:, kc * T:(kc + 1) * T]),
                             allv())
                vstep(lambda v, kc=kc: (
                    v.tensor_tensor(tmp1[:, :], tmp2[:, :], rep_a[:, :], OP.mult),
                    v.tensor_copy(tint, tmp1[:, :]),
                    v.tensor_copy(xln[:, kc * T:(kc + 1) * T], tint)),
                    ("s", s_t2), *([allt()] if kc == 0 else []))

        arena = sbt("arena", [128, 30848], BF16)
        A = arena.ap()
        srow = A[0:1, 29824:30848]
        ctx_all = A[:, 0:16384]
        Ksb = A[:, 16384:18432]
        Vsb = A[:, 18432:20480]
        Vtm = A[:, 20480:22528]
        Pbf = A[:, 22528:24576]
        Ptrb = A[:, 24576:26624]
        qh_bf = A[:, 26624:27648]
        kh_bf = Pbf[:, 0:T]
        sel8b = Ptrb[0:8, 0:1024]
        x2b = qh_bf
        ctmb = A[:, 29696:29824]
        wqb = [A[:, 27648:29696]]
        wkb = [Ptrb]
        wvb = [Vtm]
        ring = [A[:, i * 6144:(i + 1) * 6144] for i in range(4)]
        hb = A[:, 24576:30720]
        rwbb = A[:, 24576:25088]

        def emit_attention(layer):
            prep_out = prep0_out if layer == 0 else prep1_out
            causal = (layer == 0)

            if layer == 1:
                vstep(lambda v: (v.tensor_tensor(cosb[:, :], cosb[:, :], rep_b[:, :], OP.mult),
                                 v.tensor_tensor(sinb[:, :], sinb[:, :], rep_b[:, :], OP.mult)))

            def load_whead(e, w_idx, h, buf):
                r, half = h // 2, h % 2
                bap = buf
                e.dma_start(
                    out=bap.rearrange("p (kc o) -> p kc o", o=128),
                    in_=prep_out[r, w_idx, :, half * 128:(half + 1) * 128].rearrange(
                        "(kc p) o -> p kc o", p=128)).then_inc(s_ld, 16)

            for w_idx, bufs in ((0, wqb), (1, wkb), (2, wvb)):
                syncstep(lambda e, w_idx=w_idx, bufs=bufs: load_whead(e, w_idx, 0, bufs[0]),
                         allcc())
                B.inc("ld", 16)

            for h in range(NH):
                ld_w = B.val("ld")

                for which, bufs, w_idx in (("q", wqb, 0), ("k", wkb, 1), ("v", wvb, 2)):
                    def tmm(t, which=which, bufs=bufs, h=h, ld_w=ld_w):
                        wt(t, "ld", ld_w)
                        dst = pja if which != "v" else pjb
                        for kcc in range(KC):
                            for half in range(2):
                                sl = slice(half * 512, (half + 1) * 512)
                                t.matmul(dst[:, sl], bufs[0][:, kcc * 128:(kcc + 1) * 128],
                                         xln[:, kcc * T + half * 512:kcc * T + (half + 1) * 512],
                                         start=(kcc == 0), stop=(kcc == KC - 1))
                    t_mm = tstep(tmm, allv())
                    if h + 1 < NH:
                        syncstep(lambda e, w_idx=w_idx, bufs=bufs, h=h:
                                 load_whead(e, w_idx, h + 1, bufs[0]), ("t", t_mm))
                        B.inc("ld", 16)

                    if which == "v":
                        if layer == 0:
                            d_s = sstep(lambda s: s.copy(kh_bf, pjb[:, :]),
                                        ("t", t_mm), allld())
                            dsem = ("s", d_s)
                        else:
                            d_v = vstep(lambda v: v.tensor_tensor(kh_bf, pjb[:, :],
                                                                  rep_b[:, :], OP.mult),
                                        ("t", t_mm), allld())
                            dsem = ("v", d_v)
                        syncstep(lambda e, h=h: e.dma_start(out=kv_in[1, h, :, :],
                                                            in_=kh_bf
                                                            ).then_inc(s_ld, 16), dsem)
                        B.inc("ld", 16)
                    else:
                        s_e = sstep(lambda s: s.copy(kh_bf, pja[:, :]), ("t", t_mm))
                        t_j = tstep(lambda t: [t.matmul(pjb[:, sl], jrot[:, :], kh_bf[0:128, sl],
                                                        start=True, stop=True)
                                               for sl in (slice(0, 512), slice(512, 1024))],
                                    ("s", s_e))

                        def vrope(v, which=which, h=h):
                            v.tensor_tensor(tmp1[:, :], pja[:, :], cosb[:, :], OP.mult)
                            v.tensor_tensor(tmp2[:, :], pjb[:, :], sinb[:, :], OP.mult)
                            dst = ctx_all[:, h * T:(h + 1) * T] if which == "q" else kh_bf
                            v.tensor_tensor(dst, tmp1[:, :], tmp2[:, :], OP.add)
                        v_r = vstep(vrope, ("t", t_j))
                        if which == "k":
                            syncstep(lambda e, h=h: e.dma_start(out=kv_in[0, h, :, :],
                                                                in_=kh_bf
                                                                ).then_inc(s_ld, 16),
                                     ("v", v_r))
                            B.inc("ld", 16)

            B.mark()
            cc_kv = collective([kv_in.opt()], [kv_out.opt()], allld(), groups=PAIRS)
            B.mark()

            for h in range(NH):
                def ldkv(e, h=h, cc_kv=cc_kv):
                    wt(e, "cc", cc_kv)
                    for rq in range(2):
                        for kv, dstt in ((0, Ksb), (1, Vsb)):
                            src = kv_out[rq, kv, h, :, :]
                            dstap = dstt.rearrange("p (u two n) -> p u two n",
                                                        two=2, n=128)[:, :, rq, :]
                            e.dma_start(out=dstap, in_=src).then_inc(s_ld, 16)
                prev_t = B.val("t")
                prev_s = B.val("s")

                def ldkv_g(e, ldkv=ldkv, prev_t=prev_t, prev_s=prev_s):
                    wt(e, "t", prev_t)
                    wt(e, "s", prev_s)
                    ldkv(e)
                B.emit("sync", ldkv_g)
                B.inc("ld", 16 * 4)
                ld_kv = B.val("ld")

                t_v = tstep(lambda t: [t.transpose(trp[:, u * 128:(u + 1) * 128],
                                                   Vsb[:, u * 128:(u + 1) * 128],
                                                   ident[:, :]) for u in range(NKT)],
                            ("ld", ld_kv), alls())
                s_vtm = vstep(lambda v: v.tensor_copy(Vtm[:, :], trp[:, 0:2048]), ("t", t_v))

                s_q = sstep(lambda s, h=h: s.activation(qh_bf[:, :],
                                                        ctx_all[:, h * T:(h + 1) * T],
                                                        AF.Square), allv())
                s_k = vstep(lambda v: (v.tensor_tensor(Pbf[:, 0:T], Ksb[:, 0:T],
                                                       Ksb[:, 0:T], OP.mult),
                                       v.tensor_tensor(Pbf[:, T:SEQ], Ksb[:, T:SEQ],
                                                       Ksb[:, T:SEQ], OP.mult)),
                            ("ld", ld_kv))

                def tqn(t, s_q=s_q, s_k=s_k):
                    wt(t, "s", s_q)
                    wt(t, "v", s_k)
                    t.matmul(smp[0:1, 0:512], ones_bf[:, 0:1], qh_bf[:, 0:512],
                             start=True, stop=True)
                    t.matmul(spx[0:1, 0:512], ones_bf[:, 0:1], qh_bf[:, 512:1024],
                             start=True, stop=True)
                    for half in range(2):
                        t.matmul(pja[:, half * 512:(half + 1) * 512], ones_bf[:, :],
                                 Pbf[:, half * 512:(half + 1) * 512], start=True, stop=True)
                        t.matmul(pjb[:, half * 512:(half + 1) * 512], ones_bf[:, :],
                                 Pbf[:, T + half * 512:T + (half + 1) * 512],
                                 start=True, stop=True)
                t_n = tstep(tqn)

                def vkn(v, t_n=t_n):
                    v.reduce_max(k2col[:, :], pja[:, :], AX.X)
                    v.reduce_max(qncol[:, 0:1], pjb[:, :], AX.X)
                    v.tensor_tensor(k2col[:, :], k2col[:, :], qncol[:, 0:1], OP.max)
                    v.tensor_copy(srow[0:1, 0:512], smp[0:1, 0:512])
                    v.tensor_copy(srow[0:1, 512:1024], spx[0:1, 0:512])
                v_n = vstep(vkn, ("t", t_n))

                def tqc(t):
                    for jj in range(NQ):
                        t.matmul(smp[:, 256 + jj:257 + jj],
                                 srow[0:1, jj * 128:(jj + 1) * 128],
                                 ones_bf[0:1, 0:1], start=True, stop=True)
                t_c = tstep(tqc, ("v", v_n))
                v_c2 = vstep(lambda v: v.tensor_copy(qncol[:, :], smp[:, 256:256 + NQ]),
                             ("t", t_c))
                v_b = vstep(lambda v: (
                    v.tensor_tensor(bcol[:, :], qncol[:, :],
                                    k2col[:, :].broadcast_to([128, NQ]), OP.add),
                    v.tensor_scalar(bcol[:, :], bcol[:, :], -1.0 / 256.0, None, OP.mult)),
                    ("v", v_c2))

                exp_par = {0: 0, 1: 0}
                ptr_par = {0: 0, 1: 0}
                pv_par = {0: 0, 1: 0}
                trr_par = {0: 0, 1: 0}
                cn_hist = 0
                g = 0
                pv_last = 0
                for j in range(NQ):
                    w = (2 * j + 2) if causal else NKT
                    cw = w * 128
                    nslice = (cw + T - 1) // T
                    for sl_i in range(nslice):
                        par = g % 2
                        c0 = sl_i * T
                        c1 = min(cw, (sl_i + 1) * T)
                        nn = c1 - c0
                        Sp = pja if par == 0 else pjb
                        ph = par * T

                        def tsc(t, h=h, j=j, c0=c0, nn=nn, Sp=Sp):
                            for off in range(0, nn, 512):
                                t.matmul(Sp[:, off:off + min(512, nn - off)],
                                         ctx_all[:, h * T + j * 128:h * T + (j + 1) * 128],
                                         Ksb[:, c0 + off:c0 + off + min(512, nn - off)],
                                         start=True, stop=True)
                        t_sc = tstep(tsc, ("v", v_b), ("s", exp_par[par]))

                        if causal and sl_i == nslice - 1:
                            mlo = cw - 256 - c0
                            v_m = vstep(lambda v, Sp=Sp, mlo=mlo: (
                                v.tensor_tensor(Sp[:, mlo:mlo + 128], Sp[:, mlo:mlo + 128],
                                                mA[:, :], OP.add),
                                v.tensor_tensor(Sp[:, mlo + 128:mlo + 256],
                                                Sp[:, mlo + 128:mlo + 256], mB[:, :], OP.add)),
                                ("t", t_sc))
                            dep = ("v", v_m)
                        else:
                            dep = ("t", t_sc)

                        def sexp(s, j=j, sl_i=sl_i, nn=nn, Sp=Sp, ph=ph):
                            s.activation(Pbf[:, ph:ph + nn], Sp[:, 0:nn], AF.Exp,
                                         bias=bcol[:, j:j + 1], scale=ISQD,
                                         accum_out=(lcol[:, j:j + 1] if sl_i == 0
                                                    else rlcol[:, j:j + 1]))
                        s_e = sstep(sexp, dep)
                        exp_par[par] = s_e

                        ws = nn // 128
                        t_tr = tstep(lambda t, ph=ph, ws=ws: [t.transpose(
                            trp[:, ph + u * 128:ph + (u + 1) * 128],
                            Pbf[:, ph + u * 128:ph + (u + 1) * 128],
                            ident[:, :]) for u in range(ws)],
                            ("s", s_e), allv())
                        trr_par[par] = t_tr
                        v_ptr = vstep(lambda v, ph=ph, ws=ws: v.tensor_copy(
                            Ptrb[:, ph:ph + ws * 128], trp[:, ph:ph + ws * 128]),
                            ("t", t_tr), ("t", pv_par[par]))
                        ptr_par[par] = v_ptr

                        def tpv(t, j=j, sl_i=sl_i, nslice=nslice, ph=ph, ws=ws,
                                c0=c0):
                            o = (j % 4) * 128
                            for u in range(ws):
                                t.matmul(smp[:, o:o + 128],
                                         Ptrb[:, ph + u * 128:ph + (u + 1) * 128],
                                         Vtm[:, c0 + u * 128:c0 + (u + 1) * 128],
                                         start=(sl_i == 0 and u == 0),
                                         stop=(sl_i == nslice - 1 and u == ws - 1))
                        t_pv = tstep(tpv, ("v", v_ptr), ("v", cn_hist))
                        pv_par[par] = t_pv
                        pv_last = t_pv
                        g += 1

                    if nslice == 2:
                        vstep(lambda v, j=j: v.tensor_tensor(
                            lcol[:, j:j + 1], lcol[:, j:j + 1], rlcol[:, j:j + 1],
                            OP.add), ("s", B.val("s")))
                    v_rl = vstep(lambda v, j=j: v.reciprocal(rlcol[:, j:j + 1],
                                                             lcol[:, j:j + 1]),
                                 ("s", B.val("s")), ("v", B.val("v")))

                    if j % 4 == 3:
                        j0 = j - 3

                        def vcn4(v, j0=j0):
                            for jj in range(4):
                                v.tensor_tensor(
                                    qh_bf[0:128, jj * 128:(jj + 1) * 128],
                                    smp[:, jj * 128:(jj + 1) * 128],
                                    rlcol[:, j0 + jj:j0 + jj + 1].broadcast_to([128, 128]),
                                    OP.mult)
                        s_cn = vstep(vcn4, ("t", pv_last), ("v", v_rl))
                        cn_hist = s_cn
                        t_ct = tstep(lambda t: [t.transpose(
                            trp[:, 1536 + jj * 128:1536 + (jj + 1) * 128],
                            qh_bf[0:128, jj * 128:(jj + 1) * 128], ident[:, :])
                            for jj in range(4)], ("v", s_cn))
                        vstep(lambda v, h=h, j0=j0: v.tensor_copy(
                            ctx_all[:, h * T + j0 * 128:h * T + (j0 + 4) * 128],
                            trp[:, 1536:2048]), ("t", t_ct))

            if layer == 1:
                emit_l1_ctx_quant()
                osrc = xln
            else:
                osrc = ctx_all

            for oc in range(KC):
                wob = wqb[0] if oc % 2 == 0 else wkb[0]
                syncstep(lambda e, oc=oc, wob=wob: load_whead(e, 3, oc, wob),
                         allt())
                B.inc("ld", 16)
                ld_o = B.val("ld")

                def tom(t, oc=oc, ld_o=ld_o, osrc=osrc, wob=wob):
                    wt(t, "ld", ld_o)
                    dst = pja if (oc % 2 == 0) else pjb
                    for kcc in range(KC):
                        for half in range(2):
                            sl = slice(half * 512, (half + 1) * 512)
                            t.matmul(dst[:, sl], wob[:, kcc * 128:(kcc + 1) * 128],
                                     osrc[:, kcc * T + half * 512:kcc * T + (half + 1) * 512],
                                     start=(kcc == 0), stop=(kcc == KC - 1))
                t_o = tstep(tom, allv(), alls())

                if layer == 0:
                    vstep(lambda v, oc=oc: v.scalar_tensor_tensor(
                        x[:, oc * T:(oc + 1) * T],
                        (pja if oc % 2 == 0 else pjb)[:, :], 1.0,
                        x[:, oc * T:(oc + 1) * T], OP.mult, OP.add), ("t", t_o))
                else:
                    vstep(lambda v, oc=oc: (
                        v.tensor_tensor(tmp1[:, :],
                                        (pja if oc % 2 == 0 else pjb)[:, :],
                                        rep_b[:, :], OP.mult),
                        v.tensor_tensor(x[:, oc * T:(oc + 1) * T],
                                        x[:, oc * T:(oc + 1) * T], tmp1[:, :], OP.add)),
                        ("t", t_o))

        def emit_l0_ffn():
            l0ATb = Vtm
            l0Bb = Pbf
            gw_b = Ptrb
            gdma(lambda g: g.dma_start(out=l0ATb.rearrange("p (kc o) -> p kc o", o=128),
                                       in_=l0f_AT.rearrange("(kc p) o -> p kc o", p=128)),
                 allt(), alls())
            gdma(lambda g: g.dma_start(out=l0Bb[:, 0:D], in_=l0f_B))
            gdma(lambda g: g.dma_start(
                out=gw_b[:, 0:KC * 8].rearrange("p (kc o) -> p kc o", o=8),
                in_=l0f_gwT.rearrange("(kc p) o -> p kc o", p=128)))
            gdma(lambda g: g.dma_start(out=biasrow[:, :], in_=l0f_b_h))
            gdma(lambda g: g.dma_start(out=gbrow[:, :], in_=l0f_gb_h))
            gd_w = B.val("gd")

            def tgl(t, gd_w=gd_w):
                wt(t, "gd", gd_w)
                for kcc in range(KC):
                    for half, bank in ((0, smp), (1, spx)):
                        bankap = bank[0:8, 0:512]
                        t.matmul(bankap, gw_b[:, kcc * 8:(kcc + 1) * 8],
                                 xln[:, kcc * T + half * 512:kcc * T + (half + 1) * 512],
                                 start=(kcc == 0), stop=False)
                for half, bank in ((0, smp), (1, spx)):
                    t.matmul(bank[0:8, 0:512], gbrow[0:1, 0:8],
                             onesrow[0:1, 0:512], start=False, stop=True)
            t_gl = tstep(tgl, allv())
            v_gl = vstep(lambda v: (v.tensor_copy(tmp3[0:8, 0:512], smp[0:8, 0:512]),
                                    v.tensor_copy(tmp3[0:8, 512:1024], spx[0:8, 0:512])),
                         ("t", t_gl))

            for blk in range(NQ):
                t_tr = tstep(lambda t, blk=blk: t.transpose(
                    smp[:, 0:128], tmp3[0:8, blk * 128:(blk + 1) * 128] if False else
                    tmp3[0:128, blk * 128:(blk + 1) * 128], identf[:, :]),
                    ("v", B.val("v")))

                def vtop(v, blk=blk):
                    v.tensor_copy(gtok[:, 0:8], smp[:, 0:8])
                    v.reduce_max(gm1[:, :], gtok[:, 0:8], AX.X)
                    v.tensor_tensor(gtk2[:, 0:8], gtok[:, 0:8],
                                    gm1[:, :].broadcast_to([128, 8]), OP.is_equal)
                    v.scalar_tensor_tensor(gtk3[:, 0:8], gtk2[:, 0:8], -1e30,
                                           gtok[:, 0:8], OP.mult, OP.add)
                    v.reduce_max(gm2[:, :], gtk3[:, 0:8], AX.X)
                v_t1 = vstep(vtop, ("t", t_tr))
                v_t2 = vstep(lambda v: v.tensor_scalar(gm1[:, :], gm1[:, :], -1.0,
                                                       None, OP.mult), ("v", v_t1))
                s_ex2 = sstep(lambda s: s.activation(gtk2[:, 0:8], gtok[:, 0:8], AF.Exp,
                                                     bias=gm1[:, :], scale=1.0),
                              ("v", v_t2))

                def vw(v, blk=blk):
                    v.tensor_tensor(gtk3[:, 0:8], gtok[:, 0:8],
                                    gm2[:, :].broadcast_to([128, 8]), OP.is_ge)
                    v.tensor_tensor(gtk2[:, 0:8], gtk2[:, 0:8], gtk3[:, 0:8], OP.mult)
                    v.reduce_sum(gden[:, :], gtk2[:, 0:8], AX.X)
                    v.reciprocal(gden[:, :], gden[:, :])
                    v.tensor_tensor(gtok[:, 0:8], gtk2[:, 0:8], gden[:, :].broadcast_to([128, 8]), OP.mult)
                v_t3 = vstep(vw, ("s", s_ex2))
                t_tb = tstep(lambda t: t.transpose(smp[:, 128:256], gtok[:, 0:128],
                                                   identf[:, :]), ("v", v_t3))
                v_tb = vstep(lambda v, blk=blk: v.tensor_copy(
                    wselb[0:8, blk * 128:(blk + 1) * 128], smp[0:8, 128:256]), ("t", t_tb))

            t_w16 = tstep(lambda t: [t.matmul((pja if False else smp)[:, 0:512] if False
                                              else pja[:, half * 512:(half + 1) * 512],
                                              e16[:, :], wselb[0:8, half * 512:(half + 1) * 512],
                                              start=True, stop=True) for half in range(2)],
                          ("v", B.val("v")))
            v_w16 = vstep(lambda v: v.tensor_copy(wrepf[:, :], pja[:, :]), ("t", t_w16))

            def txa(t):
                for kcc in range(KC):
                    for half in range(2):
                        sl = slice(half * 512, (half + 1) * 512)
                        t.matmul(pjb[:, sl], l0ATb[:, kcc * 128:(kcc + 1) * 128],
                                 xln[:, kcc * T + half * 512:kcc * T + (half + 1) * 512],
                                 start=(kcc == 0), stop=(kcc == KC - 1))
            t_xa = tstep(txa, ("v", v_w16))
            v_xa = vstep(lambda v: (v.tensor_tensor(tmp1[:, :], pjb[:, :], wrepf[:, :],
                                                    OP.mult),
                                    v.tensor_copy(qh_bf[:, :], tmp1[:, :])), ("t", t_xa))

            wfb = [Ksb, Vsb]
            for oc in range(KC):
                gdma(lambda g, oc=oc: g.dma_start(
                    out=wfb[oc % 2][:, 0:D].rearrange("p (kc o) -> p kc o", o=128),
                    in_=l0fWT[:, oc * 128:(oc + 1) * 128].rearrange("(kc p) o -> p kc o", p=128)
                ), allt())
                gd_t = B.val("gd")

                def tb(t, oc=oc, gd_t=gd_t):
                    wt(t, "gd", gd_t)
                    dst = pja if oc % 2 == 0 else pjb
                    for kcc in range(KC):
                        for half in range(2):
                            sl = slice(half * 512, (half + 1) * 512)
                            t.matmul(dst[:, sl], wfb[oc % 2][:, kcc * 128:(kcc + 1) * 128],
                                     xln[:, kcc * T + half * 512:kcc * T + (half + 1) * 512],
                                     start=(kcc == 0), stop=False)
                    for half in range(2):
                        sl = slice(half * 512, (half + 1) * 512)
                        t.matmul(dst[:, sl], biasrow[0:1, oc * 128:(oc + 1) * 128],
                                 onesrow[0:1, 0:512], start=False, stop=False)
                        t.matmul(dst[:, sl], l0Bb[:, oc * 128:(oc + 1) * 128],
                                 qh_bf[:, sl], start=False, stop=True)
                t_b = tstep(tb, ("v", v_xa) if oc == 0 else allv())
                vstep(lambda v, oc=oc: v.scalar_tensor_tensor(
                    x[:, oc * T:(oc + 1) * T], (pja if oc % 2 == 0 else pjb)[:, :],
                    1.0, x[:, oc * T:(oc + 1) * T], OP.mult, OP.add), ("t", t_b))

        def emit_l1_ffn():
            gdma(lambda g: g.dma_start(
                out=rwbb.rearrange("p (kc o) -> p kc o", o=32),
                in_=l1f_rwT.rearrange("(kc p) o -> p kc o", p=128)),
                allt(), alls())
            gd_t = B.val("gd")

            def trt(t, gd_t=gd_t):
                wt(t, "gd", gd_t)
                for kcc in range(KC):
                    for half, bank in ((0, smp), (1, spx)):
                        t.matmul(bank[0:32, 0:512], rwbb[:, kcc * 32:(kcc + 1) * 32],
                                 xln[:, kcc * T + half * 512:kcc * T + (half + 1) * 512],
                                 start=(kcc == 0), stop=(kcc == KC - 1))
            t_r = tstep(trt, allv())
            v_r = vstep(lambda v: (v.tensor_copy(tmp3[0:32, 0:512], smp[0:32, 0:512]),
                                   v.tensor_copy(tmp3[0:32, 512:1024], spx[0:32, 0:512])),
                        ("t", t_r))

            for blk in range(NQ):
                t_tr = tstep(lambda t, blk=blk: t.transpose(
                    smp[:, 0:128], tmp3[0:128, blk * 128:(blk + 1) * 128], identf[:, :]),
                    ("v", B.val("v")))

                def vtop(v, blk=blk):
                    v.tensor_copy(gtok[:, 0:32], smp[:, 0:32])
                    v.reduce_max(gm1[:, :], gtok[:, 0:32], AX.X)
                    v.tensor_copy(gtk3[:, 0:32], gtok[:, 0:32])
                v_t1 = vstep(vtop, ("t", t_tr))
                for it in range(7):
                    def vit(v, it=it):
                        v.reduce_max(gm2[:, :], gtk3[:, 0:32], AX.X)
                        v.tensor_tensor(gtk2[:, 0:32], gtk3[:, 0:32],
                                        gm2[:, :].broadcast_to([128, 32]), OP.is_equal)
                        v.scalar_tensor_tensor(gtk3[:, 0:32], gtk2[:, 0:32], -1e30,
                                               gtk3[:, 0:32], OP.mult, OP.add)
                    vstep(vit)
                v_m8 = vstep(lambda v: v.reduce_max(gm2[:, :], gtk3[:, 0:32], AX.X))
                v_neg = vstep(lambda v: v.tensor_scalar(gm1[:, :], gm1[:, :], -1.0,
                                                        None, OP.mult))
                s_ex = sstep(lambda s: s.activation(gtk2[:, 0:32], gtok[:, 0:32], AF.Exp,
                                                    bias=gm1[:, :], scale=1.0),
                             ("v", B.val("v")))

                def vw(v, blk=blk):
                    v.tensor_tensor(gtk3[:, 0:32], gtok[:, 0:32],
                                    gm2[:, :].broadcast_to([128, 32]), OP.is_ge)
                    v.tensor_tensor(gtk2[:, 0:32], gtk2[:, 0:32], gtk3[:, 0:32], OP.mult)
                    v.reduce_sum(gden[:, :], gtk2[:, 0:32], AX.X)
                    v.reciprocal(gden[:, :], gden[:, :])
                    v.tensor_tensor(gtok[:, 0:32], gtk2[:, 0:32], gden[:, :].broadcast_to([128, 32]), OP.mult)
                v_t3 = vstep(vw, ("s", s_ex))
                t_tb = tstep(lambda t: t.transpose(smp[:, 128:256], gtok[:, 0:128],
                                                   identf[:, :]), ("v", v_t3))
                vstep(lambda v, blk=blk: v.tensor_copy(
                    wselb[0:32, blk * 128:(blk + 1) * 128], smp[0:32, 128:256]),
                    ("t", t_tb))

            piece_idx = [0]

            def load_piece(src_ap, dst, *waits):
                return gdma(lambda g, src_ap=src_ap, dst=dst:
                            g.dma_start(out=dst, in_=src_ap), *waits)

            def piece_srcs(e):
                return [
                    (l1f_wg[e, 0:1024, :].rearrange("(kc p) o -> p kc o", p=128),
                     ring, "wg0"),
                    (l1f_wg[e, 1024:2048, :].rearrange("(kc p) o -> p kc o", p=128),
                     ring, "wg1"),
                    (l1f_wu[e, 0:1024, :].rearrange("(kc p) o -> p kc o", p=128),
                     ring, "wu0"),
                    (l1f_wu[e, 1024:2048, :].rearrange("(kc p) o -> p kc o", p=128),
                     ring, "wu1"),
                    (l1f_wd[e, :, 0:1024].rearrange("(k6 p) o -> p k6 o", p=128),
                     ring, "wdA"),
                    (l1f_wd[e, :, 1024:2048].rearrange("(k6 p) o -> p k6 o", p=128),
                     ring, "wdB"),
                ]

            def slot(e, i):
                return (e * 6 + i) % 4

            gd_piece = {}
            for i in range(4):
                src, _, _ = piece_srcs(0)[i]
                dstap = ring[slot(0, i)].rearrange(
                    "p (a b) -> p a b", a=8 if i < 4 else 6)
                gd_piece[(0, i)] = load_piece(src, dstap, allt(), allv())

            for e in range(32):
                syncstep(lambda ee, e=e: ee.dma_start(
                    out=selbuf[:, :], in_=sel32_h[e, :].rearrange("(r n) -> r n", n=128)
                ).then_inc(s_ld, 16), allt())
                B.inc("ld", 16)
                ld_sel = B.val("ld")
                t_wr = tstep(lambda t, e=e, ld_sel=ld_sel: (
                    wt(t, "ld", ld_sel),
                    [t.matmul(pjb[:, half * 512:(half + 1) * 512], selbuf[:, :],
                              wselb[0:32, half * 512:(half + 1) * 512],
                              start=True, stop=True) for half in range(2)]), allv())
                v_wr = vstep(lambda v: v.tensor_copy(wrepf[:, :], pjb[:, :]), ("t", t_wr))

                last_si = 0
                last_h = v_wr
                for oc6 in range(6):
                    def tg(t, e=e, oc6=oc6):
                        wt(t, "gd", max(gd_piece[(e, 0)], gd_piece[(e, 1)]))
                        for kcc in range(KC):
                            piece = 0 if kcc < 8 else 1
                            rb = ring[slot(e, piece)]
                            kloc = kcc % 8
                            for half in range(2):
                                sl = slice(half * 512, (half + 1) * 512)
                                t.matmul(pja[:, sl],
                                         rb[:, kloc * 768 + oc6 * 128:kloc * 768 + (oc6 + 1) * 128],
                                         xln[:, kcc * T + half * 512:kcc * T + (half + 1) * 512],
                                         start=(kcc == 0), stop=(kcc == KC - 1))
                    t_g = tstep(tg, allv() if oc6 == 0 else ("s", last_si))
                    s_si = sstep(lambda s, oc6=oc6: s.activation(tmp3[:, :], pja[:, :],
                                                                 AF.Silu), ("t", t_g))
                    last_si = s_si
                    def tu(t, e=e, oc6=oc6):
                        wt(t, "gd", max(gd_piece[(e, 2)], gd_piece[(e, 3)]))
                        for kcc in range(KC):
                            piece = 2 if kcc < 8 else 3
                            rb = ring[slot(e, piece)]
                            kloc = kcc % 8
                            for half in range(2):
                                sl = slice(half * 512, (half + 1) * 512)
                                t.matmul(pjb[:, sl],
                                         rb[:, kloc * 768 + oc6 * 128:kloc * 768 + (oc6 + 1) * 128],
                                         xln[:, kcc * T + half * 512:kcc * T + (half + 1) * 512],
                                         start=(kcc == 0), stop=(kcc == KC - 1))
                    t_u = tstep(tu, ("v", last_h))
                    last_h = vstep(lambda v, oc6=oc6: (
                        v.tensor_tensor(tmp1[:, :], tmp3[:, :], pjb[:, :], OP.mult),
                        v.tensor_tensor(tmp2[:, :], tmp1[:, :], wrepf[:, :], OP.mult),
                        v.tensor_copy(hb[:, oc6 * T:(oc6 + 1) * T], tmp2[:, :])),
                        ("t", t_u), ("s", s_si))

                t_after_g = B.val("t")
                for i in (4, 5):
                    src, _, _ = piece_srcs(e)[i]
                    dstap = ring[slot(e, i)].rearrange("p (a b) -> p a b", a=6)
                    gd_piece[(e, i)] = load_piece(src, dstap, ("t", t_after_g))

                t_after_u = B.val("t")
                if e + 1 < 32:
                    for i in (0, 1):
                        src, _, _ = piece_srcs(e + 1)[i]
                        dstap = ring[slot(e + 1, i)].rearrange(
                            "p (a b) -> p a b", a=8)
                        gd_piece[(e + 1, i)] = load_piece(src, dstap, ("t", t_after_u))

                ev_hist = [last_h, last_h]
                for dm in range(KC):
                    def td(t, e=e, dm=dm):
                        wt(t, "gd", gd_piece[(e, 4 if dm < 8 else 5)])
                        for k6 in range(6):
                            piece = 4 if dm < 8 else 5
                            rb = ring[slot(e, piece)]
                            dloc = dm % 8
                            for half in range(2):
                                sl = slice(half * 512, (half + 1) * 512)
                                t.matmul((pja if dm % 2 == 0 else pjb)[:, sl],
                                         rb[:, k6 * 1024 + dloc * 128:k6 * 1024 + (dloc + 1) * 128],
                                         hb[:, k6 * T + half * 512:k6 * T + (half + 1) * 512],
                                         start=(k6 == 0), stop=(k6 == 5))
                    wdep = ("v", ev_hist[-2]) if dm >= 2 else ("v", last_h)
                    t_d = tstep(td, wdep)
                    ev = vstep(lambda v, dm=dm: v.scalar_tensor_tensor(
                        x[:, dm * T:(dm + 1) * T],
                        (pja if dm % 2 == 0 else pjb)[:, :], 1.0,
                        x[:, dm * T:(dm + 1) * T], OP.mult, OP.add), ("t", t_d))
                    ev_hist.append(ev)
                    if e + 1 < 32:
                        if dm == 7:
                            src, _, _ = piece_srcs(e + 1)[2]
                            dstap = ring[slot(e + 1, 2)].rearrange(
                                "p (a b) -> p a b", a=8)
                            gd_piece[(e + 1, 2)] = load_piece(src, dstap,
                                                             ("t", B.val("t")))
                        if dm == KC - 1:
                            src, _, _ = piece_srcs(e + 1)[3]
                            dstap = ring[slot(e + 1, 3)].rearrange(
                                "p (a b) -> p a b", a=8)
                            gd_piece[(e + 1, 3)] = load_piece(src, dstap,
                                                             ("t", B.val("t")))

        try:
            B.mark()
            emit_ln_stats()
            B.mark()
            emit_ln_apply()
            B.mark()
            if NBLOCK >= 1:
                emit_attention(0)
            if NBLOCK >= 2:
                emit_ln_stats()
                emit_ln_apply()
                emit_l0_ffn()
            if NBLOCK >= 3:
                emit_ln_stats()
                emit_l1_input_quant()
                emit_attention(1)
            if NBLOCK >= 4:
                emit_ln_stats()
                emit_ln_apply()
                emit_l1_ffn()
        except StopEmission:
            pass

        if DEVDUMP:
            def dumps(e):
                import contextlib
                ctxmgr = nc.allow_non_contiguous_dma(reason="debug dumps")
                ctxmgr.__enter__()
                e.dma_start(out=dbg_ctx, in_=ctx_all).then_inc(s_st, 16)
                e.dma_start(out=dbg_k, in_=Ksb).then_inc(s_st, 16)
                e.dma_start(out=dbg_v, in_=Vsb).then_inc(s_st, 16)
                e.dma_start(out=dbg_xln, in_=xln[:, :]).then_inc(s_st, 16)
                e.dma_start(out=dbg_cols[:, 0:8], in_=lcol[:, :]).then_inc(s_st, 16)
                e.dma_start(out=dbg_cols[:, 8:16], in_=bcol[:, :]).then_inc(s_st, 16)
                e.dma_start(out=dbg_cols[:, 16:24], in_=qncol[:, :]).then_inc(s_st, 16)
                e.dma_start(out=dbg_cols[:, 24:25], in_=k2col[:, :]).then_inc(s_st, 16)
                e.dma_start(out=dbg_cols[:, 25:33], in_=rlcol[:, :]).then_inc(s_st, 16)
                e.dma_start(out=dbg_cols[:, 33:34], in_=mean_rep[:, 0:1]).then_inc(s_st, 16)
                e.dma_start(out=dbg_cols[:, 34:35], in_=rstd_rep[:, 0:1]).then_inc(s_st, 16)
                e.dma_start(out=dbg_kvin, in_=kv_in).then_inc(s_st, 16)
                e.dma_start(out=dbg_p0in, in_=prep0_in).then_inc(s_st, 16)
                e.dma_start(out=dbg_p0out, in_=prep0_out[0:2]).then_inc(s_st, 16)
            B.inc("st", 16 * 14)
            B.emit("sync", W(dumps := dumps, ) if False else
                   (lambda e, f=dumps: ([wt(e, sn, vv) for sn, vv in
                                         (allv(), allt(), alls(), allgd())], f(e))))
        syncstep(lambda e: e.dma_start(
            out=out_fm.rearrange("(kc p) t -> p kc t", p=128),
            in_=x[:, :].rearrange("p (kc t) -> p kc t", t=T)).then_inc(s_st, 16),
            allv(), allt(), alls(), allgd())
        B.inc("st", 16)
        syncstep(lambda e: None, ("st", B.val("st")))

        block = ctx.enter_context(nc.Block())

        @block.sync
        def _(sync):
            for fn in B.q["sync"]:
                fn(sync)

        @block.gpsimd
        def _(g):
            for fn in B.q["gpsimd"]:
                fn(g)

        @block.vector
        def _(v):
            for fn in B.q["vector"]:
                fn(v)

        @block.scalar
        def _(s):
            for fn in B.q["scalar"]:
                fn(s)

        @block.tensor
        def _(t):
            for fn in B.q["tensor"]:
                fn(t)

    return nc



def _host_constants():
    ones = np.ones((128, 128), dtype=ml_dtypes.bfloat16)
    ident = np.eye(128, dtype=ml_dtypes.bfloat16)
    identf = np.eye(128, dtype=np.float32)
    J = np.zeros((128, 128), dtype=np.float32)
    J[np.arange(64), np.arange(64) + 64] = -1.0
    J[np.arange(64) + 64, np.arange(64)] = 1.0
    jrotT = J.T.astype(ml_dtypes.bfloat16)
    e16 = np.zeros((8, 128), dtype=ml_dtypes.bfloat16)
    for e in range(8):
        e16[e, e * 16:(e + 1) * 16] = 1.0
    sel8 = np.zeros((8, 8 * 128), dtype=ml_dtypes.bfloat16)
    for j in range(8):
        sel8[j, j * 128:(j + 1) * 128] = 1.0
    sel32 = np.zeros((32, 32 * 128), dtype=ml_dtypes.bfloat16)
    for e in range(32):
        sel32[e, e * 128:(e + 1) * 128] = 1.0
    tri = np.where(np.arange(128)[None, :] <= np.arange(128)[:, None], 0.0, -1e30
                   ).astype(np.float32)
    full = np.full((128, 128), -1e30, dtype=np.float32)
    zero = np.zeros((128, 128), dtype=np.float32)
    return ones, ident, identf, jrotT, e16, sel8, sel32, tri, full, zero


def _rope_tables_stacked():
    half = 64
    inv = (1.0 / 10000.0 ** (np.arange(half, dtype=np.float64) / half))
    tpos = np.arange(SEQ, dtype=np.float64)
    fr = np.outer(tpos, inv)
    cos = np.cos(fr).astype(np.float32)
    sin = np.sin(fr).astype(np.float32)
    c2 = np.concatenate([cos, cos], axis=1).T.copy()
    s2 = np.concatenate([sin, sin], axis=1).T.copy()
    return c2, s2


_NC_CACHE = {}


def _get_nc():
    key = NBLOCK
    if key not in _NC_CACHE:
        _NC_CACHE[key] = build_program()
    return _NC_CACHE[key]


def kernel(**inputs):
    f32 = np.float32
    xfull = np.ascontiguousarray(inputs["L0_attn_x"], dtype=f32)
    ones, ident, identf, jrotT, e16, sel8, sel32, tri, full, zero = _host_constants()
    c2, s2 = _rope_tables_stacked()

    wmap = {
        "l0q": inputs["L0_attn_w_q"], "l0k": inputs["L0_attn_w_k"],
        "l0v": inputs["L0_attn_w_v"], "l0o": inputs["L0_attn_w_o"],
        "l1q": inputs["L1_attn_w_q"], "l1k": inputs["L1_attn_w_k"],
        "l1v": inputs["L1_attn_w_v"], "l1o": inputs["L1_attn_w_o"],
    }
    l0fWT = np.ascontiguousarray(inputs["L0_ffn_W"].T, dtype=f32)
    l0f_b = np.ascontiguousarray(inputs["L0_ffn_b"], dtype=f32).reshape(1, D)
    l0f_AT = np.ascontiguousarray(
        inputs["L0_ffn_lora_A"].reshape(128, D).T, dtype=f32)
    l0f_B = np.ascontiguousarray(
        np.swapaxes(inputs["L0_ffn_lora_B"], 1, 2).reshape(128, D), dtype=f32)
    l0f_gwT = np.ascontiguousarray(inputs["L0_ffn_gate_w"].T, dtype=f32)
    l0f_gb = np.ascontiguousarray(inputs["L0_ffn_gate_b"], dtype=f32).reshape(1, 8)
    l1f_rwT = np.ascontiguousarray(inputs["L1_ffn_router_w"].T, dtype=f32)
    l1f_wg = np.ascontiguousarray(inputs["L1_ffn_w_gate"], dtype=f32)
    l1f_wu = np.ascontiguousarray(inputs["L1_ffn_w_up"], dtype=f32)
    l1f_wd = np.ascontiguousarray(inputs["L1_ffn_w_down"], dtype=f32)

    in_maps = []
    for c in range(NCORE):
        b, p = c // 2, c % 2
        gtiles = [2 * j + p for j in range(NQ)]
        cols = np.concatenate([np.arange(g * 128, (g + 1) * 128) for g in gtiles])
        xs = np.ascontiguousarray(xfull[b].T[:, cols])
        im = {
            "x_fm": xs,
            "cos_own": np.ascontiguousarray(c2[:, cols]),
            "sin_own": np.ascontiguousarray(s2[:, cols]),
            "maskA": tri if p == 0 else zero,
            "maskB": full if p == 0 else tri,
            "ones128": ones, "ident": ident, "identf": identf,
            "jrot": jrotT, "e16": e16, "sel8": sel8, "sel32": sel32,
            "l0fWT": l0fWT, "l0f_b": l0f_b, "l0f_AT": l0f_AT, "l0f_B": l0f_B,
            "l0f_gwT": l0f_gwT, "l0f_gb": l0f_gb, "l1f_rwT": l1f_rwT,
            "l1f_wg": l1f_wg, "l1f_wu": l1f_wu, "l1f_wd": l1f_wd,
        }
        for nm, w in wmap.items():
            im[nm + "_s"] = np.ascontiguousarray(
                np.asarray(w, dtype=f32)[c * 256:(c + 1) * 256, :])
        in_maps.append(im)

    nc = _get_nc()
    res = run_bass_kernel_spmd(nc, in_maps, core_ids=list(range(NCORE)))

    out = np.empty((4, SEQ, D), dtype=f32)
    for c in range(NCORE):
        b, p = c // 2, c % 2
        gtiles = [2 * j + p for j in range(NQ)]
        cols = np.concatenate([np.arange(g * 128, (g + 1) * 128) for g in gtiles])
        of = np.asarray(res.results[c]["out_fm"])
        out[b, cols, :] = of.T
    return out


# version 06
# speedup vs baseline: 1.0447x; 1.0447x over previous
import math
from contextlib import ExitStack

import numpy as np
import ml_dtypes

import concourse.bass as bass
import concourse.mybir as mybir
from concourse.bass_utils import run_bass_kernel_spmd

F32 = mybir.dt.float32
BF16 = mybir.dt.bfloat16
I32 = mybir.dt.int32
AF = mybir.ActivationFunctionType
OP = mybir.AluOpType
AX = mybir.AxisListType

D = 2048
T = 1024
NH = 16
HD = 128
KC = D // 128
NQ = T // 128
SEQ = 2048
NKT = SEQ // 128
NCORE = 8
NBLOCK = 4
PHASES = 99
DEVDUMP = False

EPS_LN = 1e-5
EPS_L0Q = 1e-8
EPS_L1 = 1e-5
QB = 127.0
ISQD = 1.0 / math.sqrt(128.0)

ENGINES = ("sync", "gpsimd", "vector", "scalar", "tensor")


class StopEmission(Exception):
    pass


class Build:
    def __init__(self):
        self.q = {e: [] for e in ENGINES}
        self.ctr = {}
        self.phase = 0
        self.stopped = False

    def mark(self):
        self.phase += 1
        if self.phase >= PHASES:
            self.stopped = True

    def go(self):
        return not self.stopped

    def inc(self, sem, by=1):
        if not self.stopped:
            self.ctr[sem] = self.ctr.get(sem, 0) + by
        return self.ctr.get(sem, 0)

    def val(self, sem):
        return self.ctr.get(sem, 0)

    def emit(self, eng, fn):
        if not self.stopped:
            self.q[eng].append(fn)


def build_program():
    nc = bass.Bass("TRN2", target_bir_lowering=False, debug=False,
                   detect_race_conditions=False)

    di = lambda n, s, dt=F32: nc.dram_tensor(n, s, dt, kind="ExternalInput").ap()
    x_fm = di("x_fm", [D, T])
    cos_own_h = di("cos_own", [128, T])
    sin_own_h = di("sin_own", [128, T])
    maskA_h = di("maskA", [128, 128])
    maskB_h = di("maskB", [128, 128])
    ones_h = di("ones128", [128, 128], BF16)
    ident_h = di("ident", [128, 128], BF16)
    identf_h = di("identf", [128, 128], F32)
    jrot_h = di("jrot", [128, 128], BF16)
    e16_h = di("e16", [8, 128], BF16)
    sel8_h = di("sel8", [8, 8 * 128], BF16)
    sel32_h = di("sel32", [32, 32 * 128], BF16)
    wsl = {nm: di(nm + "_s", [256, D]) for nm in
           ("l0q", "l0k", "l0v", "l0o", "l1q", "l1k", "l1v", "l1o")}
    l0fWT = di("l0fWT", [D, D])
    l0f_b_h = di("l0f_b", [1, D])
    l0f_AT = di("l0f_AT", [D, 128])
    l0f_B = di("l0f_B", [128, D])
    l0f_gwT = di("l0f_gwT", [D, 8])
    l0f_gb_h = di("l0f_gb", [1, 8])
    l1f_rwT = di("l1f_rwT", [D, 32])
    l1f_wg = di("l1f_wg", [32, D, 768])
    l1f_wu = di("l1f_wu", [32, D, 768])
    l1f_wd = di("l1f_wd", [32, 768, D])
    out_fm = nc.dram_tensor("out_fm", [D, T], F32, kind="ExternalOutput").ap()
    if DEVDUMP:
        dbg_ctx = nc.dram_tensor("dbg_ctx", [128, NH * T], BF16, kind="ExternalOutput").ap()
        dbg_k = nc.dram_tensor("dbg_k", [128, SEQ], BF16, kind="ExternalOutput").ap()
        dbg_v = nc.dram_tensor("dbg_v", [128, SEQ], BF16, kind="ExternalOutput").ap()
        dbg_xln = nc.dram_tensor("dbg_xln", [128, KC * T], BF16, kind="ExternalOutput").ap()
        dbg_cols = nc.dram_tensor("dbg_cols", [128, 64], F32, kind="ExternalOutput").ap()
        dbg_kvin = nc.dram_tensor("dbg_kvin", [2, NH, HD, T], BF16, kind="ExternalOutput").ap()
        dbg_p0in = nc.dram_tensor("dbg_p0in", [4, D, 256], BF16, kind="ExternalOutput").ap()
        dbg_p0out = nc.dram_tensor("dbg_p0out", [2, 4, D, 256], BF16, kind="ExternalOutput").ap()

    prep0_in = nc.dram_tensor("prep0_in", [4, D, 256], BF16).ap()
    prep0_out = nc.dram_tensor("prep0_out", [NCORE, 4, D, 256], BF16).ap()
    prep1_in = nc.dram_tensor("prep1_in", [4, D, 256], BF16).ap()
    prep1_out = nc.dram_tensor("prep1_out", [NCORE, 4, D, 256], BF16).ap()
    sums_in = nc.dram_tensor("sums_in", [4, 1], F32).ap()
    sums_out = nc.dram_tensor("sums_out", [NCORE * 4, 1], F32).ap()
    kv_in = nc.dram_tensor("kv_in", [NH, 2, HD, T], BF16).ap()
    kv_outA = nc.dram_tensor("kv_outA", [2, 8, 2, HD, T], BF16).ap()
    kv_outB = nc.dram_tensor("kv_outB", [2, 8, 2, HD, T], BF16).ap()
    SHARD = 2 * NH * HD * T

    B = Build()
    ALL8 = [list(range(NCORE))]

    with ExitStack() as ctx:
        sbt = lambda n, s, dt=F32: ctx.enter_context(nc.sbuf_tensor(n, s, dt))
        pst = lambda n, s, dt=F32: ctx.enter_context(nc.psum_tensor(n, s, dt))
        sem = lambda n: ctx.enter_context(nc.semaphore(n))

        x = sbt("x", [128, KC * T])
        xln = sbt("xln", [128, KC * T], BF16)
        mean_rep = sbt("mean_rep", [128, T])
        rstd_rep = sbt("rstd_rep", [128, T])
        rep_a = sbt("rep_a", [128, T])
        rep_b = sbt("rep_b", [128, T])
        tmp1 = sbt("tmp1", [128, T])
        tmp2 = sbt("tmp2", [128, T])
        tmp3 = sbt("tmp3", [128, T])
        cosb = sbt("cosb", [128, T])
        sinb = sbt("sinb", [128, T])
        ones_bf = sbt("ones_bfb", [128, 128], BF16)
        ident = sbt("identb", [128, 128], BF16)
        identf = sbt("identfb", [128, 128])
        jrot = sbt("jrotb", [128, 128], BF16)
        e16 = sbt("e16b", [8, 128], BF16)
        selbuf = sbt("selbuf", [32, 128], BF16)
        mA = sbt("mA", [128, 128])
        mB = sbt("mB", [128, 128])
        fone = sbt("fone", [128, 1])
        onesrow = sbt("onesrow", [1, 512], BF16)
        qncol = sbt("qncol", [128, NQ])
        k2col = sbt("k2col", [128, 1])
        bcol = sbt("bcol", [128, NQ])
        lcol = sbt("lcol", [128, NQ])
        rlcol = sbt("rlcol", [128, NQ])
        gcolM = sbt("gcolM", [128, NQ])
        gcolR = sbt("gcolR", [128, NQ])
        gcolA = sbt("gcolA", [128, NQ])
        gcolB = sbt("gcolB", [128, NQ])
        brow = sbt("brow", [8, 256], BF16)
        sums8 = sbt("sums8", [32, 1])
        scol = sbt("scol", [128, 4])
        thcol = sbt("thcol", [128, 4])
        gtok = sbt("gtok", [128, 128])
        gtk2 = sbt("gtk2", [128, 32])
        gtk3 = sbt("gtk3", [128, 32])
        gm1 = sbt("gm1", [128, 1])
        gm2 = sbt("gm2", [128, 1])
        gden = sbt("gden", [128, 1])
        biasrow = sbt("biasrow", [1, D], BF16)
        gbrow = sbt("gbrow", [1, 8], BF16)
        wselb = sbt("wselb", [32, T], BF16)
        wrepf = sbt("wrepf", [128, T])

        tint = tmp2.ap().bitcast(I32)[:, 0:T]
        Mmax = tmp3

        s_ld = sem("s_ld")
        s_gd = sem("s_gd")
        s_cc = sem("s_cc")
        s_t = sem("s_t")
        s_v = sem("s_v")
        s_s = sem("s_s")
        s_st = sem("s_st")
        SEMS = {"ld": s_ld, "gd": s_gd, "cc": s_cc, "t": s_t, "v": s_v,
                "s": s_s, "st": s_st}

        def wt(e, semname, val):
            if val > 0:
                e.wait_ge(SEMS[semname], val)

        def dma(eng, out, in_, semname="ld", qsem=None):
            tgt = B.inc(semname, 16)

            def f(e, out=out, in_=in_, tgt=tgt):
                e.dma_start(out=out, in_=in_).then_inc(SEMS[semname], 16)
            B.emit(eng, f)
            return tgt

        s_vx = sem("s_vx")
        s_sx = sem("s_sx")
        s_gx = sem("s_gx")
        AUX = {"vector": s_vx, "scalar": s_sx, "gpsimd": s_gx}

        class _Rec:

            def __init__(self, eng, aux, ctr):
                self._e = eng
                self._aux = aux
                self._ctr = ctr
                self.last = None

            def __getattr__(self, name):
                a = getattr(self._e, name)
                if callable(a):
                    def call(*args, _a=a, _n=name, **kw):
                        if self._aux is not None and _n not in (
                                "wait_ge", "sem_inc", "nop", "partition_id",
                                "compute_val"):
                            if self._ctr[0] > 0:
                                self._e.wait_ge(self._aux, self._ctr[0])
                        r = _a(*args, **kw)
                        if hasattr(r, "then_inc"):
                            self.last = r
                            if self._aux is not None and _n not in (
                                    "wait_ge", "sem_inc", "nop"):
                                r.then_inc(self._aux, 1)
                                self._ctr[0] += 1
                        return r
                    return call
                return a

        AUXCTR = {"vector": [0], "scalar": [0], "gpsimd": [0]}

        def _mkstep(eng, semname, semh):
            aux = AUX.get(eng)
            actr = AUXCTR.get(eng)

            def step(fn, *waits):
                B.inc(semname)

                def f(e, fn=fn, waits=waits):
                    for sname, val in waits:
                        wt(e, sname, val)
                    rec = _Rec(e, aux, actr)
                    fn(rec)
                    assert rec.last is not None, "step emitted no instruction"
                    if aux is not None:
                        e.wait_ge(aux, actr[0])
                        e.sem_inc(semh, 1)
                    else:
                        rec.last.then_inc(semh, 1)
                B.emit(eng, f)
                return B.val(semname)
            return step

        vstep = _mkstep("vector", "v", s_v)
        sstep = _mkstep("scalar", "s", s_s)
        tstep = _mkstep("tensor", "t", s_t)
        gstep = _mkstep("gpsimd", "gd", s_gd)

        def gdma(fn, *waits):
            B.inc("gd", 16)

            def f(g, fn=fn, waits=waits):
                for sname, val in waits:
                    wt(g, sname, val)
                fn(g).then_inc(s_gd, 16)
            B.emit("gpsimd", f)
            return B.val("gd")

        def syncstep(fn, *waits):
            def f(e, fn=fn, waits=waits):
                for sname, val in waits:
                    wt(e, sname, val)
                fn(e)
            B.emit("sync", f)

        PAIRS = [[0, 1], [2, 3], [4, 5], [6, 7]]

        def collective(ins, outs, *waits, groups=None):
            tgt = B.inc("cc", 1)
            g_ = groups or ALL8

            def f(g, ins=ins, outs=outs, tgt=tgt, waits=waits, g_=g_):
                for sname, val in waits:
                    wt(g, sname, val)
                g.collective_compute("AllGather", OP.bypass, replica_groups=g_,
                                     ins=ins, outs=outs).then_inc(s_cc, 1)
            B.emit("gpsimd", f)
            return tgt

        def allv():
            return ("v", B.val("v"))

        def alls():
            return ("s", B.val("s"))

        def allt():
            return ("t", B.val("t"))

        def allld():
            return ("ld", B.val("ld"))

        def allgd():
            return ("gd", B.val("gd"))

        def allcc():
            return ("cc", B.val("cc"))

        pja = pst("pja", [128, T])
        pjb = pst("pjb", [128, T])
        trp = pst("trp", [128, 2048], BF16)
        smp = pst("smp", [128, 512])
        spx = pst("spx", [128, 512])

        for o, i in ((x[:, :].rearrange("p (kc t) -> p kc t", t=T),
                      x_fm.rearrange("(kc p) t -> p kc t", p=128)),
                     (cosb[:, :], cos_own_h), (sinb[:, :], sin_own_h),
                     (mA[:, :], maskA_h), (mB[:, :], maskB_h),
                     (ones_bf[:, :], ones_h), (ident[:, :], ident_h),
                     (identf[:, :], identf_h), (jrot[:, :], jrot_h),
                     (e16[:, :], e16_h)):
            dma("sync", o, i)
        LD_INIT = B.val("ld")
        vstep(lambda v: (v.memset(onesrow[:, :], 1.0), v.memset(fone[:, :], 1.0)))
        B.mark()


        with nc.sbuf_tensor("qsl0", [128, D], F32) as qsl0, \
             nc.sbuf_tensor("qsl1", [128, D], F32) as qsl1, \
             nc.sbuf_tensor("qc1", [128, D], F32) as qc1, \
             nc.sbuf_tensor("qtr", [128, D], BF16) as qtr, \
             nc.sbuf_tensor("qsc", [128, 16], F32) as qsc, \
             nc.sbuf_tensor("qth", [128, 16], F32) as qth:
            qsl = [qsl0, qsl1]

            def emit_prep(names, prep_in_ap, l1_mode):
                for wi, nm in enumerate(names):
                    for rt in range(2):
                        syncstep(lambda e, nm=nm, rt=rt:
                                 e.dma_start(out=qsl[rt][:, :],
                                             in_=wsl[nm][rt * 128:(rt + 1) * 128, :]
                                             ).then_inc(s_ld, 16), allv())
                        B.inc("ld", 16)
                        ld_t = B.val("ld")

                        def vq(v, rt=rt, wi=wi, l1_mode=l1_mode):
                            if l1_mode:
                                thr1 = thcol[:, wi:wi + 1]
                                scl1 = scol[:, wi:wi + 1]
                                thr = lambda g: thr1.broadcast_to([128, 128])
                                scl = lambda g: scl1.broadcast_to([128, 128])
                            else:
                                v.tensor_reduce(qsc[:, :],
                                                qsl[rt][:, :].rearrange("p (g n) -> p g n", g=16),
                                                AX.X, OP.add, apply_absolute_value=True)
                                v.tensor_scalar(qsc[:, :], qsc[:, :], 1.0 / 128, EPS_L0Q,
                                                OP.mult, OP.max)
                                v.tensor_scalar(qth[:, :], qsc[:, :], 0.5, None, OP.mult)
                                thr = lambda g: qth[:, g:g + 1].broadcast_to([128, 128])
                                scl = lambda g: qsc[:, g:g + 1].broadcast_to([128, 128])
                            for g in range(16):
                                sl = slice(g * 128, (g + 1) * 128)
                                v.tensor_tensor(qc1[:, sl], qsl[rt][:, sl], thr(g), OP.is_ge)
                            v.tensor_scalar(qsl[rt][:, :], qsl[rt][:, :], -1.0, None, OP.mult)
                            for g in range(16):
                                sl = slice(g * 128, (g + 1) * 128)
                                v.tensor_tensor(qsl[rt][:, sl], qsl[rt][:, sl], thr(g), OP.is_le)
                            v.tensor_tensor(qc1[:, :], qc1[:, :], qsl[rt][:, :], OP.add)
                            for g in range(16):
                                sl = slice(g * 128, (g + 1) * 128)
                                v.scalar_tensor_tensor(qtr[:, sl], qc1[:, sl], -1.0, scl(g),
                                                       OP.add, OP.mult)
                        v_t = vstep(vq, ("ld", ld_t), alls())

                        t_t = tstep(lambda t: [t.transpose(trp[:, kc * 128:(kc + 1) * 128],
                                                           qtr[:, kc * 128:(kc + 1) * 128],
                                                           ident[:, :])
                                               for kc in range(KC)], ("v", v_t), alls())
                        s_t2 = sstep(lambda s: s.copy(qtr[:, :], trp[:, 0:D]), ("t", t_t))
                        syncstep(lambda e, wi=wi, rt=rt:
                                 e.dma_start(
                                     out=prep_in_ap[wi, :, rt * 128:(rt + 1) * 128].rearrange(
                                         "(kc p) o -> p kc o", p=128),
                                     in_=qtr.ap().rearrange("p (kc o) -> p kc o", o=128)
                                 ).then_inc(s_ld, 16), ("s", s_t2))
                        B.inc("ld", 16)

            emit_prep(("l0q", "l0k", "l0v", "l0o"), prep0_in, l1_mode=False)
            B.mark()
            cc_prep0 = collective([prep0_in.opt()], [prep0_out.opt()], allld())
            B.mark()

            vstep(lambda v: v.memset(Mmax[:, 0:16], 0.0))
            for wi, nm in enumerate(("l1q", "l1k", "l1v", "l1o")):
                for rt in range(2):
                    syncstep(lambda e, nm=nm, rt=rt:
                             e.dma_start(out=qsl[rt][:, :],
                                         in_=wsl[nm][rt * 128:(rt + 1) * 128, :]
                                         ).then_inc(s_ld, 16), allv())
                    B.inc("ld", 16)
                    ld_t = B.val("ld")

                    def vs(v, wi=wi, rt=rt):
                        v.tensor_reduce(Mmax[:, 8 + wi:9 + wi], qsl[rt][:, :], AX.X,
                                        OP.add, apply_absolute_value=True)
                        v.tensor_tensor(Mmax[:, wi:wi + 1], Mmax[:, wi:wi + 1],
                                        Mmax[:, 8 + wi:9 + wi], OP.add)
                    vstep(vs, ("ld", ld_t))
            t_t = tstep(lambda t: t.matmul(smp[0:4, 0:1], Mmax[:, 0:4], fone[:, :],
                                           start=True, stop=True), allv())
            v_t = vstep(lambda v: v.tensor_copy(Mmax[0:4, 16:17] if False else tmp1[0:4, 0:1],
                                                smp[0:4, 0:1]), ("t", t_t))
            syncstep(lambda e: e.dma_start(out=sums_in, in_=tmp1[0:4, 0:1]
                                           ).then_inc(s_ld, 16), ("v", v_t))
            B.inc("ld", 16)
            B.mark()
            cc_sums = collective([sums_in.opt()], [sums_out.opt()], allld())
            B.mark()

            syncstep(lambda e: e.dma_start(out=sums8[:, :], in_=sums_out
                                           ).then_inc(s_ld, 16), allcc())
            B.inc("ld", 16)
            ld_t = B.val("ld")
            syncstep(lambda e: e.dma_start(
                out=gtok[0:8, 0:4], in_=sums_out.rearrange("(r w) one -> r (w one)", w=4)
            ).then_inc(s_ld, 16))
            B.inc("ld", 16)
            ld_t = B.val("ld")
            v_t = vstep(lambda v: v.memset(mean_rep[0:8, 0:128], 1.0), ("ld", ld_t))
            t_t = tstep(lambda t: t.matmul(smp[:, 4:8], mean_rep[0:8, 0:128],
                                           gtok[0:8, 0:4], start=True, stop=True),
                        ("v", v_t))
            vstep(lambda v: (v.tensor_scalar(scol[:, 0:4], smp[:, 4:8],
                                             1.0 / (D * D), EPS_L1, OP.mult, OP.add),
                             v.tensor_scalar(thcol[:, 0:4], scol[:, 0:4], 0.5, None,
                                             OP.mult)),
                  ("t", t_t))

            B.mark()
            emit_prep(("l1q", "l1k", "l1v", "l1o"), prep1_in, l1_mode=True)
            B.mark()
            cc_prep1 = collective([prep1_in.opt()], [prep1_out.opt()], allld())
            B.mark()

        def emit_ln_stats():
            for kc in range(KC):
                xb = x2b[:, 0:512] if False else None
                s1 = sstep(lambda s, kc=kc: s.copy(xln[:, kc * T:(kc + 1) * T],
                                                   x[:, kc * T:(kc + 1) * T]),
                           *([("ld", LD_INIT), allv(), allt()] if kc == 0 else []))
                half_buf = x2b.ap().rearrange("p (two t) -> p two t", two=2)[:, kc % 2, :] \
                    if False else None
                def ssq(s, kc=kc, prev_t=B.val("t")):
                    wt(s, "t", prev_t)
                    s.activation(x2b[:, 0:T], xln[:, kc * T:(kc + 1) * T], AF.Square)
                s2 = sstep(ssq)

                def tmm(t, kc=kc):
                    for half in range(2):
                        sl = slice(half * 512, (half + 1) * 512)
                        t.matmul(pja[:, sl], ones_bf[:, :],
                                 xln[:, kc * T + half * 512:kc * T + (half + 1) * 512],
                                 start=(kc == 0), stop=(kc == KC - 1))
                        t.matmul(pjb[:, sl], ones_bf[:, :], x2b[:, sl],
                                 start=(kc == 0), stop=(kc == KC - 1))
                tstep(tmm, ("s", s2))
            t_stats = B.val("t")

            def vst(v):
                v.tensor_scalar(mean_rep[:, :], pja[:, :], 1.0 / D, None, OP.mult)
                v.tensor_scalar(tmp1[:, :], pjb[:, :], 1.0 / D, None, OP.mult)
                v.tensor_tensor(tmp2[:, :], mean_rep[:, :], mean_rep[:, :], OP.mult)
                v.tensor_tensor(tmp1[:, :], tmp1[:, :], tmp2[:, :], OP.subtract)
                v.tensor_scalar(tmp1[:, :], tmp1[:, :], EPS_LN, None, OP.add)
            v_t = vstep(vst, ("t", t_stats))
            s_t2 = sstep(lambda s: s.activation(tmp2[:, :], tmp1[:, :], AF.Sqrt),
                         ("v", v_t))
            vstep(lambda v: v.reciprocal(rstd_rep[:, :], tmp2[:, :]), ("s", s_t2))

        def emit_ln_apply():
            for kc in range(KC):
                def va(v, kc=kc):
                    v.scalar_tensor_tensor(tmp1[:, :], mean_rep[:, :], -1.0,
                                           x[:, kc * T:(kc + 1) * T], OP.mult, OP.add)
                    v.tensor_tensor(xln[:, kc * T:(kc + 1) * T], tmp1[:, :],
                                    rstd_rep[:, :], OP.mult)
                vstep(va, *([allt()] if kc == 0 else []))

        def emit_colrow_broadcast(colA, colB, repA, repB):
            syncstep(lambda e: e.dma_start(out=sel8b, in_=sel8_h).then_inc(s_ld, 16),
                     allt(), alls())
            B.inc("ld", 16)
            ld_s8 = B.val("ld")
            v_t = vstep(lambda v: (v.memset(tmp1[:, 0:128], 0.0),
                                   v.memset(tmp2[:, 0:128], 0.0),
                                   v.tensor_copy(tmp1[:, 0:NQ], colA),
                                   v.tensor_copy(tmp2[:, 0:NQ], colB)))
            t_t = tstep(lambda t: (t.transpose(smp[:, 0:128], tmp1[:, 0:128], identf[:, :]),
                                   t.transpose(smp[:, 128:256], tmp2[:, 0:128], identf[:, :])),
                        ("v", v_t))
            v_t = vstep(lambda v: (v.tensor_copy(brow[0:NQ, 0:128], smp[0:NQ, 0:128]),
                                   v.tensor_copy(brow[0:NQ, 128:256], smp[0:NQ, 128:256])),
                        ("t", t_t))

            def tbc(t, ld_s8=ld_s8):
                wt(t, "ld", ld_s8)
                for j in range(NQ):
                    t.matmul(pja[:, j * 128:(j + 1) * 128],
                             sel8b[:, j * 128:(j + 1) * 128],
                             brow[0:NQ, 0:128], start=True, stop=True)
                    t.matmul(pjb[:, j * 128:(j + 1) * 128],
                             sel8b[:, j * 128:(j + 1) * 128],
                             brow[0:NQ, 128:256], start=True, stop=True)
            t_t = tstep(tbc, ("v", v_t))
            vstep(lambda v: (v.tensor_copy(repA, pja[:, :]),
                             v.tensor_copy(repB, pjb[:, :])), ("t", t_t))

        def emit_maxcols(src_is_ln):
            vstep(lambda v: v.memset(Mmax[:, 0:T], 0.0))
            for kc in range(KC):
                if src_is_ln:
                    v_t = vstep(lambda v, kc=kc: v.scalar_tensor_tensor(
                        tmp1[:, :], mean_rep[:, :], -1.0, x[:, kc * T:(kc + 1) * T],
                        OP.mult, OP.add))
                    s_t2 = sstep(lambda s: s.activation(tmp2[:, :], tmp1[:, :], AF.Abs),
                                 ("v", v_t))
                else:
                    s_t2 = sstep(lambda s, kc=kc: s.activation(
                        tmp2[:, :], ctx_all[:, kc * T:(kc + 1) * T], AF.Abs), allv())
                vstep(lambda v: v.tensor_max(Mmax[:, 0:T], Mmax[:, 0:T], tmp2[:, :]),
                      ("s", s_t2))
            for batch in range(2):
                t_t = tstep(lambda t, batch=batch: [t.transpose(
                    smp[:, jj * 128:(jj + 1) * 128],
                    Mmax[:, (batch * 4 + jj) * 128:(batch * 4 + jj + 1) * 128],
                    identf[:, :]) for jj in range(4)], allv())
                vstep(lambda v, batch=batch: [v.reduce_max(
                    gcolM[:, batch * 4 + jj:batch * 4 + jj + 1],
                    smp[:, jj * 128:(jj + 1) * 128], AX.X) for jj in range(4)],
                    ("t", t_t))
            if src_is_ln:
                for batch in range(2):
                    t_t = tstep(lambda t, batch=batch: [t.transpose(
                        smp[:, jj * 128:(jj + 1) * 128],
                        rstd_rep[:, (batch * 4 + jj) * 128:(batch * 4 + jj + 1) * 128],
                        identf[:, :]) for jj in range(4)], allv())
                    vstep(lambda v, batch=batch: [v.tensor_copy(
                        gcolR[:, batch * 4 + jj:batch * 4 + jj + 1],
                        smp[:, jj * 128:jj * 128 + 1]) for jj in range(4)], ("t", t_t))

        def emit_l1_input_quant():
            emit_maxcols(src_is_ln=True)

            def vg(v):
                v.tensor_tensor(gcolA[:, :], gcolR[:, :], gcolM[:, :], OP.mult)
                v.tensor_scalar(gcolA[:, :], gcolA[:, :], EPS_L1, None, OP.add)
                v.tensor_scalar(gcolB[:, :], gcolA[:, :], 1.0 / QB, None, OP.mult)
                v.reciprocal(gcolM[:, :], gcolA[:, :])
                v.tensor_tensor(gcolA[:, :], gcolM[:, :], gcolR[:, :], OP.mult)
                v.tensor_scalar(gcolA[:, :], gcolA[:, :], QB, None, OP.mult)
            vstep(vg)
            emit_colrow_broadcast(gcolA[:, :], gcolB[:, :], rep_a[:, :], rep_b[:, :])
            for kc in range(KC):
                v_t = vstep(lambda v, kc=kc: (
                    v.scalar_tensor_tensor(tmp1[:, :], mean_rep[:, :], -1.0,
                                           x[:, kc * T:(kc + 1) * T], OP.mult, OP.add),
                    v.tensor_tensor(tmp1[:, :], tmp1[:, :], rep_a[:, :], OP.mult),
                    v.tensor_copy(tint, tmp1[:, :]),
                    v.tensor_copy(xln[:, kc * T:(kc + 1) * T], tint)),
                    *([allt()] if kc == 0 else []))

        def emit_l1_ctx_quant():
            emit_maxcols(src_is_ln=False)

            def vg(v):
                v.tensor_scalar(gcolA[:, :], gcolM[:, :], EPS_L1, None, OP.add)
                v.tensor_scalar(gcolB[:, :], gcolA[:, :], 1.0 / QB, None, OP.mult)
                v.reciprocal(gcolM[:, :], gcolA[:, :])
                v.tensor_scalar(gcolA[:, :], gcolM[:, :], QB, None, OP.mult)
            vstep(vg)
            emit_colrow_broadcast(gcolA[:, :], gcolB[:, :], rep_a[:, :], rep_b[:, :])
            for kc in range(KC):
                s_t2 = sstep(lambda s, kc=kc: s.copy(tmp2[:, :],
                                                     ctx_all[:, kc * T:(kc + 1) * T]),
                             allv())
                vstep(lambda v, kc=kc: (
                    v.tensor_tensor(tmp1[:, :], tmp2[:, :], rep_a[:, :], OP.mult),
                    v.tensor_copy(tint, tmp1[:, :]),
                    v.tensor_copy(xln[:, kc * T:(kc + 1) * T], tint)),
                    ("s", s_t2), *([allt()] if kc == 0 else []))

        arena = sbt("arena", [128, 30848], BF16)
        A = arena.ap()
        srow = A[0:1, 29824:30848]
        ctx_all = A[:, 0:16384]
        Ksb = A[:, 16384:18432]
        Vsb = A[:, 18432:20480]
        Vtm = A[:, 20480:22528]
        Pbf = A[:, 22528:24576]
        Ptrb = A[:, 24576:26624]
        qh_bf = A[:, 26624:27648]
        kh_bf = Pbf[:, 0:T]
        sel8b = Ptrb[0:8, 0:1024]
        x2b = qh_bf
        ctmb = A[:, 29696:29824]
        wqb = [A[:, 27648:29696]]
        wkb = [Ptrb]
        wvb = [Vtm]
        ring = [A[:, i * 6144:(i + 1) * 6144] for i in range(4)]
        hb = A[:, 24576:30720]
        rwbb = A[:, 24576:25088]

        def emit_attention(layer):
            prep_out = prep0_out if layer == 0 else prep1_out
            causal = (layer == 0)

            if layer == 1:
                vstep(lambda v: (v.tensor_tensor(cosb[:, :], cosb[:, :], rep_b[:, :], OP.mult),
                                 v.tensor_tensor(sinb[:, :], sinb[:, :], rep_b[:, :], OP.mult)))

            def load_whead(e, w_idx, h, buf):
                r, half = h // 2, h % 2
                bap = buf
                e.dma_start(
                    out=bap.rearrange("p (kc o) -> p kc o", o=128),
                    in_=prep_out[r, w_idx, :, half * 128:(half + 1) * 128].rearrange(
                        "(kc p) o -> p kc o", p=128)).then_inc(s_ld, 16)

            for w_idx, bufs in ((0, wqb), (1, wkb), (2, wvb)):
                syncstep(lambda e, w_idx=w_idx, bufs=bufs: load_whead(e, w_idx, 0, bufs[0]),
                         allcc())
                B.inc("ld", 16)

            cc_kv1 = [None]
            for h in range(NH):
                if h == 8:
                    cc_kv1[0] = collective([kv_in[0:8].opt()], [kv_outA.opt()],
                                           allld(), groups=PAIRS)
                ld_w = B.val("ld")

                for which, bufs, w_idx in (("q", wqb, 0), ("k", wkb, 1), ("v", wvb, 2)):
                    def tmm(t, which=which, bufs=bufs, h=h, ld_w=ld_w):
                        wt(t, "ld", ld_w)
                        dst = pja if which != "v" else pjb
                        for kcc in range(KC):
                            for half in range(2):
                                sl = slice(half * 512, (half + 1) * 512)
                                t.matmul(dst[:, sl], bufs[0][:, kcc * 128:(kcc + 1) * 128],
                                         xln[:, kcc * T + half * 512:kcc * T + (half + 1) * 512],
                                         start=(kcc == 0), stop=(kcc == KC - 1))
                    t_mm = tstep(tmm, allv())
                    if h + 1 < NH:
                        syncstep(lambda e, w_idx=w_idx, bufs=bufs, h=h:
                                 load_whead(e, w_idx, h + 1, bufs[0]), ("t", t_mm))
                        B.inc("ld", 16)

                    if which == "v":
                        if layer == 0:
                            d_s = sstep(lambda s: s.copy(kh_bf, pjb[:, :]),
                                        ("t", t_mm), allld())
                            dsem = ("s", d_s)
                        else:
                            d_v = vstep(lambda v: v.tensor_tensor(kh_bf, pjb[:, :],
                                                                  rep_b[:, :], OP.mult),
                                        ("t", t_mm), allld())
                            dsem = ("v", d_v)
                        syncstep(lambda e, h=h: e.dma_start(out=kv_in[h, 1, :, :],
                                                            in_=kh_bf
                                                            ).then_inc(s_ld, 16), dsem)
                        B.inc("ld", 16)
                    else:
                        s_e = sstep(lambda s: s.copy(kh_bf, pja[:, :]), ("t", t_mm))
                        t_j = tstep(lambda t: [t.matmul(pjb[:, sl], jrot[:, :], kh_bf[0:128, sl],
                                                        start=True, stop=True)
                                               for sl in (slice(0, 512), slice(512, 1024))],
                                    ("s", s_e))

                        def vrope(v, which=which, h=h):
                            v.tensor_tensor(tmp1[:, :], pja[:, :], cosb[:, :], OP.mult)
                            v.tensor_tensor(tmp2[:, :], pjb[:, :], sinb[:, :], OP.mult)
                            dst = ctx_all[:, h * T:(h + 1) * T] if which == "q" else kh_bf
                            v.tensor_tensor(dst, tmp1[:, :], tmp2[:, :], OP.add)
                        v_r = vstep(vrope, ("t", t_j))
                        if which == "k":
                            syncstep(lambda e, h=h: e.dma_start(out=kv_in[h, 0, :, :],
                                                                in_=kh_bf
                                                                ).then_inc(s_ld, 16),
                                     ("v", v_r))
                            B.inc("ld", 16)

            B.mark()
            cc_kv2 = collective([kv_in[8:16].opt()], [kv_outB.opt()],
                                allld(), groups=PAIRS)
            B.mark()

            for h in range(NH):
                cc_kv = cc_kv1[0] if h < 8 else cc_kv2
                def ldkv(e, h=h, cc_kv=cc_kv):
                    wt(e, "cc", cc_kv)
                    for rq in range(2):
                        for kv, dstt in ((0, Ksb), (1, Vsb)):
                            src = (kv_outA if h < 8 else kv_outB)[rq, h % 8, kv, :, :]
                            dstap = dstt.rearrange("p (u two n) -> p u two n",
                                                        two=2, n=128)[:, :, rq, :]
                            e.dma_start(out=dstap, in_=src).then_inc(s_ld, 16)
                prev_t = B.val("t")
                prev_s = B.val("s")

                def ldkv_g(e, ldkv=ldkv, prev_t=prev_t, prev_s=prev_s):
                    wt(e, "t", prev_t)
                    wt(e, "s", prev_s)
                    ldkv(e)
                B.emit("sync", ldkv_g)
                B.inc("ld", 16 * 4)
                ld_kv = B.val("ld")

                t_v = tstep(lambda t: [t.transpose(trp[:, u * 128:(u + 1) * 128],
                                                   Vsb[:, u * 128:(u + 1) * 128],
                                                   ident[:, :]) for u in range(NKT)],
                            ("ld", ld_kv), alls())
                s_vtm = vstep(lambda v: v.tensor_copy(Vtm[:, :], trp[:, 0:2048]), ("t", t_v))

                s_q = sstep(lambda s, h=h: s.activation(qh_bf[:, :],
                                                        ctx_all[:, h * T:(h + 1) * T],
                                                        AF.Square), allv())
                s_k = vstep(lambda v: (v.tensor_tensor(Pbf[:, 0:T], Ksb[:, 0:T],
                                                       Ksb[:, 0:T], OP.mult),
                                       v.tensor_tensor(Pbf[:, T:SEQ], Ksb[:, T:SEQ],
                                                       Ksb[:, T:SEQ], OP.mult)),
                            ("ld", ld_kv))

                def tqn(t, s_q=s_q, s_k=s_k):
                    wt(t, "s", s_q)
                    wt(t, "v", s_k)
                    t.matmul(smp[0:1, 0:512], ones_bf[:, 0:1], qh_bf[:, 0:512],
                             start=True, stop=True)
                    t.matmul(spx[0:1, 0:512], ones_bf[:, 0:1], qh_bf[:, 512:1024],
                             start=True, stop=True)
                    for half in range(2):
                        t.matmul(pja[:, half * 512:(half + 1) * 512], ones_bf[:, :],
                                 Pbf[:, half * 512:(half + 1) * 512], start=True, stop=True)
                        t.matmul(pjb[:, half * 512:(half + 1) * 512], ones_bf[:, :],
                                 Pbf[:, T + half * 512:T + (half + 1) * 512],
                                 start=True, stop=True)
                t_n = tstep(tqn)

                def vkn(v, t_n=t_n):
                    v.reduce_max(k2col[:, :], pja[:, :], AX.X)
                    v.reduce_max(qncol[:, 0:1], pjb[:, :], AX.X)
                    v.tensor_tensor(k2col[:, :], k2col[:, :], qncol[:, 0:1], OP.max)
                    v.tensor_copy(srow[0:1, 0:512], smp[0:1, 0:512])
                    v.tensor_copy(srow[0:1, 512:1024], spx[0:1, 0:512])
                v_n = vstep(vkn, ("t", t_n))

                def tqc(t):
                    for jj in range(NQ):
                        t.matmul(smp[:, 256 + jj:257 + jj],
                                 srow[0:1, jj * 128:(jj + 1) * 128],
                                 ones_bf[0:1, 0:1], start=True, stop=True)
                t_c = tstep(tqc, ("v", v_n))
                v_c2 = vstep(lambda v: v.tensor_copy(qncol[:, :], smp[:, 256:256 + NQ]),
                             ("t", t_c))
                v_b = vstep(lambda v: (
                    v.tensor_tensor(bcol[:, :], qncol[:, :],
                                    k2col[:, :].broadcast_to([128, NQ]), OP.add),
                    v.tensor_scalar(bcol[:, :], bcol[:, :], -1.0 / 256.0, None, OP.mult)),
                    ("v", v_c2))

                exp_hist = {}
                pv_hist = []
                for j in range(NQ):
                    w = (2 * j + 2) if causal else NKT
                    cw = w * 128
                    nslice = (cw + T - 1) // T
                    for sl_i in range(nslice):
                        c0 = sl_i * T
                        c1 = min(cw, (sl_i + 1) * T)
                        Sp = pja if sl_i == 0 else pjb

                        def tsc(t, h=h, j=j, c0=c0, c1=c1, Sp=Sp):
                            nn = c1 - c0
                            for off in range(0, nn, 512):
                                t.matmul(Sp[:, off:off + min(512, nn - off)],
                                         ctx_all[:, h * T + j * 128:h * T + (j + 1) * 128],
                                         Ksb[:, c0 + off:c0 + off + min(512, nn - off)],
                                         start=True, stop=True)
                        prevdep = exp_hist.get((j - 1, sl_i), 0)
                        t_sc = tstep(tsc, ("v", v_b), ("s", prevdep))

                        if causal and sl_i == nslice - 1:
                            mlo = cw - 256 - c0
                            v_m = vstep(lambda v, Sp=Sp, mlo=mlo: (
                                v.tensor_tensor(Sp[:, mlo:mlo + 128], Sp[:, mlo:mlo + 128],
                                                mA[:, :], OP.add),
                                v.tensor_tensor(Sp[:, mlo + 128:mlo + 256],
                                                Sp[:, mlo + 128:mlo + 256], mB[:, :], OP.add)),
                                ("t", t_sc))
                            dep = ("v", v_m)
                        else:
                            dep = ("t", t_sc)

                        def sexp(s, h=h, j=j, sl_i=sl_i, c0=c0, c1=c1, Sp=Sp):
                            s.activation(Pbf[:, c0:c1], Sp[:, 0:c1 - c0], AF.Exp,
                                         bias=bcol[:, j:j + 1], scale=ISQD,
                                         accum_out=(lcol[:, j:j + 1] if sl_i == 0
                                                    else rlcol[:, j:j + 1]))
                        s_e = sstep(sexp, dep)
                        exp_hist[(j, sl_i)] = s_e
                    if nslice == 2:
                        v_l = vstep(lambda v, j=j: v.tensor_tensor(
                            lcol[:, j:j + 1], lcol[:, j:j + 1], rlcol[:, j:j + 1], OP.add),
                            ("s", B.val("s")))
                    v_rl = vstep(lambda v, j=j: v.reciprocal(rlcol[:, j:j + 1],
                                                             lcol[:, j:j + 1]),
                                 ("s", B.val("s")), ("v", B.val("v")))

                    t_tr = tstep(lambda t, j=j, w=w: [t.transpose(
                        trp[:, u * 128:(u + 1) * 128], Pbf[:, u * 128:(u + 1) * 128],
                        ident[:, :]) for u in range(w)], ("s", B.val("s")), allv())
                    s_ptr = vstep(lambda v, w=w: v.tensor_copy(Ptrb[:, 0:w * 128],
                                                        trp[:, 0:w * 128]), ("t", t_tr))

                    def tpv(t, j=j, w=w):
                        o = (j % 4) * 128
                        for u in range(w):
                            t.matmul(smp[:, o:o + 128], Ptrb[:, u * 128:(u + 1) * 128],
                                     Vtm[:, u * 128:(u + 1) * 128],
                                     start=(u == 0), stop=(u == w - 1))
                    t_pv = tstep(tpv, ("v", s_ptr))
                    pv_hist.append((t_pv, v_rl))
                    if j % 4 == 3:
                        j0 = j - 3
                        def vcn4(v, j0=j0):
                            for jj in range(4):
                                v.tensor_tensor(
                                    qh_bf[0:128, jj * 128:(jj + 1) * 128],
                                    smp[:, jj * 128:(jj + 1) * 128],
                                    rlcol[:, j0 + jj:j0 + jj + 1].broadcast_to([128, 128]),
                                    OP.mult)
                        s_cn = vstep(vcn4, ("t", pv_hist[-1][0]))
                        t_ct = tstep(lambda t: [t.transpose(
                            trp[:, 1536 + jj * 128:1536 + (jj + 1) * 128],
                            qh_bf[0:128, jj * 128:(jj + 1) * 128], ident[:, :])
                            for jj in range(4)], ("v", s_cn))
                        s_cw = vstep(lambda v, h=h, j0=j0: v.tensor_copy(
                            ctx_all[:, h * T + j0 * 128:h * T + (j0 + 4) * 128],
                            trp[:, 1536:2048]), ("t", t_ct))

            B.mark()
            if layer == 1:
                emit_l1_ctx_quant()
                osrc = xln
            else:
                osrc = ctx_all

            for oc in range(KC):
                wob = wqb[0] if oc % 2 == 0 else wkb[0]
                syncstep(lambda e, oc=oc, wob=wob: load_whead(e, 3, oc, wob),
                         allt())
                B.inc("ld", 16)
                ld_o = B.val("ld")

                def tom(t, oc=oc, ld_o=ld_o, osrc=osrc, wob=wob):
                    wt(t, "ld", ld_o)
                    dst = pja if (oc % 2 == 0) else pjb
                    for kcc in range(KC):
                        for half in range(2):
                            sl = slice(half * 512, (half + 1) * 512)
                            t.matmul(dst[:, sl], wob[:, kcc * 128:(kcc + 1) * 128],
                                     osrc[:, kcc * T + half * 512:kcc * T + (half + 1) * 512],
                                     start=(kcc == 0), stop=(kcc == KC - 1))
                t_o = tstep(tom, allv(), alls())

                if layer == 0:
                    vstep(lambda v, oc=oc: v.scalar_tensor_tensor(
                        x[:, oc * T:(oc + 1) * T],
                        (pja if oc % 2 == 0 else pjb)[:, :], 1.0,
                        x[:, oc * T:(oc + 1) * T], OP.mult, OP.add), ("t", t_o))
                else:
                    vstep(lambda v, oc=oc: (
                        v.tensor_tensor(tmp1[:, :],
                                        (pja if oc % 2 == 0 else pjb)[:, :],
                                        rep_b[:, :], OP.mult),
                        v.tensor_tensor(x[:, oc * T:(oc + 1) * T],
                                        x[:, oc * T:(oc + 1) * T], tmp1[:, :], OP.add)),
                        ("t", t_o))

        def emit_l0_ffn():
            l0ATb = Vtm
            l0Bb = Pbf
            gw_b = Ptrb
            gdma(lambda g: g.dma_start(out=l0ATb.rearrange("p (kc o) -> p kc o", o=128),
                                       in_=l0f_AT.rearrange("(kc p) o -> p kc o", p=128)),
                 allt(), alls())
            gdma(lambda g: g.dma_start(out=l0Bb[:, 0:D], in_=l0f_B))
            gdma(lambda g: g.dma_start(
                out=gw_b[:, 0:KC * 8].rearrange("p (kc o) -> p kc o", o=8),
                in_=l0f_gwT.rearrange("(kc p) o -> p kc o", p=128)))
            gdma(lambda g: g.dma_start(out=biasrow[:, :], in_=l0f_b_h))
            gdma(lambda g: g.dma_start(out=gbrow[:, :], in_=l0f_gb_h))
            gd_w = B.val("gd")

            def tgl(t, gd_w=gd_w):
                wt(t, "gd", gd_w)
                for kcc in range(KC):
                    for half, bank in ((0, smp), (1, spx)):
                        bankap = bank[0:8, 0:512]
                        t.matmul(bankap, gw_b[:, kcc * 8:(kcc + 1) * 8],
                                 xln[:, kcc * T + half * 512:kcc * T + (half + 1) * 512],
                                 start=(kcc == 0), stop=False)
                for half, bank in ((0, smp), (1, spx)):
                    t.matmul(bank[0:8, 0:512], gbrow[0:1, 0:8],
                             onesrow[0:1, 0:512], start=False, stop=True)
            t_gl = tstep(tgl, allv())
            v_gl = vstep(lambda v: (v.tensor_copy(tmp3[0:8, 0:512], smp[0:8, 0:512]),
                                    v.tensor_copy(tmp3[0:8, 512:1024], spx[0:8, 0:512])),
                         ("t", t_gl))

            for blk in range(NQ):
                t_tr = tstep(lambda t, blk=blk: t.transpose(
                    smp[:, 0:128], tmp3[0:8, blk * 128:(blk + 1) * 128] if False else
                    tmp3[0:128, blk * 128:(blk + 1) * 128], identf[:, :]),
                    ("v", B.val("v")))

                def vtop(v, blk=blk):
                    v.tensor_copy(gtok[:, 0:8], smp[:, 0:8])
                    v.reduce_max(gm1[:, :], gtok[:, 0:8], AX.X)
                    v.tensor_tensor(gtk2[:, 0:8], gtok[:, 0:8],
                                    gm1[:, :].broadcast_to([128, 8]), OP.is_equal)
                    v.scalar_tensor_tensor(gtk3[:, 0:8], gtk2[:, 0:8], -1e30,
                                           gtok[:, 0:8], OP.mult, OP.add)
                    v.reduce_max(gm2[:, :], gtk3[:, 0:8], AX.X)
                v_t1 = vstep(vtop, ("t", t_tr))
                v_t2 = vstep(lambda v: v.tensor_scalar(gm1[:, :], gm1[:, :], -1.0,
                                                       None, OP.mult), ("v", v_t1))
                s_ex2 = sstep(lambda s: s.activation(gtk2[:, 0:8], gtok[:, 0:8], AF.Exp,
                                                     bias=gm1[:, :], scale=1.0),
                              ("v", v_t2))

                def vw(v, blk=blk):
                    v.tensor_tensor(gtk3[:, 0:8], gtok[:, 0:8],
                                    gm2[:, :].broadcast_to([128, 8]), OP.is_ge)
                    v.tensor_tensor(gtk2[:, 0:8], gtk2[:, 0:8], gtk3[:, 0:8], OP.mult)
                    v.reduce_sum(gden[:, :], gtk2[:, 0:8], AX.X)
                    v.reciprocal(gden[:, :], gden[:, :])
                    v.tensor_tensor(gtok[:, 0:8], gtk2[:, 0:8], gden[:, :].broadcast_to([128, 8]), OP.mult)
                v_t3 = vstep(vw, ("s", s_ex2))
                t_tb = tstep(lambda t: t.transpose(smp[:, 128:256], gtok[:, 0:128],
                                                   identf[:, :]), ("v", v_t3))
                v_tb = vstep(lambda v, blk=blk: v.tensor_copy(
                    wselb[0:8, blk * 128:(blk + 1) * 128], smp[0:8, 128:256]), ("t", t_tb))

            t_w16 = tstep(lambda t: [t.matmul((pja if False else smp)[:, 0:512] if False
                                              else pja[:, half * 512:(half + 1) * 512],
                                              e16[:, :], wselb[0:8, half * 512:(half + 1) * 512],
                                              start=True, stop=True) for half in range(2)],
                          ("v", B.val("v")))
            v_w16 = vstep(lambda v: v.tensor_copy(wrepf[:, :], pja[:, :]), ("t", t_w16))

            def txa(t):
                for kcc in range(KC):
                    for half in range(2):
                        sl = slice(half * 512, (half + 1) * 512)
                        t.matmul(pjb[:, sl], l0ATb[:, kcc * 128:(kcc + 1) * 128],
                                 xln[:, kcc * T + half * 512:kcc * T + (half + 1) * 512],
                                 start=(kcc == 0), stop=(kcc == KC - 1))
            t_xa = tstep(txa, ("v", v_w16))
            v_xa = vstep(lambda v: (v.tensor_tensor(tmp1[:, :], pjb[:, :], wrepf[:, :],
                                                    OP.mult),
                                    v.tensor_copy(qh_bf[:, :], tmp1[:, :])), ("t", t_xa))

            wfb = [Ksb, Vsb]
            for oc in range(KC):
                gdma(lambda g, oc=oc: g.dma_start(
                    out=wfb[oc % 2][:, 0:D].rearrange("p (kc o) -> p kc o", o=128),
                    in_=l0fWT[:, oc * 128:(oc + 1) * 128].rearrange("(kc p) o -> p kc o", p=128)
                ), allt())
                gd_t = B.val("gd")

                def tb(t, oc=oc, gd_t=gd_t):
                    wt(t, "gd", gd_t)
                    dst = pja if oc % 2 == 0 else pjb
                    for kcc in range(KC):
                        for half in range(2):
                            sl = slice(half * 512, (half + 1) * 512)
                            t.matmul(dst[:, sl], wfb[oc % 2][:, kcc * 128:(kcc + 1) * 128],
                                     xln[:, kcc * T + half * 512:kcc * T + (half + 1) * 512],
                                     start=(kcc == 0), stop=False)
                    for half in range(2):
                        sl = slice(half * 512, (half + 1) * 512)
                        t.matmul(dst[:, sl], biasrow[0:1, oc * 128:(oc + 1) * 128],
                                 onesrow[0:1, 0:512], start=False, stop=False)
                        t.matmul(dst[:, sl], l0Bb[:, oc * 128:(oc + 1) * 128],
                                 qh_bf[:, sl], start=False, stop=True)
                t_b = tstep(tb, ("v", v_xa) if oc == 0 else allv())
                vstep(lambda v, oc=oc: v.scalar_tensor_tensor(
                    x[:, oc * T:(oc + 1) * T], (pja if oc % 2 == 0 else pjb)[:, :],
                    1.0, x[:, oc * T:(oc + 1) * T], OP.mult, OP.add), ("t", t_b))

        def emit_l1_ffn():
            gdma(lambda g: g.dma_start(
                out=rwbb.rearrange("p (kc o) -> p kc o", o=32),
                in_=l1f_rwT.rearrange("(kc p) o -> p kc o", p=128)),
                allt(), alls())
            gd_t = B.val("gd")

            def trt(t, gd_t=gd_t):
                wt(t, "gd", gd_t)
                for kcc in range(KC):
                    for half, bank in ((0, smp), (1, spx)):
                        t.matmul(bank[0:32, 0:512], rwbb[:, kcc * 32:(kcc + 1) * 32],
                                 xln[:, kcc * T + half * 512:kcc * T + (half + 1) * 512],
                                 start=(kcc == 0), stop=(kcc == KC - 1))
            t_r = tstep(trt, allv())
            v_r = vstep(lambda v: (v.tensor_copy(tmp3[0:32, 0:512], smp[0:32, 0:512]),
                                   v.tensor_copy(tmp3[0:32, 512:1024], spx[0:32, 0:512])),
                        ("t", t_r))

            for blk in range(NQ):
                t_tr = tstep(lambda t, blk=blk: t.transpose(
                    smp[:, 0:128], tmp3[0:128, blk * 128:(blk + 1) * 128], identf[:, :]),
                    ("v", B.val("v")))

                def vtop(v, blk=blk):
                    v.tensor_copy(gtok[:, 0:32], smp[:, 0:32])
                    v.reduce_max(gm1[:, :], gtok[:, 0:32], AX.X)
                    v.tensor_copy(gtk3[:, 0:32], gtok[:, 0:32])
                v_t1 = vstep(vtop, ("t", t_tr))
                for it in range(7):
                    def vit(v, it=it):
                        v.reduce_max(gm2[:, :], gtk3[:, 0:32], AX.X)
                        v.tensor_tensor(gtk2[:, 0:32], gtk3[:, 0:32],
                                        gm2[:, :].broadcast_to([128, 32]), OP.is_equal)
                        v.scalar_tensor_tensor(gtk3[:, 0:32], gtk2[:, 0:32], -1e30,
                                               gtk3[:, 0:32], OP.mult, OP.add)
                    vstep(vit)
                v_m8 = vstep(lambda v: v.reduce_max(gm2[:, :], gtk3[:, 0:32], AX.X))
                v_neg = vstep(lambda v: v.tensor_scalar(gm1[:, :], gm1[:, :], -1.0,
                                                        None, OP.mult))
                s_ex = sstep(lambda s: s.activation(gtk2[:, 0:32], gtok[:, 0:32], AF.Exp,
                                                    bias=gm1[:, :], scale=1.0),
                             ("v", B.val("v")))

                def vw(v, blk=blk):
                    v.tensor_tensor(gtk3[:, 0:32], gtok[:, 0:32],
                                    gm2[:, :].broadcast_to([128, 32]), OP.is_ge)
                    v.tensor_tensor(gtk2[:, 0:32], gtk2[:, 0:32], gtk3[:, 0:32], OP.mult)
                    v.reduce_sum(gden[:, :], gtk2[:, 0:32], AX.X)
                    v.reciprocal(gden[:, :], gden[:, :])
                    v.tensor_tensor(gtok[:, 0:32], gtk2[:, 0:32], gden[:, :].broadcast_to([128, 32]), OP.mult)
                v_t3 = vstep(vw, ("s", s_ex))
                t_tb = tstep(lambda t: t.transpose(smp[:, 128:256], gtok[:, 0:128],
                                                   identf[:, :]), ("v", v_t3))
                vstep(lambda v, blk=blk: v.tensor_copy(
                    wselb[0:32, blk * 128:(blk + 1) * 128], smp[0:32, 128:256]),
                    ("t", t_tb))

            piece_idx = [0]

            def load_piece(src_ap, dst, *waits):
                return gdma(lambda g, src_ap=src_ap, dst=dst:
                            g.dma_start(out=dst, in_=src_ap), *waits)

            def piece_srcs(e):
                return [
                    (l1f_wg[e, 0:1024, :].rearrange("(kc p) o -> p kc o", p=128),
                     ring, "wg0"),
                    (l1f_wg[e, 1024:2048, :].rearrange("(kc p) o -> p kc o", p=128),
                     ring, "wg1"),
                    (l1f_wu[e, 0:1024, :].rearrange("(kc p) o -> p kc o", p=128),
                     ring, "wu0"),
                    (l1f_wu[e, 1024:2048, :].rearrange("(kc p) o -> p kc o", p=128),
                     ring, "wu1"),
                    (l1f_wd[e, :, 0:1024].rearrange("(k6 p) o -> p k6 o", p=128),
                     ring, "wdA"),
                    (l1f_wd[e, :, 1024:2048].rearrange("(k6 p) o -> p k6 o", p=128),
                     ring, "wdB"),
                ]

            def slot(e, i):
                return (e * 6 + i) % 4

            gd_piece = {}
            for i in range(4):
                src, _, _ = piece_srcs(0)[i]
                dstap = ring[slot(0, i)].rearrange(
                    "p (a b) -> p a b", a=8 if i < 4 else 6)
                gd_piece[(0, i)] = load_piece(src, dstap, allt(), allv())

            for e in range(32):
                syncstep(lambda ee, e=e: ee.dma_start(
                    out=selbuf[:, :], in_=sel32_h[e, :].rearrange("(r n) -> r n", n=128)
                ).then_inc(s_ld, 16), allt())
                B.inc("ld", 16)
                ld_sel = B.val("ld")
                t_wr = tstep(lambda t, e=e, ld_sel=ld_sel: (
                    wt(t, "ld", ld_sel),
                    [t.matmul(pjb[:, half * 512:(half + 1) * 512], selbuf[:, :],
                              wselb[0:32, half * 512:(half + 1) * 512],
                              start=True, stop=True) for half in range(2)]), allv())
                v_wr = vstep(lambda v: v.tensor_copy(wrepf[:, :], pjb[:, :]), ("t", t_wr))

                last_si = 0
                last_h = v_wr
                for oc6 in range(6):
                    def tg(t, e=e, oc6=oc6):
                        wt(t, "gd", max(gd_piece[(e, 0)], gd_piece[(e, 1)]))
                        for kcc in range(KC):
                            piece = 0 if kcc < 8 else 1
                            rb = ring[slot(e, piece)]
                            kloc = kcc % 8
                            for half in range(2):
                                sl = slice(half * 512, (half + 1) * 512)
                                t.matmul(pja[:, sl],
                                         rb[:, kloc * 768 + oc6 * 128:kloc * 768 + (oc6 + 1) * 128],
                                         xln[:, kcc * T + half * 512:kcc * T + (half + 1) * 512],
                                         start=(kcc == 0), stop=(kcc == KC - 1))
                    t_g = tstep(tg, allv() if oc6 == 0 else ("s", last_si))
                    s_si = sstep(lambda s, oc6=oc6: s.activation(tmp3[:, :], pja[:, :],
                                                                 AF.Silu), ("t", t_g))
                    last_si = s_si
                    def tu(t, e=e, oc6=oc6):
                        wt(t, "gd", max(gd_piece[(e, 2)], gd_piece[(e, 3)]))
                        for kcc in range(KC):
                            piece = 2 if kcc < 8 else 3
                            rb = ring[slot(e, piece)]
                            kloc = kcc % 8
                            for half in range(2):
                                sl = slice(half * 512, (half + 1) * 512)
                                t.matmul(pjb[:, sl],
                                         rb[:, kloc * 768 + oc6 * 128:kloc * 768 + (oc6 + 1) * 128],
                                         xln[:, kcc * T + half * 512:kcc * T + (half + 1) * 512],
                                         start=(kcc == 0), stop=(kcc == KC - 1))
                    t_u = tstep(tu, ("v", last_h))
                    last_h = vstep(lambda v, oc6=oc6: (
                        v.tensor_tensor(tmp1[:, :], tmp3[:, :], pjb[:, :], OP.mult),
                        v.tensor_tensor(tmp2[:, :], tmp1[:, :], wrepf[:, :], OP.mult),
                        v.tensor_copy(hb[:, oc6 * T:(oc6 + 1) * T], tmp2[:, :])),
                        ("t", t_u), ("s", s_si))

                t_after_g = B.val("t")
                for i in (4, 5):
                    src, _, _ = piece_srcs(e)[i]
                    dstap = ring[slot(e, i)].rearrange("p (a b) -> p a b", a=6)
                    gd_piece[(e, i)] = load_piece(src, dstap, ("t", t_after_g))

                t_after_u = B.val("t")
                if e + 1 < 32:
                    for i in (0, 1):
                        src, _, _ = piece_srcs(e + 1)[i]
                        dstap = ring[slot(e + 1, i)].rearrange(
                            "p (a b) -> p a b", a=8)
                        gd_piece[(e + 1, i)] = load_piece(src, dstap, ("t", t_after_u))

                ev_hist = [last_h, last_h]
                for dm in range(KC):
                    def td(t, e=e, dm=dm):
                        wt(t, "gd", gd_piece[(e, 4 if dm < 8 else 5)])
                        for k6 in range(6):
                            piece = 4 if dm < 8 else 5
                            rb = ring[slot(e, piece)]
                            dloc = dm % 8
                            for half in range(2):
                                sl = slice(half * 512, (half + 1) * 512)
                                t.matmul((pja if dm % 2 == 0 else pjb)[:, sl],
                                         rb[:, k6 * 1024 + dloc * 128:k6 * 1024 + (dloc + 1) * 128],
                                         hb[:, k6 * T + half * 512:k6 * T + (half + 1) * 512],
                                         start=(k6 == 0), stop=(k6 == 5))
                    wdep = ("v", ev_hist[-2]) if dm >= 2 else ("v", last_h)
                    t_d = tstep(td, wdep)
                    ev = vstep(lambda v, dm=dm: v.scalar_tensor_tensor(
                        x[:, dm * T:(dm + 1) * T],
                        (pja if dm % 2 == 0 else pjb)[:, :], 1.0,
                        x[:, dm * T:(dm + 1) * T], OP.mult, OP.add), ("t", t_d))
                    ev_hist.append(ev)
                    if e + 1 < 32:
                        if dm == 7:
                            src, _, _ = piece_srcs(e + 1)[2]
                            dstap = ring[slot(e + 1, 2)].rearrange(
                                "p (a b) -> p a b", a=8)
                            gd_piece[(e + 1, 2)] = load_piece(src, dstap,
                                                             ("t", B.val("t")))
                        if dm == KC - 1:
                            src, _, _ = piece_srcs(e + 1)[3]
                            dstap = ring[slot(e + 1, 3)].rearrange(
                                "p (a b) -> p a b", a=8)
                            gd_piece[(e + 1, 3)] = load_piece(src, dstap,
                                                             ("t", B.val("t")))

        try:
            B.mark()
            emit_ln_stats()
            B.mark()
            emit_ln_apply()
            B.mark()
            if NBLOCK >= 1:
                emit_attention(0)
            if NBLOCK >= 2:
                emit_ln_stats()
                emit_ln_apply()
                emit_l0_ffn()
            if NBLOCK >= 3:
                emit_ln_stats()
                emit_l1_input_quant()
                emit_attention(1)
            if NBLOCK >= 4:
                emit_ln_stats()
                emit_ln_apply()
                emit_l1_ffn()
        except StopEmission:
            pass

        if DEVDUMP:
            def dumps(e):
                import contextlib
                ctxmgr = nc.allow_non_contiguous_dma(reason="debug dumps")
                ctxmgr.__enter__()
                e.dma_start(out=dbg_ctx, in_=ctx_all).then_inc(s_st, 16)
                e.dma_start(out=dbg_k, in_=Ksb).then_inc(s_st, 16)
                e.dma_start(out=dbg_v, in_=Vsb).then_inc(s_st, 16)
                e.dma_start(out=dbg_xln, in_=xln[:, :]).then_inc(s_st, 16)
                e.dma_start(out=dbg_cols[:, 0:8], in_=lcol[:, :]).then_inc(s_st, 16)
                e.dma_start(out=dbg_cols[:, 8:16], in_=bcol[:, :]).then_inc(s_st, 16)
                e.dma_start(out=dbg_cols[:, 16:24], in_=qncol[:, :]).then_inc(s_st, 16)
                e.dma_start(out=dbg_cols[:, 24:25], in_=k2col[:, :]).then_inc(s_st, 16)
                e.dma_start(out=dbg_cols[:, 25:33], in_=rlcol[:, :]).then_inc(s_st, 16)
                e.dma_start(out=dbg_cols[:, 33:34], in_=mean_rep[:, 0:1]).then_inc(s_st, 16)
                e.dma_start(out=dbg_cols[:, 34:35], in_=rstd_rep[:, 0:1]).then_inc(s_st, 16)
                e.dma_start(out=dbg_kvin, in_=kv_in).then_inc(s_st, 16)
                e.dma_start(out=dbg_p0in, in_=prep0_in).then_inc(s_st, 16)
                e.dma_start(out=dbg_p0out, in_=prep0_out[0:2]).then_inc(s_st, 16)
            B.inc("st", 16 * 14)
            B.emit("sync", W(dumps := dumps, ) if False else
                   (lambda e, f=dumps: ([wt(e, sn, vv) for sn, vv in
                                         (allv(), allt(), alls(), allgd())], f(e))))
        syncstep(lambda e: e.dma_start(
            out=out_fm.rearrange("(kc p) t -> p kc t", p=128),
            in_=x[:, :].rearrange("p (kc t) -> p kc t", t=T)).then_inc(s_st, 16),
            allv(), allt(), alls(), allgd())
        B.inc("st", 16)
        syncstep(lambda e: None, ("st", B.val("st")))

        block = ctx.enter_context(nc.Block())

        @block.sync
        def _(sync):
            for fn in B.q["sync"]:
                fn(sync)

        @block.gpsimd
        def _(g):
            for fn in B.q["gpsimd"]:
                fn(g)

        @block.vector
        def _(v):
            for fn in B.q["vector"]:
                fn(v)

        @block.scalar
        def _(s):
            for fn in B.q["scalar"]:
                fn(s)

        @block.tensor
        def _(t):
            for fn in B.q["tensor"]:
                fn(t)

    return nc



def _host_constants():
    ones = np.ones((128, 128), dtype=ml_dtypes.bfloat16)
    ident = np.eye(128, dtype=ml_dtypes.bfloat16)
    identf = np.eye(128, dtype=np.float32)
    J = np.zeros((128, 128), dtype=np.float32)
    J[np.arange(64), np.arange(64) + 64] = -1.0
    J[np.arange(64) + 64, np.arange(64)] = 1.0
    jrotT = J.T.astype(ml_dtypes.bfloat16)
    e16 = np.zeros((8, 128), dtype=ml_dtypes.bfloat16)
    for e in range(8):
        e16[e, e * 16:(e + 1) * 16] = 1.0
    sel8 = np.zeros((8, 8 * 128), dtype=ml_dtypes.bfloat16)
    for j in range(8):
        sel8[j, j * 128:(j + 1) * 128] = 1.0
    sel32 = np.zeros((32, 32 * 128), dtype=ml_dtypes.bfloat16)
    for e in range(32):
        sel32[e, e * 128:(e + 1) * 128] = 1.0
    tri = np.where(np.arange(128)[None, :] <= np.arange(128)[:, None], 0.0, -1e30
                   ).astype(np.float32)
    full = np.full((128, 128), -1e30, dtype=np.float32)
    zero = np.zeros((128, 128), dtype=np.float32)
    return ones, ident, identf, jrotT, e16, sel8, sel32, tri, full, zero


def _rope_tables_stacked():
    half = 64
    inv = (1.0 / 10000.0 ** (np.arange(half, dtype=np.float64) / half))
    tpos = np.arange(SEQ, dtype=np.float64)
    fr = np.outer(tpos, inv)
    cos = np.cos(fr).astype(np.float32)
    sin = np.sin(fr).astype(np.float32)
    c2 = np.concatenate([cos, cos], axis=1).T.copy()
    s2 = np.concatenate([sin, sin], axis=1).T.copy()
    return c2, s2


_NC_CACHE = {}


def _get_nc():
    key = NBLOCK
    if key not in _NC_CACHE:
        _NC_CACHE[key] = build_program()
    return _NC_CACHE[key]


def kernel(**inputs):
    f32 = np.float32
    xfull = np.ascontiguousarray(inputs["L0_attn_x"], dtype=f32)
    ones, ident, identf, jrotT, e16, sel8, sel32, tri, full, zero = _host_constants()
    c2, s2 = _rope_tables_stacked()

    wmap = {
        "l0q": inputs["L0_attn_w_q"], "l0k": inputs["L0_attn_w_k"],
        "l0v": inputs["L0_attn_w_v"], "l0o": inputs["L0_attn_w_o"],
        "l1q": inputs["L1_attn_w_q"], "l1k": inputs["L1_attn_w_k"],
        "l1v": inputs["L1_attn_w_v"], "l1o": inputs["L1_attn_w_o"],
    }
    l0fWT = np.ascontiguousarray(inputs["L0_ffn_W"].T, dtype=f32)
    l0f_b = np.ascontiguousarray(inputs["L0_ffn_b"], dtype=f32).reshape(1, D)
    l0f_AT = np.ascontiguousarray(
        inputs["L0_ffn_lora_A"].reshape(128, D).T, dtype=f32)
    l0f_B = np.ascontiguousarray(
        np.swapaxes(inputs["L0_ffn_lora_B"], 1, 2).reshape(128, D), dtype=f32)
    l0f_gwT = np.ascontiguousarray(inputs["L0_ffn_gate_w"].T, dtype=f32)
    l0f_gb = np.ascontiguousarray(inputs["L0_ffn_gate_b"], dtype=f32).reshape(1, 8)
    l1f_rwT = np.ascontiguousarray(inputs["L1_ffn_router_w"].T, dtype=f32)
    l1f_wg = np.ascontiguousarray(inputs["L1_ffn_w_gate"], dtype=f32)
    l1f_wu = np.ascontiguousarray(inputs["L1_ffn_w_up"], dtype=f32)
    l1f_wd = np.ascontiguousarray(inputs["L1_ffn_w_down"], dtype=f32)

    in_maps = []
    for c in range(NCORE):
        b, p = c // 2, c % 2
        gtiles = [2 * j + p for j in range(NQ)]
        cols = np.concatenate([np.arange(g * 128, (g + 1) * 128) for g in gtiles])
        xs = np.ascontiguousarray(xfull[b].T[:, cols])
        im = {
            "x_fm": xs,
            "cos_own": np.ascontiguousarray(c2[:, cols]),
            "sin_own": np.ascontiguousarray(s2[:, cols]),
            "maskA": tri if p == 0 else zero,
            "maskB": full if p == 0 else tri,
            "ones128": ones, "ident": ident, "identf": identf,
            "jrot": jrotT, "e16": e16, "sel8": sel8, "sel32": sel32,
            "l0fWT": l0fWT, "l0f_b": l0f_b, "l0f_AT": l0f_AT, "l0f_B": l0f_B,
            "l0f_gwT": l0f_gwT, "l0f_gb": l0f_gb, "l1f_rwT": l1f_rwT,
            "l1f_wg": l1f_wg, "l1f_wu": l1f_wu, "l1f_wd": l1f_wd,
        }
        for nm, w in wmap.items():
            im[nm + "_s"] = np.ascontiguousarray(
                np.asarray(w, dtype=f32)[c * 256:(c + 1) * 256, :])
        in_maps.append(im)

    nc = _get_nc()
    res = run_bass_kernel_spmd(nc, in_maps, core_ids=list(range(NCORE)))

    out = np.empty((4, SEQ, D), dtype=f32)
    for c in range(NCORE):
        b, p = c // 2, c % 2
        gtiles = [2 * j + p for j in range(NQ)]
        cols = np.concatenate([np.arange(g * 128, (g + 1) * 128) for g in gtiles])
        of = np.asarray(res.results[c]["out_fm"])
        out[b, cols, :] = of.T
    return out


# version 07
# speedup vs baseline: 1.0597x; 1.0144x over previous
import math
from contextlib import ExitStack

import numpy as np
import ml_dtypes

import concourse.bass as bass
import concourse.mybir as mybir
from concourse.bass_utils import run_bass_kernel_spmd

F32 = mybir.dt.float32
BF16 = mybir.dt.bfloat16
I32 = mybir.dt.int32
AF = mybir.ActivationFunctionType
OP = mybir.AluOpType
AX = mybir.AxisListType

D = 2048
T = 1024
NH = 16
HD = 128
KC = D // 128
NQ = T // 128
SEQ = 2048
NKT = SEQ // 128
NCORE = 8
NBLOCK = 4
PHASES = 99
DEVDUMP = False

EPS_LN = 1e-5
EPS_L0Q = 1e-8
EPS_L1 = 1e-5
QB = 127.0
ISQD = 1.0 / math.sqrt(128.0)

ENGINES = ("sync", "gpsimd", "vector", "scalar", "tensor")


class StopEmission(Exception):
    pass


class Build:
    def __init__(self):
        self.q = {e: [] for e in ENGINES}
        self.ctr = {}
        self.phase = 0
        self.stopped = False

    def mark(self):
        self.phase += 1
        if self.phase >= PHASES:
            self.stopped = True

    def go(self):
        return not self.stopped

    def inc(self, sem, by=1):
        if not self.stopped:
            self.ctr[sem] = self.ctr.get(sem, 0) + by
        return self.ctr.get(sem, 0)

    def val(self, sem):
        return self.ctr.get(sem, 0)

    def emit(self, eng, fn):
        if not self.stopped:
            self.q[eng].append(fn)


def build_program():
    nc = bass.Bass("TRN2", target_bir_lowering=False, debug=False,
                   detect_race_conditions=False)

    di = lambda n, s, dt=F32: nc.dram_tensor(n, s, dt, kind="ExternalInput").ap()
    x_fm = di("x_fm", [D, T])
    cos_own_h = di("cos_own", [128, T])
    sin_own_h = di("sin_own", [128, T])
    maskA_h = di("maskA", [128, 128])
    maskB_h = di("maskB", [128, 128])
    ones_h = di("ones128", [128, 128], BF16)
    ident_h = di("ident", [128, 128], BF16)
    identf_h = di("identf", [128, 128], F32)
    jrot_h = di("jrot", [128, 128], BF16)
    e16_h = di("e16", [8, 128], BF16)
    sel8_h = di("sel8", [8, 8 * 128], BF16)
    sel32_h = di("sel32", [32, 32 * 128], BF16)
    wsl = {nm: di(nm + "_s", [256, D]) for nm in
           ("l0q", "l0k", "l0v", "l0o", "l1q", "l1k", "l1v", "l1o")}
    l0fWT = di("l0fWT", [D, D])
    l0f_b_h = di("l0f_b", [1, D])
    l0f_AT = di("l0f_AT", [D, 128])
    l0f_B = di("l0f_B", [128, D])
    l0f_gwT = di("l0f_gwT", [D, 8])
    l0f_gb_h = di("l0f_gb", [1, 8])
    l1f_rwT = di("l1f_rwT", [D, 32])
    l1f_wg = di("l1f_wg", [32, D, 768])
    l1f_wu = di("l1f_wu", [32, D, 768])
    l1f_wd = di("l1f_wd", [32, 768, D])
    out_fm = nc.dram_tensor("out_fm", [D, T], F32, kind="ExternalOutput").ap()
    if DEVDUMP:
        dbg_ctx = nc.dram_tensor("dbg_ctx", [128, NH * T], BF16, kind="ExternalOutput").ap()
        dbg_k = nc.dram_tensor("dbg_k", [128, SEQ], BF16, kind="ExternalOutput").ap()
        dbg_v = nc.dram_tensor("dbg_v", [128, SEQ], BF16, kind="ExternalOutput").ap()
        dbg_xln = nc.dram_tensor("dbg_xln", [128, KC * T], BF16, kind="ExternalOutput").ap()
        dbg_cols = nc.dram_tensor("dbg_cols", [128, 64], F32, kind="ExternalOutput").ap()
        dbg_kvin = nc.dram_tensor("dbg_kvin", [2, NH, HD, T], BF16, kind="ExternalOutput").ap()
        dbg_p0in = nc.dram_tensor("dbg_p0in", [4, D, 256], BF16, kind="ExternalOutput").ap()
        dbg_p0out = nc.dram_tensor("dbg_p0out", [2, 4, D, 256], BF16, kind="ExternalOutput").ap()

    prep0_in = nc.dram_tensor("prep0_in", [4, D, 256], BF16).ap()
    prep0_out = nc.dram_tensor("prep0_out", [NCORE, 4, D, 256], BF16).ap()
    prep1_in = nc.dram_tensor("prep1_in", [4, D, 256], BF16).ap()
    prep1_out = nc.dram_tensor("prep1_out", [NCORE, 4, D, 256], BF16).ap()
    sums_in = nc.dram_tensor("sums_in", [4, 1], F32).ap()
    sums_out = nc.dram_tensor("sums_out", [NCORE * 4, 1], F32).ap()
    kv_in = nc.dram_tensor("kv_in", [NH, 2, HD, T], BF16).ap()
    kv_outA = nc.dram_tensor("kv_outA", [2, 8, 2, HD, T], BF16).ap()
    kv_outB = nc.dram_tensor("kv_outB", [2, 8, 2, HD, T], BF16).ap()
    SHARD = 2 * NH * HD * T

    B = Build()
    ALL8 = [list(range(NCORE))]

    with ExitStack() as ctx:
        sbt = lambda n, s, dt=F32: ctx.enter_context(nc.sbuf_tensor(n, s, dt))
        pst = lambda n, s, dt=F32: ctx.enter_context(nc.psum_tensor(n, s, dt))
        sem = lambda n: ctx.enter_context(nc.semaphore(n))

        x = sbt("x", [128, KC * T])
        xln = sbt("xln", [128, KC * T], BF16)
        mean_rep = sbt("mean_rep", [128, T])
        rstd_rep = sbt("rstd_rep", [128, T])
        rep_a = sbt("rep_a", [128, T])
        rep_b = sbt("rep_b", [128, T])
        tmp1 = sbt("tmp1", [128, T])
        tmp2 = sbt("tmp2", [128, T])
        tmp3 = sbt("tmp3", [128, T])
        cosb = sbt("cosb", [128, T])
        sinb = sbt("sinb", [128, T])
        ones_bf = sbt("ones_bfb", [128, 128], BF16)
        ident = sbt("identb", [128, 128], BF16)
        identf = sbt("identfb", [128, 128])
        jrot = sbt("jrotb", [128, 128], BF16)
        e16 = sbt("e16b", [8, 128], BF16)
        selbuf = sbt("selbuf", [32, 128], BF16)
        mA = sbt("mA", [128, 128])
        mB = sbt("mB", [128, 128])
        fone = sbt("fone", [128, 1])
        onesrow = sbt("onesrow", [1, 512], BF16)
        qncol = sbt("qncol", [128, NQ])
        k2col = sbt("k2col", [128, 1])
        bcol = sbt("bcol", [128, NQ])
        lcol = sbt("lcol", [128, NQ])
        rlcol = sbt("rlcol", [128, NQ])
        gcolM = sbt("gcolM", [128, NQ])
        gcolR = sbt("gcolR", [128, NQ])
        gcolA = sbt("gcolA", [128, NQ])
        gcolB = sbt("gcolB", [128, NQ])
        brow = sbt("brow", [8, 256], BF16)
        sums8 = sbt("sums8", [32, 1])
        scol = sbt("scol", [128, 4])
        thcol = sbt("thcol", [128, 4])
        gtok = sbt("gtok", [128, 128])
        gtk2 = sbt("gtk2", [128, 32])
        gtk3 = sbt("gtk3", [128, 32])
        gm1 = sbt("gm1", [128, 1])
        gm2 = sbt("gm2", [128, 1])
        gden = sbt("gden", [128, 1])
        biasrow = sbt("biasrow", [1, D], BF16)
        gbrow = sbt("gbrow", [1, 8], BF16)
        wselb = sbt("wselb", [32, T], BF16)
        wrepf = sbt("wrepf", [128, T])

        tint = tmp2.ap().bitcast(I32)[:, 0:T]
        Mmax = tmp3

        s_ld = sem("s_ld")
        s_gd = sem("s_gd")
        s_cc = sem("s_cc")
        s_t = sem("s_t")
        s_v = sem("s_v")
        s_s = sem("s_s")
        s_st = sem("s_st")
        SEMS = {"ld": s_ld, "gd": s_gd, "cc": s_cc, "t": s_t, "v": s_v,
                "s": s_s, "st": s_st}

        def wt(e, semname, val):
            if val > 0:
                e.wait_ge(SEMS[semname], val)

        def dma(eng, out, in_, semname="ld", qsem=None):
            tgt = B.inc(semname, 16)

            def f(e, out=out, in_=in_, tgt=tgt):
                e.dma_start(out=out, in_=in_).then_inc(SEMS[semname], 16)
            B.emit(eng, f)
            return tgt

        s_vx = sem("s_vx")
        s_sx = sem("s_sx")
        s_gx = sem("s_gx")
        AUX = {"vector": s_vx, "scalar": s_sx, "gpsimd": s_gx}

        class _Rec:

            def __init__(self, eng, aux, ctr):
                self._e = eng
                self._aux = aux
                self._ctr = ctr
                self.last = None

            def __getattr__(self, name):
                a = getattr(self._e, name)
                if callable(a):
                    def call(*args, _a=a, _n=name, **kw):
                        if self._aux is not None and _n not in (
                                "wait_ge", "sem_inc", "nop", "partition_id",
                                "compute_val"):
                            if self._ctr[0] > 0:
                                self._e.wait_ge(self._aux, self._ctr[0])
                        r = _a(*args, **kw)
                        if hasattr(r, "then_inc"):
                            self.last = r
                            if self._aux is not None and _n not in (
                                    "wait_ge", "sem_inc", "nop"):
                                r.then_inc(self._aux, 1)
                                self._ctr[0] += 1
                        return r
                    return call
                return a

        AUXCTR = {"vector": [0], "scalar": [0], "gpsimd": [0]}

        def _mkstep(eng, semname, semh):
            aux = AUX.get(eng)
            actr = AUXCTR.get(eng)

            def step(fn, *waits):
                B.inc(semname)

                def f(e, fn=fn, waits=waits):
                    for sname, val in waits:
                        wt(e, sname, val)
                    rec = _Rec(e, aux, actr)
                    fn(rec)
                    assert rec.last is not None, "step emitted no instruction"
                    if aux is not None:
                        e.wait_ge(aux, actr[0])
                        e.sem_inc(semh, 1)
                    else:
                        rec.last.then_inc(semh, 1)
                B.emit(eng, f)
                return B.val(semname)
            return step

        vstep = _mkstep("vector", "v", s_v)
        sstep = _mkstep("scalar", "s", s_s)
        tstep = _mkstep("tensor", "t", s_t)
        gstep = _mkstep("gpsimd", "gd", s_gd)

        def gdma(fn, *waits):
            B.inc("gd", 16)

            def f(g, fn=fn, waits=waits):
                for sname, val in waits:
                    wt(g, sname, val)
                fn(g).then_inc(s_gd, 16)
            B.emit("gpsimd", f)
            return B.val("gd")

        def syncstep(fn, *waits):
            def f(e, fn=fn, waits=waits):
                for sname, val in waits:
                    wt(e, sname, val)
                fn(e)
            B.emit("sync", f)

        PAIRS = [[0, 1], [2, 3], [4, 5], [6, 7]]

        def collective(ins, outs, *waits, groups=None):
            tgt = B.inc("cc", 1)
            g_ = groups or ALL8

            def f(g, ins=ins, outs=outs, tgt=tgt, waits=waits, g_=g_):
                for sname, val in waits:
                    wt(g, sname, val)
                g.collective_compute("AllGather", OP.bypass, replica_groups=g_,
                                     ins=ins, outs=outs).then_inc(s_cc, 1)
            B.emit("gpsimd", f)
            return tgt

        def allv():
            return ("v", B.val("v"))

        def alls():
            return ("s", B.val("s"))

        def allt():
            return ("t", B.val("t"))

        def allld():
            return ("ld", B.val("ld"))

        def allgd():
            return ("gd", B.val("gd"))

        def allcc():
            return ("cc", B.val("cc"))

        pja = pst("pja", [128, T])
        pjb = pst("pjb", [128, T])
        trp = pst("trp", [128, 2048], BF16)
        smp = pst("smp", [128, 512])
        spx = pst("spx", [128, 512])

        for o, i in ((x[:, :].rearrange("p (kc t) -> p kc t", t=T),
                      x_fm.rearrange("(kc p) t -> p kc t", p=128)),
                     (cosb[:, :], cos_own_h), (sinb[:, :], sin_own_h),
                     (mA[:, :], maskA_h), (mB[:, :], maskB_h),
                     (ones_bf[:, :], ones_h), (ident[:, :], ident_h),
                     (identf[:, :], identf_h), (jrot[:, :], jrot_h),
                     (e16[:, :], e16_h)):
            dma("sync", o, i)
        LD_INIT = B.val("ld")
        vstep(lambda v: (v.memset(onesrow[:, :], 1.0), v.memset(fone[:, :], 1.0)))
        B.mark()


        with nc.sbuf_tensor("qsl0", [128, D], F32) as qsl0, \
             nc.sbuf_tensor("qsl1", [128, D], F32) as qsl1, \
             nc.sbuf_tensor("qc1", [128, D], F32) as qc1, \
             nc.sbuf_tensor("qtr", [128, D], BF16) as qtr, \
             nc.sbuf_tensor("qsc", [128, 16], F32) as qsc, \
             nc.sbuf_tensor("qth", [128, 16], F32) as qth:
            qsl = [qsl0, qsl1]

            def emit_prep(names, prep_in_ap, l1_mode):
                for wi, nm in enumerate(names):
                    for rt in range(2):
                        syncstep(lambda e, nm=nm, rt=rt:
                                 e.dma_start(out=qsl[rt][:, :],
                                             in_=wsl[nm][rt * 128:(rt + 1) * 128, :]
                                             ).then_inc(s_ld, 16), allv())
                        B.inc("ld", 16)
                        ld_t = B.val("ld")

                        def vq(v, rt=rt, wi=wi, l1_mode=l1_mode):
                            if l1_mode:
                                thr1 = thcol[:, wi:wi + 1]
                                scl1 = scol[:, wi:wi + 1]
                                thr = lambda g: thr1.broadcast_to([128, 128])
                                scl = lambda g: scl1.broadcast_to([128, 128])
                            else:
                                v.tensor_reduce(qsc[:, :],
                                                qsl[rt][:, :].rearrange("p (g n) -> p g n", g=16),
                                                AX.X, OP.add, apply_absolute_value=True)
                                v.tensor_scalar(qsc[:, :], qsc[:, :], 1.0 / 128, EPS_L0Q,
                                                OP.mult, OP.max)
                                v.tensor_scalar(qth[:, :], qsc[:, :], 0.5, None, OP.mult)
                                thr = lambda g: qth[:, g:g + 1].broadcast_to([128, 128])
                                scl = lambda g: qsc[:, g:g + 1].broadcast_to([128, 128])
                            for g in range(16):
                                sl = slice(g * 128, (g + 1) * 128)
                                v.tensor_tensor(qc1[:, sl], qsl[rt][:, sl], thr(g), OP.is_ge)
                            v.tensor_scalar(qsl[rt][:, :], qsl[rt][:, :], -1.0, None, OP.mult)
                            for g in range(16):
                                sl = slice(g * 128, (g + 1) * 128)
                                v.tensor_tensor(qsl[rt][:, sl], qsl[rt][:, sl], thr(g), OP.is_le)
                            v.tensor_tensor(qc1[:, :], qc1[:, :], qsl[rt][:, :], OP.add)
                            for g in range(16):
                                sl = slice(g * 128, (g + 1) * 128)
                                v.scalar_tensor_tensor(qtr[:, sl], qc1[:, sl], -1.0, scl(g),
                                                       OP.add, OP.mult)
                        v_t = vstep(vq, ("ld", ld_t), alls())

                        t_t = tstep(lambda t: [t.transpose(trp[:, kc * 128:(kc + 1) * 128],
                                                           qtr[:, kc * 128:(kc + 1) * 128],
                                                           ident[:, :])
                                               for kc in range(KC)], ("v", v_t), alls())
                        s_t2 = sstep(lambda s: s.copy(qtr[:, :], trp[:, 0:D]), ("t", t_t))
                        syncstep(lambda e, wi=wi, rt=rt:
                                 e.dma_start(
                                     out=prep_in_ap[wi, :, rt * 128:(rt + 1) * 128].rearrange(
                                         "(kc p) o -> p kc o", p=128),
                                     in_=qtr.ap().rearrange("p (kc o) -> p kc o", o=128)
                                 ).then_inc(s_ld, 16), ("s", s_t2))
                        B.inc("ld", 16)

            emit_prep(("l0q", "l0k", "l0v", "l0o"), prep0_in, l1_mode=False)
            B.mark()
            cc_prep0 = collective([prep0_in.opt()], [prep0_out.opt()], allld())
            B.mark()

            vstep(lambda v: v.memset(Mmax[:, 0:16], 0.0))
            for wi, nm in enumerate(("l1q", "l1k", "l1v", "l1o")):
                for rt in range(2):
                    syncstep(lambda e, nm=nm, rt=rt:
                             e.dma_start(out=qsl[rt][:, :],
                                         in_=wsl[nm][rt * 128:(rt + 1) * 128, :]
                                         ).then_inc(s_ld, 16), allv())
                    B.inc("ld", 16)
                    ld_t = B.val("ld")

                    def vs(v, wi=wi, rt=rt):
                        v.tensor_reduce(Mmax[:, 8 + wi:9 + wi], qsl[rt][:, :], AX.X,
                                        OP.add, apply_absolute_value=True)
                        v.tensor_tensor(Mmax[:, wi:wi + 1], Mmax[:, wi:wi + 1],
                                        Mmax[:, 8 + wi:9 + wi], OP.add)
                    vstep(vs, ("ld", ld_t))
            t_t = tstep(lambda t: t.matmul(smp[0:4, 0:1], Mmax[:, 0:4], fone[:, :],
                                           start=True, stop=True), allv())
            v_t = vstep(lambda v: v.tensor_copy(Mmax[0:4, 16:17] if False else tmp1[0:4, 0:1],
                                                smp[0:4, 0:1]), ("t", t_t))
            syncstep(lambda e: e.dma_start(out=sums_in, in_=tmp1[0:4, 0:1]
                                           ).then_inc(s_ld, 16), ("v", v_t))
            B.inc("ld", 16)
            B.mark()
            cc_sums = collective([sums_in.opt()], [sums_out.opt()], allld())
            B.mark()

            syncstep(lambda e: e.dma_start(out=sums8[:, :], in_=sums_out
                                           ).then_inc(s_ld, 16), allcc())
            B.inc("ld", 16)
            ld_t = B.val("ld")
            syncstep(lambda e: e.dma_start(
                out=gtok[0:8, 0:4], in_=sums_out.rearrange("(r w) one -> r (w one)", w=4)
            ).then_inc(s_ld, 16))
            B.inc("ld", 16)
            ld_t = B.val("ld")
            v_t = vstep(lambda v: v.memset(mean_rep[0:8, 0:128], 1.0), ("ld", ld_t))
            t_t = tstep(lambda t: t.matmul(smp[:, 4:8], mean_rep[0:8, 0:128],
                                           gtok[0:8, 0:4], start=True, stop=True),
                        ("v", v_t))
            vstep(lambda v: (v.tensor_scalar(scol[:, 0:4], smp[:, 4:8],
                                             1.0 / (D * D), EPS_L1, OP.mult, OP.add),
                             v.tensor_scalar(thcol[:, 0:4], scol[:, 0:4], 0.5, None,
                                             OP.mult)),
                  ("t", t_t))

            B.mark()
            emit_prep(("l1q", "l1k", "l1v", "l1o"), prep1_in, l1_mode=True)
            B.mark()
            cc_prep1 = collective([prep1_in.opt()], [prep1_out.opt()], allld())
            B.mark()

        def emit_ln_stats():
            for kc in range(KC):
                xb = x2b[:, 0:512] if False else None
                s1 = sstep(lambda s, kc=kc: s.copy(xln[:, kc * T:(kc + 1) * T],
                                                   x[:, kc * T:(kc + 1) * T]),
                           *([("ld", LD_INIT), allv(), allt()] if kc == 0 else []))
                half_buf = x2b.ap().rearrange("p (two t) -> p two t", two=2)[:, kc % 2, :] \
                    if False else None
                def ssq(s, kc=kc, prev_t=B.val("t")):
                    wt(s, "t", prev_t)
                    s.activation(x2b[:, 0:T], xln[:, kc * T:(kc + 1) * T], AF.Square)
                s2 = sstep(ssq)

                def tmm(t, kc=kc):
                    for half in range(2):
                        sl = slice(half * 512, (half + 1) * 512)
                        t.matmul(pja[:, sl], ones_bf[:, :],
                                 xln[:, kc * T + half * 512:kc * T + (half + 1) * 512],
                                 start=(kc == 0), stop=(kc == KC - 1))
                        t.matmul(pjb[:, sl], ones_bf[:, :], x2b[:, sl],
                                 start=(kc == 0), stop=(kc == KC - 1))
                tstep(tmm, ("s", s2))
            t_stats = B.val("t")

            def vst(v):
                v.tensor_scalar(mean_rep[:, :], pja[:, :], 1.0 / D, None, OP.mult)
                v.tensor_scalar(tmp1[:, :], pjb[:, :], 1.0 / D, None, OP.mult)
                v.tensor_tensor(tmp2[:, :], mean_rep[:, :], mean_rep[:, :], OP.mult)
                v.tensor_tensor(tmp1[:, :], tmp1[:, :], tmp2[:, :], OP.subtract)
                v.tensor_scalar(tmp1[:, :], tmp1[:, :], EPS_LN, None, OP.add)
            v_t = vstep(vst, ("t", t_stats))
            s_t2 = sstep(lambda s: s.activation(tmp2[:, :], tmp1[:, :], AF.Sqrt),
                         ("v", v_t))
            vstep(lambda v: v.reciprocal(rstd_rep[:, :], tmp2[:, :]), ("s", s_t2))

        def emit_ln_apply():
            for kc in range(KC):
                def va(v, kc=kc):
                    v.scalar_tensor_tensor(tmp1[:, :], mean_rep[:, :], -1.0,
                                           x[:, kc * T:(kc + 1) * T], OP.mult, OP.add)
                    v.tensor_tensor(xln[:, kc * T:(kc + 1) * T], tmp1[:, :],
                                    rstd_rep[:, :], OP.mult)
                vstep(va, *([allt()] if kc == 0 else []))

        def emit_colrow_broadcast(colA, colB, repA, repB):
            syncstep(lambda e: e.dma_start(out=sel8b, in_=sel8_h).then_inc(s_ld, 16),
                     allt(), alls())
            B.inc("ld", 16)
            ld_s8 = B.val("ld")
            v_t = vstep(lambda v: (v.memset(tmp1[:, 0:128], 0.0),
                                   v.memset(tmp2[:, 0:128], 0.0),
                                   v.tensor_copy(tmp1[:, 0:NQ], colA),
                                   v.tensor_copy(tmp2[:, 0:NQ], colB)))
            t_t = tstep(lambda t: (t.transpose(smp[:, 0:128], tmp1[:, 0:128], identf[:, :]),
                                   t.transpose(smp[:, 128:256], tmp2[:, 0:128], identf[:, :])),
                        ("v", v_t))
            v_t = vstep(lambda v: (v.tensor_copy(brow[0:NQ, 0:128], smp[0:NQ, 0:128]),
                                   v.tensor_copy(brow[0:NQ, 128:256], smp[0:NQ, 128:256])),
                        ("t", t_t))

            def tbc(t, ld_s8=ld_s8):
                wt(t, "ld", ld_s8)
                for j in range(NQ):
                    t.matmul(pja[:, j * 128:(j + 1) * 128],
                             sel8b[:, j * 128:(j + 1) * 128],
                             brow[0:NQ, 0:128], start=True, stop=True)
                    t.matmul(pjb[:, j * 128:(j + 1) * 128],
                             sel8b[:, j * 128:(j + 1) * 128],
                             brow[0:NQ, 128:256], start=True, stop=True)
            t_t = tstep(tbc, ("v", v_t))
            vstep(lambda v: (v.tensor_copy(repA, pja[:, :]),
                             v.tensor_copy(repB, pjb[:, :])), ("t", t_t))

        def emit_maxcols(src_is_ln):
            vstep(lambda v: v.memset(Mmax[:, 0:T], 0.0))
            for kc in range(KC):
                if src_is_ln:
                    v_t = vstep(lambda v, kc=kc: v.scalar_tensor_tensor(
                        tmp1[:, :], mean_rep[:, :], -1.0, x[:, kc * T:(kc + 1) * T],
                        OP.mult, OP.add))
                    s_t2 = sstep(lambda s: s.activation(tmp2[:, :], tmp1[:, :], AF.Abs),
                                 ("v", v_t))
                else:
                    s_t2 = sstep(lambda s, kc=kc: s.activation(
                        tmp2[:, :], ctx_all[:, kc * T:(kc + 1) * T], AF.Abs), allv())
                vstep(lambda v: v.tensor_max(Mmax[:, 0:T], Mmax[:, 0:T], tmp2[:, :]),
                      ("s", s_t2))
            for batch in range(2):
                t_t = tstep(lambda t, batch=batch: [t.transpose(
                    smp[:, jj * 128:(jj + 1) * 128],
                    Mmax[:, (batch * 4 + jj) * 128:(batch * 4 + jj + 1) * 128],
                    identf[:, :]) for jj in range(4)], allv())
                vstep(lambda v, batch=batch: [v.reduce_max(
                    gcolM[:, batch * 4 + jj:batch * 4 + jj + 1],
                    smp[:, jj * 128:(jj + 1) * 128], AX.X) for jj in range(4)],
                    ("t", t_t))
            if src_is_ln:
                for batch in range(2):
                    t_t = tstep(lambda t, batch=batch: [t.transpose(
                        smp[:, jj * 128:(jj + 1) * 128],
                        rstd_rep[:, (batch * 4 + jj) * 128:(batch * 4 + jj + 1) * 128],
                        identf[:, :]) for jj in range(4)], allv())
                    vstep(lambda v, batch=batch: [v.tensor_copy(
                        gcolR[:, batch * 4 + jj:batch * 4 + jj + 1],
                        smp[:, jj * 128:jj * 128 + 1]) for jj in range(4)], ("t", t_t))

        def emit_l1_input_quant():
            emit_maxcols(src_is_ln=True)

            def vg(v):
                v.tensor_tensor(gcolA[:, :], gcolR[:, :], gcolM[:, :], OP.mult)
                v.tensor_scalar(gcolA[:, :], gcolA[:, :], EPS_L1, None, OP.add)
                v.tensor_scalar(gcolB[:, :], gcolA[:, :], 1.0 / QB, None, OP.mult)
                v.reciprocal(gcolM[:, :], gcolA[:, :])
                v.tensor_tensor(gcolA[:, :], gcolM[:, :], gcolR[:, :], OP.mult)
                v.tensor_scalar(gcolA[:, :], gcolA[:, :], QB, None, OP.mult)
            vstep(vg)
            emit_colrow_broadcast(gcolA[:, :], gcolB[:, :], rep_a[:, :], rep_b[:, :])
            for kc in range(KC):
                v_t = vstep(lambda v, kc=kc: (
                    v.scalar_tensor_tensor(tmp1[:, :], mean_rep[:, :], -1.0,
                                           x[:, kc * T:(kc + 1) * T], OP.mult, OP.add),
                    v.tensor_tensor(tmp1[:, :], tmp1[:, :], rep_a[:, :], OP.mult),
                    v.tensor_copy(tint, tmp1[:, :]),
                    v.tensor_copy(xln[:, kc * T:(kc + 1) * T], tint)),
                    *([allt()] if kc == 0 else []))

        def emit_l1_ctx_quant():
            emit_maxcols(src_is_ln=False)

            def vg(v):
                v.tensor_scalar(gcolA[:, :], gcolM[:, :], EPS_L1, None, OP.add)
                v.tensor_scalar(gcolB[:, :], gcolA[:, :], 1.0 / QB, None, OP.mult)
                v.reciprocal(gcolM[:, :], gcolA[:, :])
                v.tensor_scalar(gcolA[:, :], gcolM[:, :], QB, None, OP.mult)
            vstep(vg)
            emit_colrow_broadcast(gcolA[:, :], gcolB[:, :], rep_a[:, :], rep_b[:, :])
            for kc in range(KC):
                s_t2 = sstep(lambda s, kc=kc: s.copy(tmp2[:, :],
                                                     ctx_all[:, kc * T:(kc + 1) * T]),
                             allv())
                vstep(lambda v, kc=kc: (
                    v.tensor_tensor(tmp1[:, :], tmp2[:, :], rep_a[:, :], OP.mult),
                    v.tensor_copy(tint, tmp1[:, :]),
                    v.tensor_copy(xln[:, kc * T:(kc + 1) * T], tint)),
                    ("s", s_t2), *([allt()] if kc == 0 else []))

        arena = sbt("arena", [128, 30848], BF16)
        A = arena.ap()
        srow = A[0:1, 29824:30848]
        ctx_all = A[:, 0:16384]
        Ksb = A[:, 16384:18432]
        Vsb = A[:, 18432:20480]
        Vtm = A[:, 20480:22528]
        Pbf = A[:, 22528:24576]
        Ptrb = A[:, 24576:26624]
        qh_bf = A[:, 26624:27648]
        kh_bf = Pbf[:, 0:T]
        sel8b = Ptrb[0:8, 0:1024]
        x2b = qh_bf
        ctmb = A[:, 29696:29824]
        wqb = [A[:, 27648:29696]]
        wkb = [Ptrb]
        wvb = [Vtm]
        ring = [A[:, i * 6144:(i + 1) * 6144] for i in range(4)]
        hb = A[:, 24576:30720]
        rwbb = A[:, 24576:25088]

        def emit_attention(layer):
            prep_out = prep0_out if layer == 0 else prep1_out
            causal = (layer == 0)

            if layer == 1:
                vstep(lambda v: (v.tensor_tensor(cosb[:, :], cosb[:, :], rep_b[:, :], OP.mult),
                                 v.tensor_tensor(sinb[:, :], sinb[:, :], rep_b[:, :], OP.mult)))

            def load_whead(e, w_idx, h, buf):
                r, half = h // 2, h % 2
                bap = buf
                e.dma_start(
                    out=bap.rearrange("p (kc o) -> p kc o", o=128),
                    in_=prep_out[r, w_idx, :, half * 128:(half + 1) * 128].rearrange(
                        "(kc p) o -> p kc o", p=128)).then_inc(s_ld, 16)

            for w_idx, bufs in ((0, wqb), (1, wkb), (2, wvb)):
                syncstep(lambda e, w_idx=w_idx, bufs=bufs: load_whead(e, w_idx, 0, bufs[0]),
                         allcc())
                B.inc("ld", 16)

            cc_kv1 = [None]
            for h in range(NH):
                if h == 8:
                    cc_kv1[0] = collective([kv_in[0:8].opt()], [kv_outA.opt()],
                                           allld(), groups=PAIRS)
                ld_w = B.val("ld")

                for which, bufs, w_idx in (("q", wqb, 0), ("k", wkb, 1), ("v", wvb, 2)):
                    def tmm(t, which=which, bufs=bufs, h=h, ld_w=ld_w):
                        wt(t, "ld", ld_w)
                        dst = pja if which != "v" else pjb
                        for kcc in range(KC):
                            for half in range(2):
                                sl = slice(half * 512, (half + 1) * 512)
                                t.matmul(dst[:, sl], bufs[0][:, kcc * 128:(kcc + 1) * 128],
                                         xln[:, kcc * T + half * 512:kcc * T + (half + 1) * 512],
                                         start=(kcc == 0), stop=(kcc == KC - 1))
                    t_mm = tstep(tmm, allv())
                    if h + 1 < NH:
                        syncstep(lambda e, w_idx=w_idx, bufs=bufs, h=h:
                                 load_whead(e, w_idx, h + 1, bufs[0]), ("t", t_mm))
                        B.inc("ld", 16)

                    if which == "v":
                        if layer == 0:
                            d_s = sstep(lambda s: s.copy(kh_bf, pjb[:, :]),
                                        ("t", t_mm), allld())
                            dsem = ("s", d_s)
                        else:
                            d_v = vstep(lambda v: v.tensor_tensor(kh_bf, pjb[:, :],
                                                                  rep_b[:, :], OP.mult),
                                        ("t", t_mm), allld())
                            dsem = ("v", d_v)
                        syncstep(lambda e, h=h: e.dma_start(out=kv_in[h, 1, :, :],
                                                            in_=kh_bf
                                                            ).then_inc(s_ld, 16), dsem)
                        B.inc("ld", 16)
                    else:
                        s_e = sstep(lambda s: s.copy(kh_bf, pja[:, :]), ("t", t_mm))
                        t_j = tstep(lambda t: [t.matmul(pjb[:, sl], jrot[:, :], kh_bf[0:128, sl],
                                                        start=True, stop=True)
                                               for sl in (slice(0, 512), slice(512, 1024))],
                                    ("s", s_e))

                        def vrope(v, which=which, h=h):
                            v.tensor_tensor(tmp1[:, :], pja[:, :], cosb[:, :], OP.mult)
                            v.tensor_tensor(tmp2[:, :], pjb[:, :], sinb[:, :], OP.mult)
                            dst = ctx_all[:, h * T:(h + 1) * T] if which == "q" else kh_bf
                            v.tensor_tensor(dst, tmp1[:, :], tmp2[:, :], OP.add)
                        v_r = vstep(vrope, ("t", t_j))
                        if which == "k":
                            syncstep(lambda e, h=h: e.dma_start(out=kv_in[h, 0, :, :],
                                                                in_=kh_bf
                                                                ).then_inc(s_ld, 16),
                                     ("v", v_r))
                            B.inc("ld", 16)

            B.mark()
            cc_kv2 = collective([kv_in[8:16].opt()], [kv_outB.opt()],
                                allld(), groups=PAIRS)
            B.mark()

            kv_free = {"t_sc": 0, "s_k": 0, "t_v": 0}
            for h in range(NH):
                cc_kv = cc_kv1[0] if h < 8 else cc_kv2
                def ldkv(e, h=h, cc_kv=cc_kv):
                    wt(e, "cc", cc_kv)
                    for rq in range(2):
                        for kv, dstt in ((0, Ksb), (1, Vsb)):
                            src = (kv_outA if h < 8 else kv_outB)[rq, h % 8, kv, :, :]
                            dstap = dstt.rearrange("p (u two n) -> p u two n",
                                                        two=2, n=128)[:, :, rq, :]
                            e.dma_start(out=dstap, in_=src).then_inc(s_ld, 16)
                pt = max(kv_free["t_sc"], kv_free["t_v"])
                pv_ = kv_free["s_k"]

                def ldkv_g(e, ldkv=ldkv, pt=pt, pv_=pv_):
                    wt(e, "t", pt)
                    wt(e, "v", pv_)
                    ldkv(e)
                B.emit("sync", ldkv_g)
                B.inc("ld", 16 * 4)
                ld_kv = B.val("ld")

                t_v = tstep(lambda t: [t.transpose(trp[:, u * 128:(u + 1) * 128],
                                                   Vsb[:, u * 128:(u + 1) * 128],
                                                   ident[:, :]) for u in range(NKT)],
                            ("ld", ld_kv), alls())
                kv_free["t_v"] = t_v
                s_vtm = vstep(lambda v: v.tensor_copy(Vtm[:, :], trp[:, 0:2048]), ("t", t_v))

                s_q = sstep(lambda s, h=h: s.activation(qh_bf[:, :],
                                                        ctx_all[:, h * T:(h + 1) * T],
                                                        AF.Square), allv())
                s_k = vstep(lambda v: (v.tensor_tensor(Pbf[:, 0:T], Ksb[:, 0:T],
                                                       Ksb[:, 0:T], OP.mult),
                                       v.tensor_tensor(Pbf[:, T:SEQ], Ksb[:, T:SEQ],
                                                       Ksb[:, T:SEQ], OP.mult)),
                            ("ld", ld_kv))

                def tqn(t, s_q=s_q, s_k=s_k):
                    wt(t, "s", s_q)
                    wt(t, "v", s_k)
                    t.matmul(smp[0:1, 0:512], ones_bf[:, 0:1], qh_bf[:, 0:512],
                             start=True, stop=True)
                    t.matmul(spx[0:1, 0:512], ones_bf[:, 0:1], qh_bf[:, 512:1024],
                             start=True, stop=True)
                    for half in range(2):
                        t.matmul(pja[:, half * 512:(half + 1) * 512], ones_bf[:, :],
                                 Pbf[:, half * 512:(half + 1) * 512], start=True, stop=True)
                        t.matmul(pjb[:, half * 512:(half + 1) * 512], ones_bf[:, :],
                                 Pbf[:, T + half * 512:T + (half + 1) * 512],
                                 start=True, stop=True)
                t_n = tstep(tqn)

                def vkn(v, t_n=t_n):
                    v.reduce_max(k2col[:, :], pja[:, :], AX.X)
                    v.reduce_max(qncol[:, 0:1], pjb[:, :], AX.X)
                    v.tensor_tensor(k2col[:, :], k2col[:, :], qncol[:, 0:1], OP.max)
                    v.tensor_copy(srow[0:1, 0:512], smp[0:1, 0:512])
                    v.tensor_copy(srow[0:1, 512:1024], spx[0:1, 0:512])
                v_n = vstep(vkn, ("t", t_n))

                def tqc(t):
                    for jj in range(NQ):
                        t.matmul(smp[:, 256 + jj:257 + jj],
                                 srow[0:1, jj * 128:(jj + 1) * 128],
                                 ones_bf[0:1, 0:1], start=True, stop=True)
                t_c = tstep(tqc, ("v", v_n))
                v_c2 = vstep(lambda v: v.tensor_copy(qncol[:, :], smp[:, 256:256 + NQ]),
                             ("t", t_c))
                v_b = vstep(lambda v: (
                    v.tensor_tensor(bcol[:, :], qncol[:, :],
                                    k2col[:, :].broadcast_to([128, NQ]), OP.add),
                    v.tensor_scalar(bcol[:, :], bcol[:, :], -1.0 / 256.0, None, OP.mult)),
                    ("v", v_c2))

                exp_hist = {}
                pv_hist = []
                for j in range(NQ):
                    w = (2 * j + 2) if causal else NKT
                    cw = w * 128
                    nslice = (cw + T - 1) // T
                    for sl_i in range(nslice):
                        c0 = sl_i * T
                        c1 = min(cw, (sl_i + 1) * T)
                        Sp = pja if sl_i == 0 else pjb

                        def tsc(t, h=h, j=j, c0=c0, c1=c1, Sp=Sp):
                            nn = c1 - c0
                            for off in range(0, nn, 512):
                                t.matmul(Sp[:, off:off + min(512, nn - off)],
                                         ctx_all[:, h * T + j * 128:h * T + (j + 1) * 128],
                                         Ksb[:, c0 + off:c0 + off + min(512, nn - off)],
                                         start=True, stop=True)
                        prevdep = exp_hist.get((j - 1, sl_i), 0)
                        t_sc = tstep(tsc, ("v", v_b), ("s", prevdep))
                        kv_free["t_sc"] = t_sc

                        if causal and sl_i == nslice - 1:
                            mlo = cw - 256 - c0
                            v_m = vstep(lambda v, Sp=Sp, mlo=mlo: (
                                v.tensor_tensor(Sp[:, mlo:mlo + 128], Sp[:, mlo:mlo + 128],
                                                mA[:, :], OP.add),
                                v.tensor_tensor(Sp[:, mlo + 128:mlo + 256],
                                                Sp[:, mlo + 128:mlo + 256], mB[:, :], OP.add)),
                                ("t", t_sc))
                            dep = ("v", v_m)
                        else:
                            dep = ("t", t_sc)

                        def sexp(s, h=h, j=j, sl_i=sl_i, c0=c0, c1=c1, Sp=Sp):
                            s.activation(Pbf[:, c0:c1], Sp[:, 0:c1 - c0], AF.Exp,
                                         bias=bcol[:, j:j + 1], scale=ISQD,
                                         accum_out=(lcol[:, j:j + 1] if sl_i == 0
                                                    else rlcol[:, j:j + 1]))
                        s_e = sstep(sexp, dep)
                        exp_hist[(j, sl_i)] = s_e
                    if nslice == 2:
                        v_l = vstep(lambda v, j=j: v.tensor_tensor(
                            lcol[:, j:j + 1], lcol[:, j:j + 1], rlcol[:, j:j + 1], OP.add),
                            ("s", B.val("s")))
                    v_rl = vstep(lambda v, j=j: v.reciprocal(rlcol[:, j:j + 1],
                                                             lcol[:, j:j + 1]),
                                 ("s", B.val("s")), ("v", B.val("v")))

                    t_tr = tstep(lambda t, j=j, w=w: [t.transpose(
                        trp[:, u * 128:(u + 1) * 128], Pbf[:, u * 128:(u + 1) * 128],
                        ident[:, :]) for u in range(w)], ("s", B.val("s")), allv())
                    s_ptr = vstep(lambda v, w=w: v.tensor_copy(Ptrb[:, 0:w * 128],
                                                        trp[:, 0:w * 128]), ("t", t_tr))

                    def tpv(t, j=j, w=w):
                        o = (j % 4) * 128
                        for u in range(w):
                            t.matmul(smp[:, o:o + 128], Ptrb[:, u * 128:(u + 1) * 128],
                                     Vtm[:, u * 128:(u + 1) * 128],
                                     start=(u == 0), stop=(u == w - 1))
                    t_pv = tstep(tpv, ("v", s_ptr))
                    pv_hist.append((t_pv, v_rl))
                    if j % 4 == 3:
                        j0 = j - 3
                        def vcn4(v, j0=j0):
                            for jj in range(4):
                                v.tensor_tensor(
                                    qh_bf[0:128, jj * 128:(jj + 1) * 128],
                                    smp[:, jj * 128:(jj + 1) * 128],
                                    rlcol[:, j0 + jj:j0 + jj + 1].broadcast_to([128, 128]),
                                    OP.mult)
                        s_cn = vstep(vcn4, ("t", pv_hist[-1][0]))
                        t_ct = tstep(lambda t: [t.transpose(
                            trp[:, 1536 + jj * 128:1536 + (jj + 1) * 128],
                            qh_bf[0:128, jj * 128:(jj + 1) * 128], ident[:, :])
                            for jj in range(4)], ("v", s_cn))
                        s_cw = vstep(lambda v, h=h, j0=j0: v.tensor_copy(
                            ctx_all[:, h * T + j0 * 128:h * T + (j0 + 4) * 128],
                            trp[:, 1536:2048]), ("t", t_ct))

            B.mark()
            if layer == 1:
                emit_l1_ctx_quant()
                osrc = xln
            else:
                osrc = ctx_all

            for oc in range(KC):
                wob = wqb[0] if oc % 2 == 0 else wkb[0]
                syncstep(lambda e, oc=oc, wob=wob: load_whead(e, 3, oc, wob),
                         allt())
                B.inc("ld", 16)
                ld_o = B.val("ld")

                def tom(t, oc=oc, ld_o=ld_o, osrc=osrc, wob=wob):
                    wt(t, "ld", ld_o)
                    dst = pja if (oc % 2 == 0) else pjb
                    for kcc in range(KC):
                        for half in range(2):
                            sl = slice(half * 512, (half + 1) * 512)
                            t.matmul(dst[:, sl], wob[:, kcc * 128:(kcc + 1) * 128],
                                     osrc[:, kcc * T + half * 512:kcc * T + (half + 1) * 512],
                                     start=(kcc == 0), stop=(kcc == KC - 1))
                t_o = tstep(tom, allv(), alls())

                if layer == 0:
                    vstep(lambda v, oc=oc: v.scalar_tensor_tensor(
                        x[:, oc * T:(oc + 1) * T],
                        (pja if oc % 2 == 0 else pjb)[:, :], 1.0,
                        x[:, oc * T:(oc + 1) * T], OP.mult, OP.add), ("t", t_o))
                else:
                    vstep(lambda v, oc=oc: (
                        v.tensor_tensor(tmp1[:, :],
                                        (pja if oc % 2 == 0 else pjb)[:, :],
                                        rep_b[:, :], OP.mult),
                        v.tensor_tensor(x[:, oc * T:(oc + 1) * T],
                                        x[:, oc * T:(oc + 1) * T], tmp1[:, :], OP.add)),
                        ("t", t_o))

        def emit_l0_ffn():
            l0ATb = Vtm
            l0Bb = Pbf
            gw_b = Ptrb
            gdma(lambda g: g.dma_start(out=l0ATb.rearrange("p (kc o) -> p kc o", o=128),
                                       in_=l0f_AT.rearrange("(kc p) o -> p kc o", p=128)),
                 allt(), alls())
            gdma(lambda g: g.dma_start(out=l0Bb[:, 0:D], in_=l0f_B))
            gdma(lambda g: g.dma_start(
                out=gw_b[:, 0:KC * 8].rearrange("p (kc o) -> p kc o", o=8),
                in_=l0f_gwT.rearrange("(kc p) o -> p kc o", p=128)))
            gdma(lambda g: g.dma_start(out=biasrow[:, :], in_=l0f_b_h))
            gdma(lambda g: g.dma_start(out=gbrow[:, :], in_=l0f_gb_h))
            gd_w = B.val("gd")

            def tgl(t, gd_w=gd_w):
                wt(t, "gd", gd_w)
                for kcc in range(KC):
                    for half, bank in ((0, smp), (1, spx)):
                        bankap = bank[0:8, 0:512]
                        t.matmul(bankap, gw_b[:, kcc * 8:(kcc + 1) * 8],
                                 xln[:, kcc * T + half * 512:kcc * T + (half + 1) * 512],
                                 start=(kcc == 0), stop=False)
                for half, bank in ((0, smp), (1, spx)):
                    t.matmul(bank[0:8, 0:512], gbrow[0:1, 0:8],
                             onesrow[0:1, 0:512], start=False, stop=True)
            t_gl = tstep(tgl, allv())
            v_gl = vstep(lambda v: (v.tensor_copy(tmp3[0:8, 0:512], smp[0:8, 0:512]),
                                    v.tensor_copy(tmp3[0:8, 512:1024], spx[0:8, 0:512])),
                         ("t", t_gl))

            for blk in range(NQ):
                t_tr = tstep(lambda t, blk=blk: t.transpose(
                    smp[:, 0:128], tmp3[0:8, blk * 128:(blk + 1) * 128] if False else
                    tmp3[0:128, blk * 128:(blk + 1) * 128], identf[:, :]),
                    ("v", B.val("v")))

                def vtop(v, blk=blk):
                    v.tensor_copy(gtok[:, 0:8], smp[:, 0:8])
                    v.reduce_max(gm1[:, :], gtok[:, 0:8], AX.X)
                    v.tensor_tensor(gtk2[:, 0:8], gtok[:, 0:8],
                                    gm1[:, :].broadcast_to([128, 8]), OP.is_equal)
                    v.scalar_tensor_tensor(gtk3[:, 0:8], gtk2[:, 0:8], -1e30,
                                           gtok[:, 0:8], OP.mult, OP.add)
                    v.reduce_max(gm2[:, :], gtk3[:, 0:8], AX.X)
                v_t1 = vstep(vtop, ("t", t_tr))
                v_t2 = vstep(lambda v: v.tensor_scalar(gm1[:, :], gm1[:, :], -1.0,
                                                       None, OP.mult), ("v", v_t1))
                s_ex2 = sstep(lambda s: s.activation(gtk2[:, 0:8], gtok[:, 0:8], AF.Exp,
                                                     bias=gm1[:, :], scale=1.0),
                              ("v", v_t2))

                def vw(v, blk=blk):
                    v.tensor_tensor(gtk3[:, 0:8], gtok[:, 0:8],
                                    gm2[:, :].broadcast_to([128, 8]), OP.is_ge)
                    v.tensor_tensor(gtk2[:, 0:8], gtk2[:, 0:8], gtk3[:, 0:8], OP.mult)
                    v.reduce_sum(gden[:, :], gtk2[:, 0:8], AX.X)
                    v.reciprocal(gden[:, :], gden[:, :])
                    v.tensor_tensor(gtok[:, 0:8], gtk2[:, 0:8], gden[:, :].broadcast_to([128, 8]), OP.mult)
                v_t3 = vstep(vw, ("s", s_ex2))
                t_tb = tstep(lambda t: t.transpose(smp[:, 128:256], gtok[:, 0:128],
                                                   identf[:, :]), ("v", v_t3))
                v_tb = vstep(lambda v, blk=blk: v.tensor_copy(
                    wselb[0:8, blk * 128:(blk + 1) * 128], smp[0:8, 128:256]), ("t", t_tb))

            t_w16 = tstep(lambda t: [t.matmul((pja if False else smp)[:, 0:512] if False
                                              else pja[:, half * 512:(half + 1) * 512],
                                              e16[:, :], wselb[0:8, half * 512:(half + 1) * 512],
                                              start=True, stop=True) for half in range(2)],
                          ("v", B.val("v")))
            v_w16 = vstep(lambda v: v.tensor_copy(wrepf[:, :], pja[:, :]), ("t", t_w16))

            def txa(t):
                for kcc in range(KC):
                    for half in range(2):
                        sl = slice(half * 512, (half + 1) * 512)
                        t.matmul(pjb[:, sl], l0ATb[:, kcc * 128:(kcc + 1) * 128],
                                 xln[:, kcc * T + half * 512:kcc * T + (half + 1) * 512],
                                 start=(kcc == 0), stop=(kcc == KC - 1))
            t_xa = tstep(txa, ("v", v_w16))
            v_xa = vstep(lambda v: (v.tensor_tensor(tmp1[:, :], pjb[:, :], wrepf[:, :],
                                                    OP.mult),
                                    v.tensor_copy(qh_bf[:, :], tmp1[:, :])), ("t", t_xa))

            wfb = [Ksb, Vsb]
            for oc in range(KC):
                gdma(lambda g, oc=oc: g.dma_start(
                    out=wfb[oc % 2][:, 0:D].rearrange("p (kc o) -> p kc o", o=128),
                    in_=l0fWT[:, oc * 128:(oc + 1) * 128].rearrange("(kc p) o -> p kc o", p=128)
                ), allt())
                gd_t = B.val("gd")

                def tb(t, oc=oc, gd_t=gd_t):
                    wt(t, "gd", gd_t)
                    dst = pja if oc % 2 == 0 else pjb
                    for kcc in range(KC):
                        for half in range(2):
                            sl = slice(half * 512, (half + 1) * 512)
                            t.matmul(dst[:, sl], wfb[oc % 2][:, kcc * 128:(kcc + 1) * 128],
                                     xln[:, kcc * T + half * 512:kcc * T + (half + 1) * 512],
                                     start=(kcc == 0), stop=False)
                    for half in range(2):
                        sl = slice(half * 512, (half + 1) * 512)
                        t.matmul(dst[:, sl], biasrow[0:1, oc * 128:(oc + 1) * 128],
                                 onesrow[0:1, 0:512], start=False, stop=False)
                        t.matmul(dst[:, sl], l0Bb[:, oc * 128:(oc + 1) * 128],
                                 qh_bf[:, sl], start=False, stop=True)
                t_b = tstep(tb, ("v", v_xa) if oc == 0 else allv())
                vstep(lambda v, oc=oc: v.scalar_tensor_tensor(
                    x[:, oc * T:(oc + 1) * T], (pja if oc % 2 == 0 else pjb)[:, :],
                    1.0, x[:, oc * T:(oc + 1) * T], OP.mult, OP.add), ("t", t_b))

        def emit_l1_ffn():
            gdma(lambda g: g.dma_start(
                out=rwbb.rearrange("p (kc o) -> p kc o", o=32),
                in_=l1f_rwT.rearrange("(kc p) o -> p kc o", p=128)),
                allt(), alls())
            gd_t = B.val("gd")

            def trt(t, gd_t=gd_t):
                wt(t, "gd", gd_t)
                for kcc in range(KC):
                    for half, bank in ((0, smp), (1, spx)):
                        t.matmul(bank[0:32, 0:512], rwbb[:, kcc * 32:(kcc + 1) * 32],
                                 xln[:, kcc * T + half * 512:kcc * T + (half + 1) * 512],
                                 start=(kcc == 0), stop=(kcc == KC - 1))
            t_r = tstep(trt, allv())
            v_r = vstep(lambda v: (v.tensor_copy(tmp3[0:32, 0:512], smp[0:32, 0:512]),
                                   v.tensor_copy(tmp3[0:32, 512:1024], spx[0:32, 0:512])),
                        ("t", t_r))

            for blk in range(NQ):
                t_tr = tstep(lambda t, blk=blk: t.transpose(
                    smp[:, 0:128], tmp3[0:128, blk * 128:(blk + 1) * 128], identf[:, :]),
                    ("v", B.val("v")))

                def vtop(v, blk=blk):
                    v.tensor_copy(gtok[:, 0:32], smp[:, 0:32])
                    v.reduce_max(gm1[:, :], gtok[:, 0:32], AX.X)
                    v.tensor_copy(gtk3[:, 0:32], gtok[:, 0:32])
                v_t1 = vstep(vtop, ("t", t_tr))
                for it in range(7):
                    def vit(v, it=it):
                        v.reduce_max(gm2[:, :], gtk3[:, 0:32], AX.X)
                        v.tensor_tensor(gtk2[:, 0:32], gtk3[:, 0:32],
                                        gm2[:, :].broadcast_to([128, 32]), OP.is_equal)
                        v.scalar_tensor_tensor(gtk3[:, 0:32], gtk2[:, 0:32], -1e30,
                                               gtk3[:, 0:32], OP.mult, OP.add)
                    vstep(vit)
                v_m8 = vstep(lambda v: v.reduce_max(gm2[:, :], gtk3[:, 0:32], AX.X))
                v_neg = vstep(lambda v: v.tensor_scalar(gm1[:, :], gm1[:, :], -1.0,
                                                        None, OP.mult))
                s_ex = sstep(lambda s: s.activation(gtk2[:, 0:32], gtok[:, 0:32], AF.Exp,
                                                    bias=gm1[:, :], scale=1.0),
                             ("v", B.val("v")))

                def vw(v, blk=blk):
                    v.tensor_tensor(gtk3[:, 0:32], gtok[:, 0:32],
                                    gm2[:, :].broadcast_to([128, 32]), OP.is_ge)
                    v.tensor_tensor(gtk2[:, 0:32], gtk2[:, 0:32], gtk3[:, 0:32], OP.mult)
                    v.reduce_sum(gden[:, :], gtk2[:, 0:32], AX.X)
                    v.reciprocal(gden[:, :], gden[:, :])
                    v.tensor_tensor(gtok[:, 0:32], gtk2[:, 0:32], gden[:, :].broadcast_to([128, 32]), OP.mult)
                v_t3 = vstep(vw, ("s", s_ex))
                t_tb = tstep(lambda t: t.transpose(smp[:, 128:256], gtok[:, 0:128],
                                                   identf[:, :]), ("v", v_t3))
                vstep(lambda v, blk=blk: v.tensor_copy(
                    wselb[0:32, blk * 128:(blk + 1) * 128], smp[0:32, 128:256]),
                    ("t", t_tb))

            piece_idx = [0]

            def load_piece(src_ap, dst, *waits):
                return gdma(lambda g, src_ap=src_ap, dst=dst:
                            g.dma_start(out=dst, in_=src_ap), *waits)

            def piece_srcs(e):
                return [
                    (l1f_wg[e, 0:1024, :].rearrange("(kc p) o -> p kc o", p=128),
                     ring, "wg0"),
                    (l1f_wg[e, 1024:2048, :].rearrange("(kc p) o -> p kc o", p=128),
                     ring, "wg1"),
                    (l1f_wu[e, 0:1024, :].rearrange("(kc p) o -> p kc o", p=128),
                     ring, "wu0"),
                    (l1f_wu[e, 1024:2048, :].rearrange("(kc p) o -> p kc o", p=128),
                     ring, "wu1"),
                    (l1f_wd[e, :, 0:1024].rearrange("(k6 p) o -> p k6 o", p=128),
                     ring, "wdA"),
                    (l1f_wd[e, :, 1024:2048].rearrange("(k6 p) o -> p k6 o", p=128),
                     ring, "wdB"),
                ]

            def slot(e, i):
                return (e * 6 + i) % 4

            gd_piece = {}
            for i in range(4):
                src, _, _ = piece_srcs(0)[i]
                dstap = ring[slot(0, i)].rearrange(
                    "p (a b) -> p a b", a=8 if i < 4 else 6)
                gd_piece[(0, i)] = load_piece(src, dstap, allt(), allv())

            for e in range(32):
                syncstep(lambda ee, e=e: ee.dma_start(
                    out=selbuf[:, :], in_=sel32_h[e, :].rearrange("(r n) -> r n", n=128)
                ).then_inc(s_ld, 16), allt())
                B.inc("ld", 16)
                ld_sel = B.val("ld")
                t_wr = tstep(lambda t, e=e, ld_sel=ld_sel: (
                    wt(t, "ld", ld_sel),
                    [t.matmul(pjb[:, half * 512:(half + 1) * 512], selbuf[:, :],
                              wselb[0:32, half * 512:(half + 1) * 512],
                              start=True, stop=True) for half in range(2)]), allv())
                v_wr = vstep(lambda v: v.tensor_copy(wrepf[:, :], pjb[:, :]), ("t", t_wr))

                last_si = 0
                last_h = v_wr
                for oc6 in range(6):
                    def tg(t, e=e, oc6=oc6):
                        wt(t, "gd", max(gd_piece[(e, 0)], gd_piece[(e, 1)]))
                        for kcc in range(KC):
                            piece = 0 if kcc < 8 else 1
                            rb = ring[slot(e, piece)]
                            kloc = kcc % 8
                            for half in range(2):
                                sl = slice(half * 512, (half + 1) * 512)
                                t.matmul(pja[:, sl],
                                         rb[:, kloc * 768 + oc6 * 128:kloc * 768 + (oc6 + 1) * 128],
                                         xln[:, kcc * T + half * 512:kcc * T + (half + 1) * 512],
                                         start=(kcc == 0), stop=(kcc == KC - 1))
                    t_g = tstep(tg, allv() if oc6 == 0 else ("s", last_si))
                    s_si = sstep(lambda s, oc6=oc6: s.activation(tmp3[:, :], pja[:, :],
                                                                 AF.Silu), ("t", t_g))
                    last_si = s_si
                    def tu(t, e=e, oc6=oc6):
                        wt(t, "gd", max(gd_piece[(e, 2)], gd_piece[(e, 3)]))
                        for kcc in range(KC):
                            piece = 2 if kcc < 8 else 3
                            rb = ring[slot(e, piece)]
                            kloc = kcc % 8
                            for half in range(2):
                                sl = slice(half * 512, (half + 1) * 512)
                                t.matmul(pjb[:, sl],
                                         rb[:, kloc * 768 + oc6 * 128:kloc * 768 + (oc6 + 1) * 128],
                                         xln[:, kcc * T + half * 512:kcc * T + (half + 1) * 512],
                                         start=(kcc == 0), stop=(kcc == KC - 1))
                    t_u = tstep(tu, ("v", last_h))
                    last_h = vstep(lambda v, oc6=oc6: (
                        v.tensor_tensor(tmp1[:, :], tmp3[:, :], pjb[:, :], OP.mult),
                        v.tensor_tensor(tmp2[:, :], tmp1[:, :], wrepf[:, :], OP.mult),
                        v.tensor_copy(hb[:, oc6 * T:(oc6 + 1) * T], tmp2[:, :])),
                        ("t", t_u), ("s", s_si))

                t_after_g = B.val("t")
                for i in (4, 5):
                    src, _, _ = piece_srcs(e)[i]
                    dstap = ring[slot(e, i)].rearrange("p (a b) -> p a b", a=6)
                    gd_piece[(e, i)] = load_piece(src, dstap, ("t", t_after_g))

                t_after_u = B.val("t")
                if e + 1 < 32:
                    for i in (0, 1):
                        src, _, _ = piece_srcs(e + 1)[i]
                        dstap = ring[slot(e + 1, i)].rearrange(
                            "p (a b) -> p a b", a=8)
                        gd_piece[(e + 1, i)] = load_piece(src, dstap, ("t", t_after_u))

                ev_hist = [last_h, last_h]
                for dm in range(KC):
                    def td(t, e=e, dm=dm):
                        wt(t, "gd", gd_piece[(e, 4 if dm < 8 else 5)])
                        for k6 in range(6):
                            piece = 4 if dm < 8 else 5
                            rb = ring[slot(e, piece)]
                            dloc = dm % 8
                            for half in range(2):
                                sl = slice(half * 512, (half + 1) * 512)
                                t.matmul((pja if dm % 2 == 0 else pjb)[:, sl],
                                         rb[:, k6 * 1024 + dloc * 128:k6 * 1024 + (dloc + 1) * 128],
                                         hb[:, k6 * T + half * 512:k6 * T + (half + 1) * 512],
                                         start=(k6 == 0), stop=(k6 == 5))
                    wdep = ("v", ev_hist[-2]) if dm >= 2 else ("v", last_h)
                    t_d = tstep(td, wdep)
                    ev = vstep(lambda v, dm=dm: v.scalar_tensor_tensor(
                        x[:, dm * T:(dm + 1) * T],
                        (pja if dm % 2 == 0 else pjb)[:, :], 1.0,
                        x[:, dm * T:(dm + 1) * T], OP.mult, OP.add), ("t", t_d))
                    ev_hist.append(ev)
                    if e + 1 < 32:
                        if dm == 7:
                            src, _, _ = piece_srcs(e + 1)[2]
                            dstap = ring[slot(e + 1, 2)].rearrange(
                                "p (a b) -> p a b", a=8)
                            gd_piece[(e + 1, 2)] = load_piece(src, dstap,
                                                             ("t", B.val("t")))
                        if dm == KC - 1:
                            src, _, _ = piece_srcs(e + 1)[3]
                            dstap = ring[slot(e + 1, 3)].rearrange(
                                "p (a b) -> p a b", a=8)
                            gd_piece[(e + 1, 3)] = load_piece(src, dstap,
                                                             ("t", B.val("t")))

        try:
            B.mark()
            emit_ln_stats()
            B.mark()
            emit_ln_apply()
            B.mark()
            if NBLOCK >= 1:
                emit_attention(0)
            if NBLOCK >= 2:
                emit_ln_stats()
                emit_ln_apply()
                emit_l0_ffn()
            if NBLOCK >= 3:
                emit_ln_stats()
                emit_l1_input_quant()
                emit_attention(1)
            if NBLOCK >= 4:
                emit_ln_stats()
                emit_ln_apply()
                emit_l1_ffn()
        except StopEmission:
            pass

        if DEVDUMP:
            def dumps(e):
                import contextlib
                ctxmgr = nc.allow_non_contiguous_dma(reason="debug dumps")
                ctxmgr.__enter__()
                e.dma_start(out=dbg_ctx, in_=ctx_all).then_inc(s_st, 16)
                e.dma_start(out=dbg_k, in_=Ksb).then_inc(s_st, 16)
                e.dma_start(out=dbg_v, in_=Vsb).then_inc(s_st, 16)
                e.dma_start(out=dbg_xln, in_=xln[:, :]).then_inc(s_st, 16)
                e.dma_start(out=dbg_cols[:, 0:8], in_=lcol[:, :]).then_inc(s_st, 16)
                e.dma_start(out=dbg_cols[:, 8:16], in_=bcol[:, :]).then_inc(s_st, 16)
                e.dma_start(out=dbg_cols[:, 16:24], in_=qncol[:, :]).then_inc(s_st, 16)
                e.dma_start(out=dbg_cols[:, 24:25], in_=k2col[:, :]).then_inc(s_st, 16)
                e.dma_start(out=dbg_cols[:, 25:33], in_=rlcol[:, :]).then_inc(s_st, 16)
                e.dma_start(out=dbg_cols[:, 33:34], in_=mean_rep[:, 0:1]).then_inc(s_st, 16)
                e.dma_start(out=dbg_cols[:, 34:35], in_=rstd_rep[:, 0:1]).then_inc(s_st, 16)
                e.dma_start(out=dbg_kvin, in_=kv_in).then_inc(s_st, 16)
                e.dma_start(out=dbg_p0in, in_=prep0_in).then_inc(s_st, 16)
                e.dma_start(out=dbg_p0out, in_=prep0_out[0:2]).then_inc(s_st, 16)
            B.inc("st", 16 * 14)
            B.emit("sync", W(dumps := dumps, ) if False else
                   (lambda e, f=dumps: ([wt(e, sn, vv) for sn, vv in
                                         (allv(), allt(), alls(), allgd())], f(e))))
        syncstep(lambda e: e.dma_start(
            out=out_fm.rearrange("(kc p) t -> p kc t", p=128),
            in_=x[:, :].rearrange("p (kc t) -> p kc t", t=T)).then_inc(s_st, 16),
            allv(), allt(), alls(), allgd())
        B.inc("st", 16)
        syncstep(lambda e: None, ("st", B.val("st")))

        block = ctx.enter_context(nc.Block())

        @block.sync
        def _(sync):
            for fn in B.q["sync"]:
                fn(sync)

        @block.gpsimd
        def _(g):
            for fn in B.q["gpsimd"]:
                fn(g)

        @block.vector
        def _(v):
            for fn in B.q["vector"]:
                fn(v)

        @block.scalar
        def _(s):
            for fn in B.q["scalar"]:
                fn(s)

        @block.tensor
        def _(t):
            for fn in B.q["tensor"]:
                fn(t)

    return nc



def _host_constants():
    ones = np.ones((128, 128), dtype=ml_dtypes.bfloat16)
    ident = np.eye(128, dtype=ml_dtypes.bfloat16)
    identf = np.eye(128, dtype=np.float32)
    J = np.zeros((128, 128), dtype=np.float32)
    J[np.arange(64), np.arange(64) + 64] = -1.0
    J[np.arange(64) + 64, np.arange(64)] = 1.0
    jrotT = J.T.astype(ml_dtypes.bfloat16)
    e16 = np.zeros((8, 128), dtype=ml_dtypes.bfloat16)
    for e in range(8):
        e16[e, e * 16:(e + 1) * 16] = 1.0
    sel8 = np.zeros((8, 8 * 128), dtype=ml_dtypes.bfloat16)
    for j in range(8):
        sel8[j, j * 128:(j + 1) * 128] = 1.0
    sel32 = np.zeros((32, 32 * 128), dtype=ml_dtypes.bfloat16)
    for e in range(32):
        sel32[e, e * 128:(e + 1) * 128] = 1.0
    tri = np.where(np.arange(128)[None, :] <= np.arange(128)[:, None], 0.0, -1e30
                   ).astype(np.float32)
    full = np.full((128, 128), -1e30, dtype=np.float32)
    zero = np.zeros((128, 128), dtype=np.float32)
    return ones, ident, identf, jrotT, e16, sel8, sel32, tri, full, zero


def _rope_tables_stacked():
    half = 64
    inv = (1.0 / 10000.0 ** (np.arange(half, dtype=np.float64) / half))
    tpos = np.arange(SEQ, dtype=np.float64)
    fr = np.outer(tpos, inv)
    cos = np.cos(fr).astype(np.float32)
    sin = np.sin(fr).astype(np.float32)
    c2 = np.concatenate([cos, cos], axis=1).T.copy()
    s2 = np.concatenate([sin, sin], axis=1).T.copy()
    return c2, s2


_NC_CACHE = {}


def _get_nc():
    key = NBLOCK
    if key not in _NC_CACHE:
        _NC_CACHE[key] = build_program()
    return _NC_CACHE[key]


def kernel(**inputs):
    f32 = np.float32
    xfull = np.ascontiguousarray(inputs["L0_attn_x"], dtype=f32)
    ones, ident, identf, jrotT, e16, sel8, sel32, tri, full, zero = _host_constants()
    c2, s2 = _rope_tables_stacked()

    wmap = {
        "l0q": inputs["L0_attn_w_q"], "l0k": inputs["L0_attn_w_k"],
        "l0v": inputs["L0_attn_w_v"], "l0o": inputs["L0_attn_w_o"],
        "l1q": inputs["L1_attn_w_q"], "l1k": inputs["L1_attn_w_k"],
        "l1v": inputs["L1_attn_w_v"], "l1o": inputs["L1_attn_w_o"],
    }
    l0fWT = np.ascontiguousarray(inputs["L0_ffn_W"].T, dtype=f32)
    l0f_b = np.ascontiguousarray(inputs["L0_ffn_b"], dtype=f32).reshape(1, D)
    l0f_AT = np.ascontiguousarray(
        inputs["L0_ffn_lora_A"].reshape(128, D).T, dtype=f32)
    l0f_B = np.ascontiguousarray(
        np.swapaxes(inputs["L0_ffn_lora_B"], 1, 2).reshape(128, D), dtype=f32)
    l0f_gwT = np.ascontiguousarray(inputs["L0_ffn_gate_w"].T, dtype=f32)
    l0f_gb = np.ascontiguousarray(inputs["L0_ffn_gate_b"], dtype=f32).reshape(1, 8)
    l1f_rwT = np.ascontiguousarray(inputs["L1_ffn_router_w"].T, dtype=f32)
    l1f_wg = np.ascontiguousarray(inputs["L1_ffn_w_gate"], dtype=f32)
    l1f_wu = np.ascontiguousarray(inputs["L1_ffn_w_up"], dtype=f32)
    l1f_wd = np.ascontiguousarray(inputs["L1_ffn_w_down"], dtype=f32)

    in_maps = []
    for c in range(NCORE):
        b, p = c // 2, c % 2
        gtiles = [2 * j + p for j in range(NQ)]
        cols = np.concatenate([np.arange(g * 128, (g + 1) * 128) for g in gtiles])
        xs = np.ascontiguousarray(xfull[b].T[:, cols])
        im = {
            "x_fm": xs,
            "cos_own": np.ascontiguousarray(c2[:, cols]),
            "sin_own": np.ascontiguousarray(s2[:, cols]),
            "maskA": tri if p == 0 else zero,
            "maskB": full if p == 0 else tri,
            "ones128": ones, "ident": ident, "identf": identf,
            "jrot": jrotT, "e16": e16, "sel8": sel8, "sel32": sel32,
            "l0fWT": l0fWT, "l0f_b": l0f_b, "l0f_AT": l0f_AT, "l0f_B": l0f_B,
            "l0f_gwT": l0f_gwT, "l0f_gb": l0f_gb, "l1f_rwT": l1f_rwT,
            "l1f_wg": l1f_wg, "l1f_wu": l1f_wu, "l1f_wd": l1f_wd,
        }
        for nm, w in wmap.items():
            im[nm + "_s"] = np.ascontiguousarray(
                np.asarray(w, dtype=f32)[c * 256:(c + 1) * 256, :])
        in_maps.append(im)

    nc = _get_nc()
    res = run_bass_kernel_spmd(nc, in_maps, core_ids=list(range(NCORE)))

    out = np.empty((4, SEQ, D), dtype=f32)
    for c in range(NCORE):
        b, p = c // 2, c % 2
        gtiles = [2 * j + p for j in range(NQ)]
        cols = np.concatenate([np.arange(g * 128, (g + 1) * 128) for g in gtiles])
        of = np.asarray(res.results[c]["out_fm"])
        out[b, cols, :] = of.T
    return out


# version 08 (final)
# speedup vs baseline: 1.0679x; 1.0078x over previous
import math
from contextlib import ExitStack

import numpy as np
import ml_dtypes

import concourse.bass as bass
import concourse.mybir as mybir
from concourse.bass_utils import run_bass_kernel_spmd

F32 = mybir.dt.float32
BF16 = mybir.dt.bfloat16
I32 = mybir.dt.int32
AF = mybir.ActivationFunctionType
OP = mybir.AluOpType
AX = mybir.AxisListType

D = 2048
T = 1024
NH = 16
HD = 128
KC = D // 128
NQ = T // 128
SEQ = 2048
NKT = SEQ // 128
NCORE = 8
NBLOCK = 4
PHASES = 99
DEVDUMP = False

EPS_LN = 1e-5
EPS_L0Q = 1e-8
EPS_L1 = 1e-5
QB = 127.0
ISQD = 1.0 / math.sqrt(128.0)

ENGINES = ("sync", "gpsimd", "vector", "scalar", "tensor")


class StopEmission(Exception):
    pass


class Build:
    def __init__(self):
        self.q = {e: [] for e in ENGINES}
        self.ctr = {}
        self.phase = 0
        self.stopped = False

    def mark(self):
        self.phase += 1
        if self.phase >= PHASES:
            self.stopped = True

    def go(self):
        return not self.stopped

    def inc(self, sem, by=1):
        if not self.stopped:
            self.ctr[sem] = self.ctr.get(sem, 0) + by
        return self.ctr.get(sem, 0)

    def val(self, sem):
        return self.ctr.get(sem, 0)

    def emit(self, eng, fn):
        if not self.stopped:
            self.q[eng].append(fn)


def build_program():
    nc = bass.Bass("TRN2", target_bir_lowering=False, debug=False,
                   detect_race_conditions=False)

    di = lambda n, s, dt=F32: nc.dram_tensor(n, s, dt, kind="ExternalInput").ap()
    x_fm = di("x_fm", [D, T])
    cos_own_h = di("cos_own", [128, T])
    sin_own_h = di("sin_own", [128, T])
    maskA_h = di("maskA", [128, 128])
    maskB_h = di("maskB", [128, 128])
    ones_h = di("ones128", [128, 128], BF16)
    ident_h = di("ident", [128, 128], BF16)
    identf_h = di("identf", [128, 128], F32)
    jrot_h = di("jrot", [128, 128], BF16)
    e16_h = di("e16", [8, 128], BF16)
    sel8_h = di("sel8", [8, 8 * 128], BF16)
    sel32_h = di("sel32", [32, 32 * 128], BF16)
    wsl = {nm: di(nm + "_s", [256, D]) for nm in
           ("l0q", "l0k", "l0v", "l0o", "l1q", "l1k", "l1v", "l1o")}
    l0fWT = di("l0fWT", [D, D])
    l0f_b_h = di("l0f_b", [1, D])
    l0f_AT = di("l0f_AT", [D, 128])
    l0f_B = di("l0f_B", [128, D])
    l0f_gwT = di("l0f_gwT", [D, 8])
    l0f_gb_h = di("l0f_gb", [1, 8])
    l1f_rwT = di("l1f_rwT", [D, 32])
    l1f_wg = di("l1f_wg", [32, D, 768])
    l1f_wu = di("l1f_wu", [32, D, 768])
    l1f_wd = di("l1f_wd", [32, 768, D])
    out_fm = nc.dram_tensor("out_fm", [D, T], F32, kind="ExternalOutput").ap()
    if DEVDUMP:
        dbg_ctx = nc.dram_tensor("dbg_ctx", [128, NH * T], BF16, kind="ExternalOutput").ap()
        dbg_k = nc.dram_tensor("dbg_k", [128, SEQ], BF16, kind="ExternalOutput").ap()
        dbg_v = nc.dram_tensor("dbg_v", [128, SEQ], BF16, kind="ExternalOutput").ap()
        dbg_xln = nc.dram_tensor("dbg_xln", [128, KC * T], BF16, kind="ExternalOutput").ap()
        dbg_cols = nc.dram_tensor("dbg_cols", [128, 64], F32, kind="ExternalOutput").ap()
        dbg_kvin = nc.dram_tensor("dbg_kvin", [2, NH, HD, T], BF16, kind="ExternalOutput").ap()
        dbg_p0in = nc.dram_tensor("dbg_p0in", [4, D, 256], BF16, kind="ExternalOutput").ap()
        dbg_p0out = nc.dram_tensor("dbg_p0out", [2, 4, D, 256], BF16, kind="ExternalOutput").ap()

    prep0_in = nc.dram_tensor("prep0_in", [4, D, 256], BF16).ap()
    prep0_out = nc.dram_tensor("prep0_out", [NCORE, 4, D, 256], BF16).ap()
    prep1_in = nc.dram_tensor("prep1_in", [4, D, 256], BF16).ap()
    prep1_out = nc.dram_tensor("prep1_out", [NCORE, 4, D, 256], BF16).ap()
    sums_in = nc.dram_tensor("sums_in", [4, 1], F32).ap()
    sums_out = nc.dram_tensor("sums_out", [NCORE * 4, 1], F32).ap()
    kv_in = nc.dram_tensor("kv_in", [NH, 2, HD, T], BF16).ap()
    kv_outA = nc.dram_tensor("kv_outA", [2, 8, 2, HD, T], BF16).ap()
    kv_outB = nc.dram_tensor("kv_outB", [2, 8, 2, HD, T], BF16).ap()
    SHARD = 2 * NH * HD * T

    B = Build()
    ALL8 = [list(range(NCORE))]

    with ExitStack() as ctx:
        sbt = lambda n, s, dt=F32: ctx.enter_context(nc.sbuf_tensor(n, s, dt))
        pst = lambda n, s, dt=F32: ctx.enter_context(nc.psum_tensor(n, s, dt))
        sem = lambda n: ctx.enter_context(nc.semaphore(n))

        x = sbt("x", [128, KC * T])
        xln = sbt("xln", [128, KC * T], BF16)
        mean_rep = sbt("mean_rep", [128, T])
        rstd_rep = sbt("rstd_rep", [128, T])
        rep_a = sbt("rep_a", [128, T])
        rep_b = sbt("rep_b", [128, T])
        tmp1 = sbt("tmp1", [128, T])
        tmp2 = sbt("tmp2", [128, T])
        tmp3 = sbt("tmp3", [128, T])
        cosb = sbt("cosb", [128, T])
        sinb = sbt("sinb", [128, T])
        ones_bf = sbt("ones_bfb", [128, 128], BF16)
        ident = sbt("identb", [128, 128], BF16)
        identf = sbt("identfb", [128, 128])
        jrot = sbt("jrotb", [128, 128], BF16)
        e16 = sbt("e16b", [8, 128], BF16)
        selbuf = sbt("selbuf", [32, 128], BF16)
        mA = sbt("mA", [128, 128])
        mB = sbt("mB", [128, 128])
        fone = sbt("fone", [128, 1])
        onesrow = sbt("onesrow", [1, 512], BF16)
        qncol = sbt("qncol", [128, NQ])
        k2col = sbt("k2col", [128, 1])
        bcol = sbt("bcol", [128, NQ])
        lcol = sbt("lcol", [128, NQ])
        rlcol = sbt("rlcol", [128, NQ])
        gcolM = sbt("gcolM", [128, NQ])
        gcolR = sbt("gcolR", [128, NQ])
        gcolA = sbt("gcolA", [128, NQ])
        gcolB = sbt("gcolB", [128, NQ])
        brow = sbt("brow", [8, 256], BF16)
        sums8 = sbt("sums8", [32, 1])
        scol = sbt("scol", [128, 4])
        thcol = sbt("thcol", [128, 4])
        gtok = sbt("gtok", [128, 128])
        gtk2 = sbt("gtk2", [128, 32])
        gtk3 = sbt("gtk3", [128, 32])
        gm1 = sbt("gm1", [128, 1])
        gm2 = sbt("gm2", [128, 1])
        gden = sbt("gden", [128, 1])
        biasrow = sbt("biasrow", [1, D], BF16)
        gbrow = sbt("gbrow", [1, 8], BF16)
        wselb = sbt("wselb", [32, T], BF16)
        wrepf = sbt("wrepf", [128, T])

        tint = tmp2.ap().bitcast(I32)[:, 0:T]
        Mmax = tmp3

        s_ld = sem("s_ld")
        s_gd = sem("s_gd")
        s_cc = sem("s_cc")
        s_t = sem("s_t")
        s_v = sem("s_v")
        s_s = sem("s_s")
        s_st = sem("s_st")
        SEMS = {"ld": s_ld, "gd": s_gd, "cc": s_cc, "t": s_t, "v": s_v,
                "s": s_s, "st": s_st}

        def wt(e, semname, val):
            if val > 0:
                e.wait_ge(SEMS[semname], val)

        def dma(eng, out, in_, semname="ld", qsem=None):
            tgt = B.inc(semname, 16)

            def f(e, out=out, in_=in_, tgt=tgt):
                e.dma_start(out=out, in_=in_).then_inc(SEMS[semname], 16)
            B.emit(eng, f)
            return tgt

        s_vx = sem("s_vx")
        s_sx = sem("s_sx")
        s_gx = sem("s_gx")
        AUX = {"vector": s_vx, "scalar": s_sx, "gpsimd": s_gx}

        class _Rec:

            def __init__(self, eng, aux, ctr):
                self._e = eng
                self._aux = aux
                self._ctr = ctr
                self.last = None

            def __getattr__(self, name):
                a = getattr(self._e, name)
                if callable(a):
                    def call(*args, _a=a, _n=name, **kw):
                        if self._aux is not None and _n not in (
                                "wait_ge", "sem_inc", "nop", "partition_id",
                                "compute_val"):
                            if self._ctr[0] > 0:
                                self._e.wait_ge(self._aux, self._ctr[0])
                        r = _a(*args, **kw)
                        if hasattr(r, "then_inc"):
                            self.last = r
                            if self._aux is not None and _n not in (
                                    "wait_ge", "sem_inc", "nop"):
                                r.then_inc(self._aux, 1)
                                self._ctr[0] += 1
                        return r
                    return call
                return a

        AUXCTR = {"vector": [0], "scalar": [0], "gpsimd": [0]}

        def _mkstep(eng, semname, semh):
            aux = AUX.get(eng)
            actr = AUXCTR.get(eng)

            def step(fn, *waits):
                B.inc(semname)

                def f(e, fn=fn, waits=waits):
                    for sname, val in waits:
                        wt(e, sname, val)
                    rec = _Rec(e, aux, actr)
                    fn(rec)
                    assert rec.last is not None, "step emitted no instruction"
                    if aux is not None:
                        e.wait_ge(aux, actr[0])
                        e.sem_inc(semh, 1)
                    else:
                        rec.last.then_inc(semh, 1)
                B.emit(eng, f)
                return B.val(semname)
            return step

        vstep = _mkstep("vector", "v", s_v)
        sstep = _mkstep("scalar", "s", s_s)
        tstep = _mkstep("tensor", "t", s_t)
        gstep = _mkstep("gpsimd", "gd", s_gd)

        def gdma(fn, *waits):
            B.inc("gd", 16)

            def f(g, fn=fn, waits=waits):
                for sname, val in waits:
                    wt(g, sname, val)
                fn(g).then_inc(s_gd, 16)
            B.emit("gpsimd", f)
            return B.val("gd")

        def syncstep(fn, *waits):
            def f(e, fn=fn, waits=waits):
                for sname, val in waits:
                    wt(e, sname, val)
                fn(e)
            B.emit("sync", f)

        PAIRS = [[0, 1], [2, 3], [4, 5], [6, 7]]

        def collective(ins, outs, *waits, groups=None):
            tgt = B.inc("cc", 1)
            g_ = groups or ALL8

            def f(g, ins=ins, outs=outs, tgt=tgt, waits=waits, g_=g_):
                for sname, val in waits:
                    wt(g, sname, val)
                g.collective_compute("AllGather", OP.bypass, replica_groups=g_,
                                     ins=ins, outs=outs).then_inc(s_cc, 1)
            B.emit("gpsimd", f)
            return tgt

        def allv():
            return ("v", B.val("v"))

        def alls():
            return ("s", B.val("s"))

        def allt():
            return ("t", B.val("t"))

        def allld():
            return ("ld", B.val("ld"))

        def allgd():
            return ("gd", B.val("gd"))

        def allcc():
            return ("cc", B.val("cc"))

        pja = pst("pja", [128, T])
        pjb = pst("pjb", [128, T])
        trp = pst("trp", [128, 2048], BF16)
        smp = pst("smp", [128, 512])
        spx = pst("spx", [128, 512])

        for o, i in ((x[:, :].rearrange("p (kc t) -> p kc t", t=T),
                      x_fm.rearrange("(kc p) t -> p kc t", p=128)),
                     (cosb[:, :], cos_own_h), (sinb[:, :], sin_own_h),
                     (mA[:, :], maskA_h), (mB[:, :], maskB_h),
                     (ones_bf[:, :], ones_h), (ident[:, :], ident_h),
                     (identf[:, :], identf_h), (jrot[:, :], jrot_h),
                     (e16[:, :], e16_h)):
            dma("sync", o, i)
        LD_INIT = B.val("ld")
        vstep(lambda v: (v.memset(onesrow[:, :], 1.0), v.memset(fone[:, :], 1.0)))
        B.mark()


        with nc.sbuf_tensor("qsl0", [128, D], F32) as qsl0, \
             nc.sbuf_tensor("qsl1", [128, D], F32) as qsl1, \
             nc.sbuf_tensor("qc1", [128, D], F32) as qc1, \
             nc.sbuf_tensor("qtr", [128, D], BF16) as qtr, \
             nc.sbuf_tensor("qsc", [128, 16], F32) as qsc, \
             nc.sbuf_tensor("qth", [128, 16], F32) as qth:
            qsl = [qsl0, qsl1]

            def emit_prep(names, prep_in_ap, l1_mode):
                for wi, nm in enumerate(names):
                    for rt in range(2):
                        syncstep(lambda e, nm=nm, rt=rt:
                                 e.dma_start(out=qsl[rt][:, :],
                                             in_=wsl[nm][rt * 128:(rt + 1) * 128, :]
                                             ).then_inc(s_ld, 16), allv())
                        B.inc("ld", 16)
                        ld_t = B.val("ld")

                        def vq(v, rt=rt, wi=wi, l1_mode=l1_mode):
                            if l1_mode:
                                thr1 = thcol[:, wi:wi + 1]
                                scl1 = scol[:, wi:wi + 1]
                                thr = lambda g: thr1.broadcast_to([128, 128])
                                scl = lambda g: scl1.broadcast_to([128, 128])
                            else:
                                v.tensor_reduce(qsc[:, :],
                                                qsl[rt][:, :].rearrange("p (g n) -> p g n", g=16),
                                                AX.X, OP.add, apply_absolute_value=True)
                                v.tensor_scalar(qsc[:, :], qsc[:, :], 1.0 / 128, EPS_L0Q,
                                                OP.mult, OP.max)
                                v.tensor_scalar(qth[:, :], qsc[:, :], 0.5, None, OP.mult)
                                thr = lambda g: qth[:, g:g + 1].broadcast_to([128, 128])
                                scl = lambda g: qsc[:, g:g + 1].broadcast_to([128, 128])
                            for g in range(16):
                                sl = slice(g * 128, (g + 1) * 128)
                                v.tensor_tensor(qc1[:, sl], qsl[rt][:, sl], thr(g), OP.is_ge)
                            v.tensor_scalar(qsl[rt][:, :], qsl[rt][:, :], -1.0, None, OP.mult)
                            for g in range(16):
                                sl = slice(g * 128, (g + 1) * 128)
                                v.tensor_tensor(qsl[rt][:, sl], qsl[rt][:, sl], thr(g), OP.is_le)
                            v.tensor_tensor(qc1[:, :], qc1[:, :], qsl[rt][:, :], OP.add)
                            for g in range(16):
                                sl = slice(g * 128, (g + 1) * 128)
                                v.scalar_tensor_tensor(qtr[:, sl], qc1[:, sl], -1.0, scl(g),
                                                       OP.add, OP.mult)
                        v_t = vstep(vq, ("ld", ld_t), alls())

                        t_t = tstep(lambda t: [t.transpose(trp[:, kc * 128:(kc + 1) * 128],
                                                           qtr[:, kc * 128:(kc + 1) * 128],
                                                           ident[:, :])
                                               for kc in range(KC)], ("v", v_t), alls())
                        s_t2 = sstep(lambda s: s.copy(qtr[:, :], trp[:, 0:D]), ("t", t_t))
                        syncstep(lambda e, wi=wi, rt=rt:
                                 e.dma_start(
                                     out=prep_in_ap[wi, :, rt * 128:(rt + 1) * 128].rearrange(
                                         "(kc p) o -> p kc o", p=128),
                                     in_=qtr.ap().rearrange("p (kc o) -> p kc o", o=128)
                                 ).then_inc(s_ld, 16), ("s", s_t2))
                        B.inc("ld", 16)

            emit_prep(("l0q", "l0k", "l0v", "l0o"), prep0_in, l1_mode=False)
            B.mark()
            cc_prep0 = collective([prep0_in.opt()], [prep0_out.opt()], allld())
            B.mark()

            vstep(lambda v: v.memset(Mmax[:, 0:16], 0.0))
            for wi, nm in enumerate(("l1q", "l1k", "l1v", "l1o")):
                for rt in range(2):
                    syncstep(lambda e, nm=nm, rt=rt:
                             e.dma_start(out=qsl[rt][:, :],
                                         in_=wsl[nm][rt * 128:(rt + 1) * 128, :]
                                         ).then_inc(s_ld, 16), allv())
                    B.inc("ld", 16)
                    ld_t = B.val("ld")

                    def vs(v, wi=wi, rt=rt):
                        v.tensor_reduce(Mmax[:, 8 + wi:9 + wi], qsl[rt][:, :], AX.X,
                                        OP.add, apply_absolute_value=True)
                        v.tensor_tensor(Mmax[:, wi:wi + 1], Mmax[:, wi:wi + 1],
                                        Mmax[:, 8 + wi:9 + wi], OP.add)
                    vstep(vs, ("ld", ld_t))
            t_t = tstep(lambda t: t.matmul(smp[0:4, 0:1], Mmax[:, 0:4], fone[:, :],
                                           start=True, stop=True), allv())
            v_t = vstep(lambda v: v.tensor_copy(Mmax[0:4, 16:17] if False else tmp1[0:4, 0:1],
                                                smp[0:4, 0:1]), ("t", t_t))
            syncstep(lambda e: e.dma_start(out=sums_in, in_=tmp1[0:4, 0:1]
                                           ).then_inc(s_ld, 16), ("v", v_t))
            B.inc("ld", 16)
            B.mark()
            cc_sums = collective([sums_in.opt()], [sums_out.opt()], allld())
            B.mark()

            syncstep(lambda e: e.dma_start(out=sums8[:, :], in_=sums_out
                                           ).then_inc(s_ld, 16), allcc())
            B.inc("ld", 16)
            ld_t = B.val("ld")
            syncstep(lambda e: e.dma_start(
                out=gtok[0:8, 0:4], in_=sums_out.rearrange("(r w) one -> r (w one)", w=4)
            ).then_inc(s_ld, 16))
            B.inc("ld", 16)
            ld_t = B.val("ld")
            v_t = vstep(lambda v: v.memset(mean_rep[0:8, 0:128], 1.0), ("ld", ld_t))
            t_t = tstep(lambda t: t.matmul(smp[:, 4:8], mean_rep[0:8, 0:128],
                                           gtok[0:8, 0:4], start=True, stop=True),
                        ("v", v_t))
            vstep(lambda v: (v.tensor_scalar(scol[:, 0:4], smp[:, 4:8],
                                             1.0 / (D * D), EPS_L1, OP.mult, OP.add),
                             v.tensor_scalar(thcol[:, 0:4], scol[:, 0:4], 0.5, None,
                                             OP.mult)),
                  ("t", t_t))

            B.mark()
            emit_prep(("l1q", "l1k", "l1v", "l1o"), prep1_in, l1_mode=True)
            B.mark()
            cc_prep1 = collective([prep1_in.opt()], [prep1_out.opt()], allld())
            B.mark()

        def emit_ln_stats():
            for kc in range(KC):
                xb = x2b[:, 0:512] if False else None
                s1 = sstep(lambda s, kc=kc: s.copy(xln[:, kc * T:(kc + 1) * T],
                                                   x[:, kc * T:(kc + 1) * T]),
                           *([("ld", LD_INIT), allv(), allt()] if kc == 0 else []))
                half_buf = x2b.ap().rearrange("p (two t) -> p two t", two=2)[:, kc % 2, :] \
                    if False else None
                def ssq(s, kc=kc, prev_t=B.val("t")):
                    wt(s, "t", prev_t)
                    s.activation(x2b[:, 0:T], xln[:, kc * T:(kc + 1) * T], AF.Square)
                s2 = sstep(ssq)

                def tmm(t, kc=kc):
                    for half in range(2):
                        sl = slice(half * 512, (half + 1) * 512)
                        t.matmul(pja[:, sl], ones_bf[:, :],
                                 xln[:, kc * T + half * 512:kc * T + (half + 1) * 512],
                                 start=(kc == 0), stop=(kc == KC - 1))
                        t.matmul(pjb[:, sl], ones_bf[:, :], x2b[:, sl],
                                 start=(kc == 0), stop=(kc == KC - 1))
                tstep(tmm, ("s", s2))
            t_stats = B.val("t")

            def vst(v):
                v.tensor_scalar(mean_rep[:, :], pja[:, :], 1.0 / D, None, OP.mult)
                v.tensor_scalar(tmp1[:, :], pjb[:, :], 1.0 / D, None, OP.mult)
                v.tensor_tensor(tmp2[:, :], mean_rep[:, :], mean_rep[:, :], OP.mult)
                v.tensor_tensor(tmp1[:, :], tmp1[:, :], tmp2[:, :], OP.subtract)
                v.tensor_scalar(tmp1[:, :], tmp1[:, :], EPS_LN, None, OP.add)
            v_t = vstep(vst, ("t", t_stats))
            s_t2 = sstep(lambda s: s.activation(tmp2[:, :], tmp1[:, :], AF.Sqrt),
                         ("v", v_t))
            vstep(lambda v: v.reciprocal(rstd_rep[:, :], tmp2[:, :]), ("s", s_t2))

        def emit_ln_apply():
            for kc in range(KC):
                def va(v, kc=kc):
                    v.scalar_tensor_tensor(tmp1[:, :], mean_rep[:, :], -1.0,
                                           x[:, kc * T:(kc + 1) * T], OP.mult, OP.add)
                    v.tensor_tensor(xln[:, kc * T:(kc + 1) * T], tmp1[:, :],
                                    rstd_rep[:, :], OP.mult)
                vstep(va, *([allt()] if kc == 0 else []))

        def emit_colrow_broadcast(colA, colB, repA, repB):
            syncstep(lambda e: e.dma_start(out=sel8b, in_=sel8_h).then_inc(s_ld, 16),
                     allt(), alls())
            B.inc("ld", 16)
            ld_s8 = B.val("ld")
            v_t = vstep(lambda v: (v.memset(tmp1[:, 0:128], 0.0),
                                   v.memset(tmp2[:, 0:128], 0.0),
                                   v.tensor_copy(tmp1[:, 0:NQ], colA),
                                   v.tensor_copy(tmp2[:, 0:NQ], colB)))
            t_t = tstep(lambda t: (t.transpose(smp[:, 0:128], tmp1[:, 0:128], identf[:, :]),
                                   t.transpose(smp[:, 128:256], tmp2[:, 0:128], identf[:, :])),
                        ("v", v_t))
            v_t = vstep(lambda v: (v.tensor_copy(brow[0:NQ, 0:128], smp[0:NQ, 0:128]),
                                   v.tensor_copy(brow[0:NQ, 128:256], smp[0:NQ, 128:256])),
                        ("t", t_t))

            def tbc(t, ld_s8=ld_s8):
                wt(t, "ld", ld_s8)
                for j in range(NQ):
                    t.matmul(pja[:, j * 128:(j + 1) * 128],
                             sel8b[:, j * 128:(j + 1) * 128],
                             brow[0:NQ, 0:128], start=True, stop=True)
                    t.matmul(pjb[:, j * 128:(j + 1) * 128],
                             sel8b[:, j * 128:(j + 1) * 128],
                             brow[0:NQ, 128:256], start=True, stop=True)
            t_t = tstep(tbc, ("v", v_t))
            vstep(lambda v: (v.tensor_copy(repA, pja[:, :]),
                             v.tensor_copy(repB, pjb[:, :])), ("t", t_t))

        def emit_maxcols(src_is_ln):
            vstep(lambda v: v.memset(Mmax[:, 0:T], 0.0))
            for kc in range(KC):
                if src_is_ln:
                    v_t = vstep(lambda v, kc=kc: v.scalar_tensor_tensor(
                        tmp1[:, :], mean_rep[:, :], -1.0, x[:, kc * T:(kc + 1) * T],
                        OP.mult, OP.add))
                    s_t2 = sstep(lambda s: s.activation(tmp2[:, :], tmp1[:, :], AF.Abs),
                                 ("v", v_t))
                else:
                    s_t2 = sstep(lambda s, kc=kc: s.activation(
                        tmp2[:, :], ctx_all[:, kc * T:(kc + 1) * T], AF.Abs), allv())
                vstep(lambda v: v.tensor_max(Mmax[:, 0:T], Mmax[:, 0:T], tmp2[:, :]),
                      ("s", s_t2))
            for batch in range(2):
                t_t = tstep(lambda t, batch=batch: [t.transpose(
                    smp[:, jj * 128:(jj + 1) * 128],
                    Mmax[:, (batch * 4 + jj) * 128:(batch * 4 + jj + 1) * 128],
                    identf[:, :]) for jj in range(4)], allv())
                vstep(lambda v, batch=batch: [v.reduce_max(
                    gcolM[:, batch * 4 + jj:batch * 4 + jj + 1],
                    smp[:, jj * 128:(jj + 1) * 128], AX.X) for jj in range(4)],
                    ("t", t_t))
            if src_is_ln:
                for batch in range(2):
                    t_t = tstep(lambda t, batch=batch: [t.transpose(
                        smp[:, jj * 128:(jj + 1) * 128],
                        rstd_rep[:, (batch * 4 + jj) * 128:(batch * 4 + jj + 1) * 128],
                        identf[:, :]) for jj in range(4)], allv())
                    vstep(lambda v, batch=batch: [v.tensor_copy(
                        gcolR[:, batch * 4 + jj:batch * 4 + jj + 1],
                        smp[:, jj * 128:jj * 128 + 1]) for jj in range(4)], ("t", t_t))

        def emit_l1_input_quant():
            emit_maxcols(src_is_ln=True)

            def vg(v):
                v.tensor_tensor(gcolA[:, :], gcolR[:, :], gcolM[:, :], OP.mult)
                v.tensor_scalar(gcolA[:, :], gcolA[:, :], EPS_L1, None, OP.add)
                v.tensor_scalar(gcolB[:, :], gcolA[:, :], 1.0 / QB, None, OP.mult)
                v.reciprocal(gcolM[:, :], gcolA[:, :])
                v.tensor_tensor(gcolA[:, :], gcolM[:, :], gcolR[:, :], OP.mult)
                v.tensor_scalar(gcolA[:, :], gcolA[:, :], QB, None, OP.mult)
            vstep(vg)
            emit_colrow_broadcast(gcolA[:, :], gcolB[:, :], rep_a[:, :], rep_b[:, :])
            for kc in range(KC):
                v_t = vstep(lambda v, kc=kc: (
                    v.scalar_tensor_tensor(tmp1[:, :], mean_rep[:, :], -1.0,
                                           x[:, kc * T:(kc + 1) * T], OP.mult, OP.add),
                    v.tensor_tensor(tmp1[:, :], tmp1[:, :], rep_a[:, :], OP.mult),
                    v.tensor_copy(tint, tmp1[:, :]),
                    v.tensor_copy(xln[:, kc * T:(kc + 1) * T], tint)),
                    *([allt()] if kc == 0 else []))

        def emit_l1_ctx_quant():
            emit_maxcols(src_is_ln=False)

            def vg(v):
                v.tensor_scalar(gcolA[:, :], gcolM[:, :], EPS_L1, None, OP.add)
                v.tensor_scalar(gcolB[:, :], gcolA[:, :], 1.0 / QB, None, OP.mult)
                v.reciprocal(gcolM[:, :], gcolA[:, :])
                v.tensor_scalar(gcolA[:, :], gcolM[:, :], QB, None, OP.mult)
            vstep(vg)
            emit_colrow_broadcast(gcolA[:, :], gcolB[:, :], rep_a[:, :], rep_b[:, :])
            for kc in range(KC):
                s_t2 = sstep(lambda s, kc=kc: s.copy(tmp2[:, :],
                                                     ctx_all[:, kc * T:(kc + 1) * T]),
                             allv())
                vstep(lambda v, kc=kc: (
                    v.tensor_tensor(tmp1[:, :], tmp2[:, :], rep_a[:, :], OP.mult),
                    v.tensor_copy(tint, tmp1[:, :]),
                    v.tensor_copy(xln[:, kc * T:(kc + 1) * T], tint)),
                    ("s", s_t2), *([allt()] if kc == 0 else []))

        arena = sbt("arena", [128, 30848], BF16)
        A = arena.ap()
        srow = A[0:1, 29824:30848]
        ctx_all = A[:, 0:16384]
        Ksb = A[:, 16384:18432]
        Vsb = A[:, 18432:20480]
        Vtm = A[:, 20480:22528]
        Pbf = A[:, 22528:24576]
        Ptrb = A[:, 24576:26624]
        qh_bf = A[:, 26624:27648]
        kh_bf = Pbf[:, 0:T]
        sel8b = Ptrb[0:8, 0:1024]
        x2b = qh_bf
        ctmb = A[:, 29696:29824]
        wqb = [A[:, 27648:29696]]
        wkb = [Ptrb]
        wvb = [Vtm]
        ring = [A[:, i * 6144:(i + 1) * 6144] for i in range(4)]
        hb = A[:, 24576:30720]
        rwbb = A[:, 24576:25088]

        def emit_attention(layer):
            prep_out = prep0_out if layer == 0 else prep1_out
            causal = (layer == 0)

            if layer == 1:
                vstep(lambda v: (v.tensor_tensor(cosb[:, :], cosb[:, :], rep_b[:, :], OP.mult),
                                 v.tensor_tensor(sinb[:, :], sinb[:, :], rep_b[:, :], OP.mult)))

            def load_whead(e, w_idx, h, buf):
                r, half = h // 2, h % 2
                bap = buf
                e.dma_start(
                    out=bap.rearrange("p (kc o) -> p kc o", o=128),
                    in_=prep_out[r, w_idx, :, half * 128:(half + 1) * 128].rearrange(
                        "(kc p) o -> p kc o", p=128)).then_inc(s_ld, 16)

            for w_idx, bufs in ((0, wqb), (1, wkb), (2, wvb)):
                syncstep(lambda e, w_idx=w_idx, bufs=bufs: load_whead(e, w_idx, 0, bufs[0]),
                         allcc())
                B.inc("ld", 16)

            cc_kv1 = [None]
            for h in range(NH):
                if h == 8:
                    cc_kv1[0] = collective([kv_in[0:8].opt()], [kv_outA.opt()],
                                           allld(), groups=PAIRS)
                ld_w = B.val("ld")

                for which, bufs, w_idx in (("q", wqb, 0), ("k", wkb, 1), ("v", wvb, 2)):
                    def tmm(t, which=which, bufs=bufs, h=h, ld_w=ld_w):
                        wt(t, "ld", ld_w)
                        dst = pja if which != "v" else pjb
                        for kcc in range(KC):
                            for half in range(2):
                                sl = slice(half * 512, (half + 1) * 512)
                                t.matmul(dst[:, sl], bufs[0][:, kcc * 128:(kcc + 1) * 128],
                                         xln[:, kcc * T + half * 512:kcc * T + (half + 1) * 512],
                                         start=(kcc == 0), stop=(kcc == KC - 1))
                    t_mm = tstep(tmm, allv())
                    if h + 1 < NH:
                        syncstep(lambda e, w_idx=w_idx, bufs=bufs, h=h:
                                 load_whead(e, w_idx, h + 1, bufs[0]), ("t", t_mm))
                        B.inc("ld", 16)

                    if which == "v":
                        if layer == 0:
                            d_s = sstep(lambda s: s.copy(kh_bf, pjb[:, :]),
                                        ("t", t_mm), allld())
                            dsem = ("s", d_s)
                        else:
                            d_v = vstep(lambda v: v.tensor_tensor(kh_bf, pjb[:, :],
                                                                  rep_b[:, :], OP.mult),
                                        ("t", t_mm), allld())
                            dsem = ("v", d_v)
                        syncstep(lambda e, h=h: e.dma_start(out=kv_in[h, 1, :, :],
                                                            in_=kh_bf
                                                            ).then_inc(s_ld, 16), dsem)
                        B.inc("ld", 16)
                    else:
                        s_e = sstep(lambda s: s.copy(kh_bf, pja[:, :]), ("t", t_mm))
                        t_j = tstep(lambda t: [t.matmul(pjb[:, sl], jrot[:, :], kh_bf[0:128, sl],
                                                        start=True, stop=True)
                                               for sl in (slice(0, 512), slice(512, 1024))],
                                    ("s", s_e))

                        def vrope(v, which=which, h=h):
                            v.tensor_tensor(tmp1[:, :], pja[:, :], cosb[:, :], OP.mult)
                            v.tensor_tensor(tmp2[:, :], pjb[:, :], sinb[:, :], OP.mult)
                            dst = ctx_all[:, h * T:(h + 1) * T] if which == "q" else kh_bf
                            v.tensor_tensor(dst, tmp1[:, :], tmp2[:, :], OP.add)
                        v_r = vstep(vrope, ("t", t_j))
                        if which == "k":
                            syncstep(lambda e, h=h: e.dma_start(out=kv_in[h, 0, :, :],
                                                                in_=kh_bf
                                                                ).then_inc(s_ld, 16),
                                     ("v", v_r))
                            B.inc("ld", 16)

            B.mark()
            cc_kv2 = collective([kv_in[8:16].opt()], [kv_outB.opt()],
                                allld(), groups=PAIRS)
            B.mark()

            kv_free = {"t_sc": 0, "s_k": 0, "t_v": 0}
            for h in range(NH):
                cc_kv = cc_kv1[0] if h < 8 else cc_kv2
                pkt = kv_free["t_sc"]
                pkv = kv_free["s_k"]
                pvt = kv_free["t_v"]

                def ldv_g(e, h=h, cc_kv=cc_kv, pvt=pvt):
                    wt(e, "cc", cc_kv)
                    wt(e, "t", pvt)
                    for rq in range(2):
                        src = (kv_outA if h < 8 else kv_outB)[rq, h % 8, 1, :, :]
                        dstap = Vsb.rearrange("p (u two n) -> p u two n",
                                              two=2, n=128)[:, :, rq, :]
                        e.dma_start(out=dstap, in_=src).then_inc(s_ld, 16)
                B.emit("sync", ldv_g)
                B.inc("ld", 16 * 2)
                ld_v = B.val("ld")

                def ldk_g(e, h=h, cc_kv=cc_kv, pkt=pkt, pkv=pkv):
                    wt(e, "cc", cc_kv)
                    wt(e, "t", pkt)
                    wt(e, "v", pkv)
                    for rq in range(2):
                        src = (kv_outA if h < 8 else kv_outB)[rq, h % 8, 0, :, :]
                        dstap = Ksb.rearrange("p (u two n) -> p u two n",
                                              two=2, n=128)[:, :, rq, :]
                        e.dma_start(out=dstap, in_=src).then_inc(s_ld, 16)
                B.emit("sync", ldk_g)
                B.inc("ld", 16 * 2)
                ld_kv = B.val("ld")

                t_v = tstep(lambda t: [t.transpose(trp[:, u * 128:(u + 1) * 128],
                                                   Vsb[:, u * 128:(u + 1) * 128],
                                                   ident[:, :]) for u in range(NKT)],
                            ("ld", ld_v), alls())
                kv_free["t_v"] = t_v
                s_vtm = vstep(lambda v: v.tensor_copy(Vtm[:, :], trp[:, 0:2048]), ("t", t_v))

                s_q = sstep(lambda s, h=h: s.activation(qh_bf[:, :],
                                                        ctx_all[:, h * T:(h + 1) * T],
                                                        AF.Square), allv())
                s_k = vstep(lambda v: (v.tensor_tensor(Pbf[:, 0:T], Ksb[:, 0:T],
                                                       Ksb[:, 0:T], OP.mult),
                                       v.tensor_tensor(Pbf[:, T:SEQ], Ksb[:, T:SEQ],
                                                       Ksb[:, T:SEQ], OP.mult)),
                            ("ld", ld_kv))

                def tqn(t, s_q=s_q, s_k=s_k):
                    wt(t, "s", s_q)
                    wt(t, "v", s_k)
                    t.matmul(smp[0:1, 0:512], ones_bf[:, 0:1], qh_bf[:, 0:512],
                             start=True, stop=True)
                    t.matmul(spx[0:1, 0:512], ones_bf[:, 0:1], qh_bf[:, 512:1024],
                             start=True, stop=True)
                    for half in range(2):
                        t.matmul(pja[:, half * 512:(half + 1) * 512], ones_bf[:, :],
                                 Pbf[:, half * 512:(half + 1) * 512], start=True, stop=True)
                        t.matmul(pjb[:, half * 512:(half + 1) * 512], ones_bf[:, :],
                                 Pbf[:, T + half * 512:T + (half + 1) * 512],
                                 start=True, stop=True)
                t_n = tstep(tqn)

                def vkn(v, t_n=t_n):
                    v.reduce_max(k2col[:, :], pja[:, :], AX.X)
                    v.reduce_max(qncol[:, 0:1], pjb[:, :], AX.X)
                    v.tensor_tensor(k2col[:, :], k2col[:, :], qncol[:, 0:1], OP.max)
                    v.tensor_copy(srow[0:1, 0:512], smp[0:1, 0:512])
                    v.tensor_copy(srow[0:1, 512:1024], spx[0:1, 0:512])
                v_n = vstep(vkn, ("t", t_n))

                def tqc(t):
                    for jj in range(NQ):
                        t.matmul(smp[:, 256 + jj:257 + jj],
                                 srow[0:1, jj * 128:(jj + 1) * 128],
                                 ones_bf[0:1, 0:1], start=True, stop=True)
                t_c = tstep(tqc, ("v", v_n))
                v_c2 = vstep(lambda v: v.tensor_copy(qncol[:, :], smp[:, 256:256 + NQ]),
                             ("t", t_c))
                v_b = vstep(lambda v: (
                    v.tensor_tensor(bcol[:, :], qncol[:, :],
                                    k2col[:, :].broadcast_to([128, NQ]), OP.add),
                    v.tensor_scalar(bcol[:, :], bcol[:, :], -1.0 / 256.0, None, OP.mult)),
                    ("v", v_c2))

                exp_hist = {}
                pv_hist = []
                for j in range(NQ):
                    w = (2 * j + 2) if causal else NKT
                    cw = w * 128
                    nslice = (cw + T - 1) // T
                    for sl_i in range(nslice):
                        c0 = sl_i * T
                        c1 = min(cw, (sl_i + 1) * T)
                        Sp = pja if sl_i == 0 else pjb

                        def tsc(t, h=h, j=j, c0=c0, c1=c1, Sp=Sp):
                            nn = c1 - c0
                            for off in range(0, nn, 512):
                                t.matmul(Sp[:, off:off + min(512, nn - off)],
                                         ctx_all[:, h * T + j * 128:h * T + (j + 1) * 128],
                                         Ksb[:, c0 + off:c0 + off + min(512, nn - off)],
                                         start=True, stop=True)
                        prevdep = exp_hist.get((j - 1, sl_i), 0)
                        t_sc = tstep(tsc, ("v", v_b), ("s", prevdep))
                        kv_free["t_sc"] = t_sc

                        if causal and sl_i == nslice - 1:
                            mlo = cw - 256 - c0
                            v_m = vstep(lambda v, Sp=Sp, mlo=mlo: (
                                v.tensor_tensor(Sp[:, mlo:mlo + 128], Sp[:, mlo:mlo + 128],
                                                mA[:, :], OP.add),
                                v.tensor_tensor(Sp[:, mlo + 128:mlo + 256],
                                                Sp[:, mlo + 128:mlo + 256], mB[:, :], OP.add)),
                                ("t", t_sc))
                            dep = ("v", v_m)
                        else:
                            dep = ("t", t_sc)

                        def sexp(s, h=h, j=j, sl_i=sl_i, c0=c0, c1=c1, Sp=Sp):
                            s.activation(Pbf[:, c0:c1], Sp[:, 0:c1 - c0], AF.Exp,
                                         bias=bcol[:, j:j + 1], scale=ISQD,
                                         accum_out=(lcol[:, j:j + 1] if sl_i == 0
                                                    else rlcol[:, j:j + 1]))
                        s_e = sstep(sexp, dep)
                        exp_hist[(j, sl_i)] = s_e
                    if nslice == 2:
                        v_l = vstep(lambda v, j=j: v.tensor_tensor(
                            lcol[:, j:j + 1], lcol[:, j:j + 1], rlcol[:, j:j + 1], OP.add),
                            ("s", B.val("s")))
                    v_rl = vstep(lambda v, j=j: v.reciprocal(rlcol[:, j:j + 1],
                                                             lcol[:, j:j + 1]),
                                 ("s", B.val("s")), ("v", B.val("v")))

                    t_tr = tstep(lambda t, j=j, w=w: [t.transpose(
                        trp[:, u * 128:(u + 1) * 128], Pbf[:, u * 128:(u + 1) * 128],
                        ident[:, :]) for u in range(w)], ("s", B.val("s")), allv())
                    s_ptr = vstep(lambda v, w=w: v.tensor_copy(Ptrb[:, 0:w * 128],
                                                        trp[:, 0:w * 128]), ("t", t_tr))

                    def tpv(t, j=j, w=w):
                        o = (j % 4) * 128
                        for u in range(w):
                            t.matmul(smp[:, o:o + 128], Ptrb[:, u * 128:(u + 1) * 128],
                                     Vtm[:, u * 128:(u + 1) * 128],
                                     start=(u == 0), stop=(u == w - 1))
                    t_pv = tstep(tpv, ("v", s_ptr))
                    pv_hist.append((t_pv, v_rl))
                    if j % 4 == 3:
                        j0 = j - 3
                        def vcn4(v, j0=j0):
                            for jj in range(4):
                                v.tensor_tensor(
                                    qh_bf[0:128, jj * 128:(jj + 1) * 128],
                                    smp[:, jj * 128:(jj + 1) * 128],
                                    rlcol[:, j0 + jj:j0 + jj + 1].broadcast_to([128, 128]),
                                    OP.mult)
                        s_cn = vstep(vcn4, ("t", pv_hist[-1][0]))
                        t_ct = tstep(lambda t: [t.transpose(
                            trp[:, 1536 + jj * 128:1536 + (jj + 1) * 128],
                            qh_bf[0:128, jj * 128:(jj + 1) * 128], ident[:, :])
                            for jj in range(4)], ("v", s_cn))
                        s_cw = vstep(lambda v, h=h, j0=j0: v.tensor_copy(
                            ctx_all[:, h * T + j0 * 128:h * T + (j0 + 4) * 128],
                            trp[:, 1536:2048]), ("t", t_ct))

            B.mark()
            if layer == 1:
                emit_l1_ctx_quant()
                osrc = xln
            else:
                osrc = ctx_all

            for oc in range(KC):
                wob = wqb[0] if oc % 2 == 0 else wkb[0]
                syncstep(lambda e, oc=oc, wob=wob: load_whead(e, 3, oc, wob),
                         allt())
                B.inc("ld", 16)
                ld_o = B.val("ld")

                def tom(t, oc=oc, ld_o=ld_o, osrc=osrc, wob=wob):
                    wt(t, "ld", ld_o)
                    dst = pja if (oc % 2 == 0) else pjb
                    for kcc in range(KC):
                        for half in range(2):
                            sl = slice(half * 512, (half + 1) * 512)
                            t.matmul(dst[:, sl], wob[:, kcc * 128:(kcc + 1) * 128],
                                     osrc[:, kcc * T + half * 512:kcc * T + (half + 1) * 512],
                                     start=(kcc == 0), stop=(kcc == KC - 1))
                t_o = tstep(tom, allv(), alls())

                if layer == 0:
                    vstep(lambda v, oc=oc: v.scalar_tensor_tensor(
                        x[:, oc * T:(oc + 1) * T],
                        (pja if oc % 2 == 0 else pjb)[:, :], 1.0,
                        x[:, oc * T:(oc + 1) * T], OP.mult, OP.add), ("t", t_o))
                else:
                    vstep(lambda v, oc=oc: (
                        v.tensor_tensor(tmp1[:, :],
                                        (pja if oc % 2 == 0 else pjb)[:, :],
                                        rep_b[:, :], OP.mult),
                        v.tensor_tensor(x[:, oc * T:(oc + 1) * T],
                                        x[:, oc * T:(oc + 1) * T], tmp1[:, :], OP.add)),
                        ("t", t_o))

        def emit_l0_ffn():
            l0ATb = Vtm
            l0Bb = Pbf
            gw_b = Ptrb
            gdma(lambda g: g.dma_start(out=l0ATb.rearrange("p (kc o) -> p kc o", o=128),
                                       in_=l0f_AT.rearrange("(kc p) o -> p kc o", p=128)),
                 allt(), alls())
            gdma(lambda g: g.dma_start(out=l0Bb[:, 0:D], in_=l0f_B))
            gdma(lambda g: g.dma_start(
                out=gw_b[:, 0:KC * 8].rearrange("p (kc o) -> p kc o", o=8),
                in_=l0f_gwT.rearrange("(kc p) o -> p kc o", p=128)))
            gdma(lambda g: g.dma_start(out=biasrow[:, :], in_=l0f_b_h))
            gdma(lambda g: g.dma_start(out=gbrow[:, :], in_=l0f_gb_h))
            gd_w = B.val("gd")

            def tgl(t, gd_w=gd_w):
                wt(t, "gd", gd_w)
                for kcc in range(KC):
                    for half, bank in ((0, smp), (1, spx)):
                        bankap = bank[0:8, 0:512]
                        t.matmul(bankap, gw_b[:, kcc * 8:(kcc + 1) * 8],
                                 xln[:, kcc * T + half * 512:kcc * T + (half + 1) * 512],
                                 start=(kcc == 0), stop=False)
                for half, bank in ((0, smp), (1, spx)):
                    t.matmul(bank[0:8, 0:512], gbrow[0:1, 0:8],
                             onesrow[0:1, 0:512], start=False, stop=True)
            t_gl = tstep(tgl, allv())
            v_gl = vstep(lambda v: (v.tensor_copy(tmp3[0:8, 0:512], smp[0:8, 0:512]),
                                    v.tensor_copy(tmp3[0:8, 512:1024], spx[0:8, 0:512])),
                         ("t", t_gl))

            for blk in range(NQ):
                t_tr = tstep(lambda t, blk=blk: t.transpose(
                    smp[:, 0:128], tmp3[0:8, blk * 128:(blk + 1) * 128] if False else
                    tmp3[0:128, blk * 128:(blk + 1) * 128], identf[:, :]),
                    ("v", B.val("v")))

                def vtop(v, blk=blk):
                    v.tensor_copy(gtok[:, 0:8], smp[:, 0:8])
                    v.reduce_max(gm1[:, :], gtok[:, 0:8], AX.X)
                    v.tensor_tensor(gtk2[:, 0:8], gtok[:, 0:8],
                                    gm1[:, :].broadcast_to([128, 8]), OP.is_equal)
                    v.scalar_tensor_tensor(gtk3[:, 0:8], gtk2[:, 0:8], -1e30,
                                           gtok[:, 0:8], OP.mult, OP.add)
                    v.reduce_max(gm2[:, :], gtk3[:, 0:8], AX.X)
                v_t1 = vstep(vtop, ("t", t_tr))
                v_t2 = vstep(lambda v: v.tensor_scalar(gm1[:, :], gm1[:, :], -1.0,
                                                       None, OP.mult), ("v", v_t1))
                s_ex2 = sstep(lambda s: s.activation(gtk2[:, 0:8], gtok[:, 0:8], AF.Exp,
                                                     bias=gm1[:, :], scale=1.0),
                              ("v", v_t2))

                def vw(v, blk=blk):
                    v.tensor_tensor(gtk3[:, 0:8], gtok[:, 0:8],
                                    gm2[:, :].broadcast_to([128, 8]), OP.is_ge)
                    v.tensor_tensor(gtk2[:, 0:8], gtk2[:, 0:8], gtk3[:, 0:8], OP.mult)
                    v.reduce_sum(gden[:, :], gtk2[:, 0:8], AX.X)
                    v.reciprocal(gden[:, :], gden[:, :])
                    v.tensor_tensor(gtok[:, 0:8], gtk2[:, 0:8], gden[:, :].broadcast_to([128, 8]), OP.mult)
                v_t3 = vstep(vw, ("s", s_ex2))
                t_tb = tstep(lambda t: t.transpose(smp[:, 128:256], gtok[:, 0:128],
                                                   identf[:, :]), ("v", v_t3))
                v_tb = vstep(lambda v, blk=blk: v.tensor_copy(
                    wselb[0:8, blk * 128:(blk + 1) * 128], smp[0:8, 128:256]), ("t", t_tb))

            t_w16 = tstep(lambda t: [t.matmul((pja if False else smp)[:, 0:512] if False
                                              else pja[:, half * 512:(half + 1) * 512],
                                              e16[:, :], wselb[0:8, half * 512:(half + 1) * 512],
                                              start=True, stop=True) for half in range(2)],
                          ("v", B.val("v")))
            v_w16 = vstep(lambda v: v.tensor_copy(wrepf[:, :], pja[:, :]), ("t", t_w16))

            def txa(t):
                for kcc in range(KC):
                    for half in range(2):
                        sl = slice(half * 512, (half + 1) * 512)
                        t.matmul(pjb[:, sl], l0ATb[:, kcc * 128:(kcc + 1) * 128],
                                 xln[:, kcc * T + half * 512:kcc * T + (half + 1) * 512],
                                 start=(kcc == 0), stop=(kcc == KC - 1))
            t_xa = tstep(txa, ("v", v_w16))
            v_xa = vstep(lambda v: (v.tensor_tensor(tmp1[:, :], pjb[:, :], wrepf[:, :],
                                                    OP.mult),
                                    v.tensor_copy(qh_bf[:, :], tmp1[:, :])), ("t", t_xa))

            wfb = [Ksb, Vsb]
            for oc in range(KC):
                gdma(lambda g, oc=oc: g.dma_start(
                    out=wfb[oc % 2][:, 0:D].rearrange("p (kc o) -> p kc o", o=128),
                    in_=l0fWT[:, oc * 128:(oc + 1) * 128].rearrange("(kc p) o -> p kc o", p=128)
                ), allt())
                gd_t = B.val("gd")

                def tb(t, oc=oc, gd_t=gd_t):
                    wt(t, "gd", gd_t)
                    dst = pja if oc % 2 == 0 else pjb
                    for kcc in range(KC):
                        for half in range(2):
                            sl = slice(half * 512, (half + 1) * 512)
                            t.matmul(dst[:, sl], wfb[oc % 2][:, kcc * 128:(kcc + 1) * 128],
                                     xln[:, kcc * T + half * 512:kcc * T + (half + 1) * 512],
                                     start=(kcc == 0), stop=False)
                    for half in range(2):
                        sl = slice(half * 512, (half + 1) * 512)
                        t.matmul(dst[:, sl], biasrow[0:1, oc * 128:(oc + 1) * 128],
                                 onesrow[0:1, 0:512], start=False, stop=False)
                        t.matmul(dst[:, sl], l0Bb[:, oc * 128:(oc + 1) * 128],
                                 qh_bf[:, sl], start=False, stop=True)
                t_b = tstep(tb, ("v", v_xa) if oc == 0 else allv())
                vstep(lambda v, oc=oc: v.scalar_tensor_tensor(
                    x[:, oc * T:(oc + 1) * T], (pja if oc % 2 == 0 else pjb)[:, :],
                    1.0, x[:, oc * T:(oc + 1) * T], OP.mult, OP.add), ("t", t_b))

        def emit_l1_ffn():
            gdma(lambda g: g.dma_start(
                out=rwbb.rearrange("p (kc o) -> p kc o", o=32),
                in_=l1f_rwT.rearrange("(kc p) o -> p kc o", p=128)),
                allt(), alls())
            gd_t = B.val("gd")

            def trt(t, gd_t=gd_t):
                wt(t, "gd", gd_t)
                for kcc in range(KC):
                    for half, bank in ((0, smp), (1, spx)):
                        t.matmul(bank[0:32, 0:512], rwbb[:, kcc * 32:(kcc + 1) * 32],
                                 xln[:, kcc * T + half * 512:kcc * T + (half + 1) * 512],
                                 start=(kcc == 0), stop=(kcc == KC - 1))
            t_r = tstep(trt, allv())
            v_r = vstep(lambda v: (v.tensor_copy(tmp3[0:32, 0:512], smp[0:32, 0:512]),
                                   v.tensor_copy(tmp3[0:32, 512:1024], spx[0:32, 0:512])),
                        ("t", t_r))

            for blk in range(NQ):
                t_tr = tstep(lambda t, blk=blk: t.transpose(
                    smp[:, 0:128], tmp3[0:128, blk * 128:(blk + 1) * 128], identf[:, :]),
                    ("v", B.val("v")))

                def vtop(v, blk=blk):
                    v.tensor_copy(gtok[:, 0:32], smp[:, 0:32])
                    v.reduce_max(gm1[:, :], gtok[:, 0:32], AX.X)
                    v.tensor_copy(gtk3[:, 0:32], gtok[:, 0:32])
                v_t1 = vstep(vtop, ("t", t_tr))
                for it in range(7):
                    def vit(v, it=it):
                        v.reduce_max(gm2[:, :], gtk3[:, 0:32], AX.X)
                        v.tensor_tensor(gtk2[:, 0:32], gtk3[:, 0:32],
                                        gm2[:, :].broadcast_to([128, 32]), OP.is_equal)
                        v.scalar_tensor_tensor(gtk3[:, 0:32], gtk2[:, 0:32], -1e30,
                                               gtk3[:, 0:32], OP.mult, OP.add)
                    vstep(vit)
                v_m8 = vstep(lambda v: v.reduce_max(gm2[:, :], gtk3[:, 0:32], AX.X))
                v_neg = vstep(lambda v: v.tensor_scalar(gm1[:, :], gm1[:, :], -1.0,
                                                        None, OP.mult))
                s_ex = sstep(lambda s: s.activation(gtk2[:, 0:32], gtok[:, 0:32], AF.Exp,
                                                    bias=gm1[:, :], scale=1.0),
                             ("v", B.val("v")))

                def vw(v, blk=blk):
                    v.tensor_tensor(gtk3[:, 0:32], gtok[:, 0:32],
                                    gm2[:, :].broadcast_to([128, 32]), OP.is_ge)
                    v.tensor_tensor(gtk2[:, 0:32], gtk2[:, 0:32], gtk3[:, 0:32], OP.mult)
                    v.reduce_sum(gden[:, :], gtk2[:, 0:32], AX.X)
                    v.reciprocal(gden[:, :], gden[:, :])
                    v.tensor_tensor(gtok[:, 0:32], gtk2[:, 0:32], gden[:, :].broadcast_to([128, 32]), OP.mult)
                v_t3 = vstep(vw, ("s", s_ex))
                t_tb = tstep(lambda t: t.transpose(smp[:, 128:256], gtok[:, 0:128],
                                                   identf[:, :]), ("v", v_t3))
                vstep(lambda v, blk=blk: v.tensor_copy(
                    wselb[0:32, blk * 128:(blk + 1) * 128], smp[0:32, 128:256]),
                    ("t", t_tb))

            piece_idx = [0]

            def load_piece(src_ap, dst, *waits):
                return gdma(lambda g, src_ap=src_ap, dst=dst:
                            g.dma_start(out=dst, in_=src_ap), *waits)

            def piece_srcs(e):
                return [
                    (l1f_wg[e, 0:1024, :].rearrange("(kc p) o -> p kc o", p=128),
                     ring, "wg0"),
                    (l1f_wg[e, 1024:2048, :].rearrange("(kc p) o -> p kc o", p=128),
                     ring, "wg1"),
                    (l1f_wu[e, 0:1024, :].rearrange("(kc p) o -> p kc o", p=128),
                     ring, "wu0"),
                    (l1f_wu[e, 1024:2048, :].rearrange("(kc p) o -> p kc o", p=128),
                     ring, "wu1"),
                    (l1f_wd[e, :, 0:1024].rearrange("(k6 p) o -> p k6 o", p=128),
                     ring, "wdA"),
                    (l1f_wd[e, :, 1024:2048].rearrange("(k6 p) o -> p k6 o", p=128),
                     ring, "wdB"),
                ]

            def slot(e, i):
                return (e * 6 + i) % 4

            gd_piece = {}
            for i in range(4):
                src, _, _ = piece_srcs(0)[i]
                dstap = ring[slot(0, i)].rearrange(
                    "p (a b) -> p a b", a=8 if i < 4 else 6)
                gd_piece[(0, i)] = load_piece(src, dstap, allt(), allv())

            for e in range(32):
                syncstep(lambda ee, e=e: ee.dma_start(
                    out=selbuf[:, :], in_=sel32_h[e, :].rearrange("(r n) -> r n", n=128)
                ).then_inc(s_ld, 16), allt())
                B.inc("ld", 16)
                ld_sel = B.val("ld")
                t_wr = tstep(lambda t, e=e, ld_sel=ld_sel: (
                    wt(t, "ld", ld_sel),
                    [t.matmul(pjb[:, half * 512:(half + 1) * 512], selbuf[:, :],
                              wselb[0:32, half * 512:(half + 1) * 512],
                              start=True, stop=True) for half in range(2)]), allv())
                v_wr = vstep(lambda v: v.tensor_copy(wrepf[:, :], pjb[:, :]), ("t", t_wr))

                last_si = 0
                last_h = v_wr
                for oc6 in range(6):
                    def tg(t, e=e, oc6=oc6):
                        wt(t, "gd", max(gd_piece[(e, 0)], gd_piece[(e, 1)]))
                        for kcc in range(KC):
                            piece = 0 if kcc < 8 else 1
                            rb = ring[slot(e, piece)]
                            kloc = kcc % 8
                            for half in range(2):
                                sl = slice(half * 512, (half + 1) * 512)
                                t.matmul(pja[:, sl],
                                         rb[:, kloc * 768 + oc6 * 128:kloc * 768 + (oc6 + 1) * 128],
                                         xln[:, kcc * T + half * 512:kcc * T + (half + 1) * 512],
                                         start=(kcc == 0), stop=(kcc == KC - 1))
                    t_g = tstep(tg, allv() if oc6 == 0 else ("s", last_si))
                    s_si = sstep(lambda s, oc6=oc6: s.activation(tmp3[:, :], pja[:, :],
                                                                 AF.Silu), ("t", t_g))
                    last_si = s_si
                    def tu(t, e=e, oc6=oc6):
                        wt(t, "gd", max(gd_piece[(e, 2)], gd_piece[(e, 3)]))
                        for kcc in range(KC):
                            piece = 2 if kcc < 8 else 3
                            rb = ring[slot(e, piece)]
                            kloc = kcc % 8
                            for half in range(2):
                                sl = slice(half * 512, (half + 1) * 512)
                                t.matmul(pjb[:, sl],
                                         rb[:, kloc * 768 + oc6 * 128:kloc * 768 + (oc6 + 1) * 128],
                                         xln[:, kcc * T + half * 512:kcc * T + (half + 1) * 512],
                                         start=(kcc == 0), stop=(kcc == KC - 1))
                    t_u = tstep(tu, ("v", last_h))
                    last_h = vstep(lambda v, oc6=oc6: (
                        v.tensor_tensor(tmp1[:, :], tmp3[:, :], pjb[:, :], OP.mult),
                        v.tensor_tensor(tmp2[:, :], tmp1[:, :], wrepf[:, :], OP.mult),
                        v.tensor_copy(hb[:, oc6 * T:(oc6 + 1) * T], tmp2[:, :])),
                        ("t", t_u), ("s", s_si))

                t_after_g = B.val("t")
                for i in (4, 5):
                    src, _, _ = piece_srcs(e)[i]
                    dstap = ring[slot(e, i)].rearrange("p (a b) -> p a b", a=6)
                    gd_piece[(e, i)] = load_piece(src, dstap, ("t", t_after_g))

                t_after_u = B.val("t")
                if e + 1 < 32:
                    for i in (0, 1):
                        src, _, _ = piece_srcs(e + 1)[i]
                        dstap = ring[slot(e + 1, i)].rearrange(
                            "p (a b) -> p a b", a=8)
                        gd_piece[(e + 1, i)] = load_piece(src, dstap, ("t", t_after_u))

                ev_hist = [last_h, last_h]
                for dm in range(KC):
                    def td(t, e=e, dm=dm):
                        wt(t, "gd", gd_piece[(e, 4 if dm < 8 else 5)])
                        for k6 in range(6):
                            piece = 4 if dm < 8 else 5
                            rb = ring[slot(e, piece)]
                            dloc = dm % 8
                            for half in range(2):
                                sl = slice(half * 512, (half + 1) * 512)
                                t.matmul((pja if dm % 2 == 0 else pjb)[:, sl],
                                         rb[:, k6 * 1024 + dloc * 128:k6 * 1024 + (dloc + 1) * 128],
                                         hb[:, k6 * T + half * 512:k6 * T + (half + 1) * 512],
                                         start=(k6 == 0), stop=(k6 == 5))
                    wdep = ("v", ev_hist[-2]) if dm >= 2 else ("v", last_h)
                    t_d = tstep(td, wdep)
                    ev = vstep(lambda v, dm=dm: v.scalar_tensor_tensor(
                        x[:, dm * T:(dm + 1) * T],
                        (pja if dm % 2 == 0 else pjb)[:, :], 1.0,
                        x[:, dm * T:(dm + 1) * T], OP.mult, OP.add), ("t", t_d))
                    ev_hist.append(ev)
                    if e + 1 < 32:
                        if dm == 7:
                            src, _, _ = piece_srcs(e + 1)[2]
                            dstap = ring[slot(e + 1, 2)].rearrange(
                                "p (a b) -> p a b", a=8)
                            gd_piece[(e + 1, 2)] = load_piece(src, dstap,
                                                             ("t", B.val("t")))
                        if dm == KC - 1:
                            src, _, _ = piece_srcs(e + 1)[3]
                            dstap = ring[slot(e + 1, 3)].rearrange(
                                "p (a b) -> p a b", a=8)
                            gd_piece[(e + 1, 3)] = load_piece(src, dstap,
                                                             ("t", B.val("t")))

        try:
            B.mark()
            emit_ln_stats()
            B.mark()
            emit_ln_apply()
            B.mark()
            if NBLOCK >= 1:
                emit_attention(0)
            if NBLOCK >= 2:
                emit_ln_stats()
                emit_ln_apply()
                emit_l0_ffn()
            if NBLOCK >= 3:
                emit_ln_stats()
                emit_l1_input_quant()
                emit_attention(1)
            if NBLOCK >= 4:
                emit_ln_stats()
                emit_ln_apply()
                emit_l1_ffn()
        except StopEmission:
            pass

        if DEVDUMP:
            def dumps(e):
                import contextlib
                ctxmgr = nc.allow_non_contiguous_dma(reason="debug dumps")
                ctxmgr.__enter__()
                e.dma_start(out=dbg_ctx, in_=ctx_all).then_inc(s_st, 16)
                e.dma_start(out=dbg_k, in_=Ksb).then_inc(s_st, 16)
                e.dma_start(out=dbg_v, in_=Vsb).then_inc(s_st, 16)
                e.dma_start(out=dbg_xln, in_=xln[:, :]).then_inc(s_st, 16)
                e.dma_start(out=dbg_cols[:, 0:8], in_=lcol[:, :]).then_inc(s_st, 16)
                e.dma_start(out=dbg_cols[:, 8:16], in_=bcol[:, :]).then_inc(s_st, 16)
                e.dma_start(out=dbg_cols[:, 16:24], in_=qncol[:, :]).then_inc(s_st, 16)
                e.dma_start(out=dbg_cols[:, 24:25], in_=k2col[:, :]).then_inc(s_st, 16)
                e.dma_start(out=dbg_cols[:, 25:33], in_=rlcol[:, :]).then_inc(s_st, 16)
                e.dma_start(out=dbg_cols[:, 33:34], in_=mean_rep[:, 0:1]).then_inc(s_st, 16)
                e.dma_start(out=dbg_cols[:, 34:35], in_=rstd_rep[:, 0:1]).then_inc(s_st, 16)
                e.dma_start(out=dbg_kvin, in_=kv_in).then_inc(s_st, 16)
                e.dma_start(out=dbg_p0in, in_=prep0_in).then_inc(s_st, 16)
                e.dma_start(out=dbg_p0out, in_=prep0_out[0:2]).then_inc(s_st, 16)
            B.inc("st", 16 * 14)
            B.emit("sync", W(dumps := dumps, ) if False else
                   (lambda e, f=dumps: ([wt(e, sn, vv) for sn, vv in
                                         (allv(), allt(), alls(), allgd())], f(e))))
        syncstep(lambda e: e.dma_start(
            out=out_fm.rearrange("(kc p) t -> p kc t", p=128),
            in_=x[:, :].rearrange("p (kc t) -> p kc t", t=T)).then_inc(s_st, 16),
            allv(), allt(), alls(), allgd())
        B.inc("st", 16)
        syncstep(lambda e: None, ("st", B.val("st")))

        block = ctx.enter_context(nc.Block())

        @block.sync
        def _(sync):
            for fn in B.q["sync"]:
                fn(sync)

        @block.gpsimd
        def _(g):
            for fn in B.q["gpsimd"]:
                fn(g)

        @block.vector
        def _(v):
            for fn in B.q["vector"]:
                fn(v)

        @block.scalar
        def _(s):
            for fn in B.q["scalar"]:
                fn(s)

        @block.tensor
        def _(t):
            for fn in B.q["tensor"]:
                fn(t)

    return nc



def _host_constants():
    ones = np.ones((128, 128), dtype=ml_dtypes.bfloat16)
    ident = np.eye(128, dtype=ml_dtypes.bfloat16)
    identf = np.eye(128, dtype=np.float32)
    J = np.zeros((128, 128), dtype=np.float32)
    J[np.arange(64), np.arange(64) + 64] = -1.0
    J[np.arange(64) + 64, np.arange(64)] = 1.0
    jrotT = J.T.astype(ml_dtypes.bfloat16)
    e16 = np.zeros((8, 128), dtype=ml_dtypes.bfloat16)
    for e in range(8):
        e16[e, e * 16:(e + 1) * 16] = 1.0
    sel8 = np.zeros((8, 8 * 128), dtype=ml_dtypes.bfloat16)
    for j in range(8):
        sel8[j, j * 128:(j + 1) * 128] = 1.0
    sel32 = np.zeros((32, 32 * 128), dtype=ml_dtypes.bfloat16)
    for e in range(32):
        sel32[e, e * 128:(e + 1) * 128] = 1.0
    tri = np.where(np.arange(128)[None, :] <= np.arange(128)[:, None], 0.0, -1e30
                   ).astype(np.float32)
    full = np.full((128, 128), -1e30, dtype=np.float32)
    zero = np.zeros((128, 128), dtype=np.float32)
    return ones, ident, identf, jrotT, e16, sel8, sel32, tri, full, zero


def _rope_tables_stacked():
    half = 64
    inv = (1.0 / 10000.0 ** (np.arange(half, dtype=np.float64) / half))
    tpos = np.arange(SEQ, dtype=np.float64)
    fr = np.outer(tpos, inv)
    cos = np.cos(fr).astype(np.float32)
    sin = np.sin(fr).astype(np.float32)
    c2 = np.concatenate([cos, cos], axis=1).T.copy()
    s2 = np.concatenate([sin, sin], axis=1).T.copy()
    return c2, s2


_NC_CACHE = {}


def _get_nc():
    key = NBLOCK
    if key not in _NC_CACHE:
        _NC_CACHE[key] = build_program()
    return _NC_CACHE[key]


def kernel(**inputs):
    f32 = np.float32
    xfull = np.ascontiguousarray(inputs["L0_attn_x"], dtype=f32)
    ones, ident, identf, jrotT, e16, sel8, sel32, tri, full, zero = _host_constants()
    c2, s2 = _rope_tables_stacked()

    wmap = {
        "l0q": inputs["L0_attn_w_q"], "l0k": inputs["L0_attn_w_k"],
        "l0v": inputs["L0_attn_w_v"], "l0o": inputs["L0_attn_w_o"],
        "l1q": inputs["L1_attn_w_q"], "l1k": inputs["L1_attn_w_k"],
        "l1v": inputs["L1_attn_w_v"], "l1o": inputs["L1_attn_w_o"],
    }
    l0fWT = np.ascontiguousarray(inputs["L0_ffn_W"].T, dtype=f32)
    l0f_b = np.ascontiguousarray(inputs["L0_ffn_b"], dtype=f32).reshape(1, D)
    l0f_AT = np.ascontiguousarray(
        inputs["L0_ffn_lora_A"].reshape(128, D).T, dtype=f32)
    l0f_B = np.ascontiguousarray(
        np.swapaxes(inputs["L0_ffn_lora_B"], 1, 2).reshape(128, D), dtype=f32)
    l0f_gwT = np.ascontiguousarray(inputs["L0_ffn_gate_w"].T, dtype=f32)
    l0f_gb = np.ascontiguousarray(inputs["L0_ffn_gate_b"], dtype=f32).reshape(1, 8)
    l1f_rwT = np.ascontiguousarray(inputs["L1_ffn_router_w"].T, dtype=f32)
    l1f_wg = np.ascontiguousarray(inputs["L1_ffn_w_gate"], dtype=f32)
    l1f_wu = np.ascontiguousarray(inputs["L1_ffn_w_up"], dtype=f32)
    l1f_wd = np.ascontiguousarray(inputs["L1_ffn_w_down"], dtype=f32)

    in_maps = []
    for c in range(NCORE):
        b, p = c // 2, c % 2
        gtiles = [2 * j + p for j in range(NQ)]
        cols = np.concatenate([np.arange(g * 128, (g + 1) * 128) for g in gtiles])
        xs = np.ascontiguousarray(xfull[b].T[:, cols])
        im = {
            "x_fm": xs,
            "cos_own": np.ascontiguousarray(c2[:, cols]),
            "sin_own": np.ascontiguousarray(s2[:, cols]),
            "maskA": tri if p == 0 else zero,
            "maskB": full if p == 0 else tri,
            "ones128": ones, "ident": ident, "identf": identf,
            "jrot": jrotT, "e16": e16, "sel8": sel8, "sel32": sel32,
            "l0fWT": l0fWT, "l0f_b": l0f_b, "l0f_AT": l0f_AT, "l0f_B": l0f_B,
            "l0f_gwT": l0f_gwT, "l0f_gb": l0f_gb, "l1f_rwT": l1f_rwT,
            "l1f_wg": l1f_wg, "l1f_wu": l1f_wu, "l1f_wd": l1f_wd,
        }
        for nm, w in wmap.items():
            im[nm + "_s"] = np.ascontiguousarray(
                np.asarray(w, dtype=f32)[c * 256:(c + 1) * 256, :])
        in_maps.append(im)

    nc = _get_nc()
    res = run_bass_kernel_spmd(nc, in_maps, core_ids=list(range(NCORE)))

    out = np.empty((4, SEQ, D), dtype=f32)
    for c in range(NCORE):
        b, p = c // 2, c % 2
        gtiles = [2 * j + p for j in range(NQ)]
        cols = np.concatenate([np.arange(g * 128, (g + 1) * 128) for g in gtiles])
        of = np.asarray(res.results[c]["out_fm"])
        out[b, cols, :] = of.T
    return out
